# version 28
# baseline (speedup 1.0000x reference)
import numpy as np
from contextlib import ExitStack

import concourse.bass as bass
import concourse.mybir as mybir
from concourse.bass_utils import run_bass_kernel_spmd

F32 = mybir.dt.float32
BF16 = mybir.dt.bfloat16
AF = mybir.ActivationFunctionType
ALU = mybir.AluOpType
AX = mybir.AxisListType

REAL = dict(B=2, S=4096, D=1024, H=16, DH=64, W=512, ST=32, DFF=4096, NC=8)


class Res:
    __slots__ = ("writer", "readers", "name")

    def __init__(self, name=""):
        self.writer = None
        self.readers = []
        self.name = name


class Step:
    __slots__ = ("eng", "fn", "deps", "dma", "val")

    def __init__(self, eng, fn, deps, dma, val):
        self.eng, self.fn, self.deps, self.dma, self.val = eng, fn, deps, dma, val


class Prog:
    def __init__(self):
        self.steps = []
        self.dma_count = {"gp": 0, "sp": 0, "act": 0}
        self.eng_count = {e: 0 for e in ("pe", "act", "dve", "gp", "sp")}

    def st(self, eng, fn, reads=(), writes=(), deps=(), dma=False):
        d = set()
        for r in reads:
            if r.writer is not None:
                d.add(r.writer)
        for w in writes:
            if w.writer is not None:
                d.add(w.writer)
            d.update(w.readers)
        d.update(x for x in deps if x is not None)
        if dma:
            self.dma_count[eng] += 1
            val = 16 * self.dma_count[eng]
        else:
            self.eng_count[eng] += 1
            val = self.eng_count[eng]
        stp = Step(eng, fn, d, dma, val)
        for r in reads:
            r.readers.append(stp)
        for w in writes:
            w.writer = stp
            w.readers = []
        self.steps.append(stp)
        return stp

    def finalize(self, block, esems, dsems):
        order = {}
        for i, s in enumerate(self.steps):
            order[id(s)] = i

        def emit(name, eng):
            water = {}
            for s in self.steps:
                if s.eng != name:
                    continue
                waits = {}
                for dp in sorted(s.deps, key=lambda x: order[id(x)]):
                    if dp.eng == name and not dp.dma:
                        if s.val - dp.val > 4:
                            continue
                    sem = dsems[dp.eng] if dp.dma else esems[dp.eng]
                    key = id(sem)
                    if waits.get(key, (None, -1))[1] < dp.val:
                        waits[key] = (sem, dp.val)
                for sem, val in waits.values():
                    if water.get(id(sem), -1) >= val:
                        continue
                    eng.wait_ge(sem, val)
                    water[id(sem)] = val
                last = s.fn(eng)
                assert last is not None, f"step fn returned None on {name}"
                last.then_inc(dsems[name] if s.dma else esems[name],
                              16 if s.dma else 1)

        @block.tensor
        def _(t):
            emit("pe", t)

        @block.scalar
        def _(sc):
            emit("act", sc)

        @block.vector
        def _(v):
            emit("dve", v)

        @block.gpsimd
        def _(g):
            emit("gp", g)

        @block.sync
        def _(s):
            emit("sp", s)


def build_program(P, detect_races=False, debug_spill=False):
    B, S, D, H, DH, W, ST, DFF, NC = (P["B"], P["S"], P["D"], P["H"], P["DH"],
                                      P["W"], P["ST"], P["DFF"], P["NC"])
    FC = D // 128
    HDH = H * DH
    QKC = HDH // 128
    NW = S // W
    TTW = W // 128
    KC = 2 * W // 128
    SK = S // ST
    HC = DFF // 128
    NP = H // 2
    VB = 65
    VWID = H * VB
    VO = (HDH + 511) // 512
    DOC = (D + 511) // 512
    NST = (D + 511) // 512
    SCALE = 1.0 / float(np.sqrt(DH))
    C1 = 12582912.0
    C2 = 6755399441055744.0
    NSLAB = B * NW
    assert SK <= 128 and D % 128 == 0 and HDH % 128 == 0 and DFF % 128 == 0

    nc = bass.Bass("TRN2", target_bir_lowering=False, debug=False,
                   detect_race_conditions=detect_races)

    x_t = nc.dram_tensor("x", [B, S, D], F32, kind="ExternalInput")
    wqkv_t = nc.dram_tensor("w_qkv", [D, 3 * HDH], F32, kind="ExternalInput")
    wg_t = nc.dram_tensor("w_gate", [D, 1], F32, kind="ExternalInput")
    bg_t = nc.dram_tensor("b_gate", [1, 1], F32, kind="ExternalInput")
    wo_t = nc.dram_tensor("w_out", [HDH, D], F32, kind="ExternalInput")
    bo_t = nc.dram_tensor("b_out", [1, D], F32, kind="ExternalInput")
    wi_t = nc.dram_tensor("w_in", [D, DFF], F32, kind="ExternalInput")
    bi_t = nc.dram_tensor("b_in", [1, DFF], F32, kind="ExternalInput")
    wo2_t = nc.dram_tensor("w_out2", [DFF, D], F32, kind="ExternalInput")
    bo2_t = nc.dram_tensor("b_out2", [1, D], F32, kind="ExternalInput")
    out_t = nc.dram_tensor("out", [B, S, D], F32, kind="ExternalOutput")

    kscr = {"kind": "ExternalOutput"} if debug_spill else {}
    if debug_spill:
        xa_dbg = nc.dram_tensor("xa_dbg", [B, 128, (W // 128) * D], F32,
                                kind="ExternalOutput")
        ta_dbg = nc.dram_tensor("ta_dbg", [B, 128, (W // 128) * D], BF16,
                                kind="ExternalOutput")
        fa_dbg = nc.dram_tensor("fa_dbg", [B, 128, (D // 128) * W], BF16,
                                kind="ExternalOutput")
        aw_dbg = nc.dram_tensor("aw_dbg", [128, (D // 128) * DFF], BF16,
                                kind="ExternalOutput")
        bin_dbg = nc.dram_tensor("bin_dbg", [128, DFF // 128], F32,
                                 kind="ExternalOutput")
        sc_dbg = nc.dram_tensor("sc_dbg", [128, 8], F32, kind="ExternalOutput")
        wt_dbg = nc.dram_tensor("wt_dbg", [128, (D // 128) * 128], F32,
                                kind="ExternalOutput")
        wq_dbg = nc.dram_tensor("wq_dbg", [128, (D // 128) * 128], F32,
                                kind="ExternalOutput")
        sqd_dbg = nc.dram_tensor("sqd_dbg", [128, 6], F32, kind="ExternalOutput")
    q_scr = nc.dram_tensor("q_scr", [B, NW, QKC, 128, W], BF16, **kscr)
    k_scr = nc.dram_tensor("k_scr", [B, QKC, 128, S], BF16, **kscr)
    v_scr = nc.dram_tensor("v_scr", [B, S, VWID], BF16, **kscr)
    x1_scr = nc.dram_tensor("x1_scr", [B, S, D], F32, **kscr)
    xn2_scr = nc.dram_tensor("xn2_scr", [NSLAB, FC, 128, W], BF16, **kscr)
    hid_scr = nc.dram_tensor("hid_scr", [NSLAB, HC, 128, W], BF16, **kscr)
    cb_dbg = (nc.dram_tensor("cb_dbg", [B * NW, 128, (HDH // 128) * W], BF16)
              if debug_spill else None)
    cc1i = nc.dram_tensor("cc1i", [1, 8], F32)
    cc1o = nc.dram_tensor("cc1o", [1, 8], F32)
    cc2i = nc.dram_tensor("cc2i", [1, 8], F32)
    cc2o = nc.dram_tensor("cc2o", [1, 8], F32)

    off_wqk = 0
    off_wv = FC * 2 * HDH
    off_wg = off_wv + FC * HDH
    off_wo = off_wg + FC
    AW_COLS = max(off_wo + QKC * D, FC * DFF, HC * D)

    es = ExitStack()
    with es:
        sb = lambda n, s, d: es.enter_context(nc.sbuf_tensor(n, s, d))
        AW = sb("AW", [128, AW_COLS], BF16)
        XA = sb("XA", [128, TTW * D], F32)
        TA = sb("TA", [128, max(TTW * D, KC * W)], BF16)
        FA = sb("FA", [128, max(FC * W, 4 * W)], BF16)
        QW = sb("QW", [128, QKC * W], BF16)
        KW = sb("KW", [128, QKC * 2 * W], BF16)
        VW = sb("VW", [128, KC * VWID], BF16)
        X1 = [sb("X1A", [128, D], F32), sb("X1B", [128, D], F32)]
        ET = sb("ET", [128, KC * W], BF16)
        CB = sb("CB", [128, QKC * W], BF16)
        LU2 = [[sb("LUE0", [65, W], BF16), sb("LUE1", [65, W], BF16)],
               [sb("LUO0", [65, W], BF16), sb("LUO1", [65, W], BF16)]]
        GU2 = [[sb("GUE0", [65, W], BF16), sb("GUE1", [65, W], BF16)],
               [sb("GUO0", [65, W], BF16), sb("GUO1", [65, W], BF16)]]
        TLE = sb("TLE", [65, W], BF16)
        TLO = sb("TLO", [65, W], BF16)
        TT1 = sb("TT1", [65, W], BF16)
        TT2 = sb("TT2", [65, W], BF16)
        SCR1 = sb("SCR1", [128, 512], F32)
        SCR2 = sb("SCR2", [128, 512], F32)
        G1 = sb("G1", [65, S], BF16)
        GR1 = None

        AB4 = sb("AB4", [4, W], BF16)
        GMR = sb("GMR", [1, W], BF16)
        GRW = sb("GRW", [1, W], BF16)
        E4 = sb("E4", [65, 16], BF16)
        S4 = sb("S4", [4, 4 * 65], BF16)
        R1S = sb("R1S", [1, 16], BF16)
        GS2 = sb("GS2", [65, 8], BF16)
        KSTR = sb("KSTR", [128, B * QKC * SK], BF16)
        VSTR = sb("VSTR", [SK, B * VWID], BF16)
        BIN = sb("BIN", [128, HC], F32)
        BB2 = sb("BB2", [128, D], F32)
        BOR = sb("BOR", [1, D], BF16)
        IDB = sb("IDB", [128, 128], BF16)
        IDF = sb("IDF", [128, 128], F32)
        SHE = sb("SHE", [65, 128], BF16)
        SHO = sb("SHO", [65, 128], BF16)
        ONEB = sb("ONEB", [1, 128], BF16)
        ONEF = sb("ONEF", [1, 128], F32)
        EPS = sb("EPS", [128, 1], F32)
        ZR8 = sb("ZR8", [1, 8], F32)
        NBG = sb("NBG", [1, 1], F32)
        BGT = sb("BGT", [1, 1], F32)
        MV6 = [sb("MV6A", [128, 6 * NST], F32), sb("MV6B", [128, 6 * NST], F32)]
        MV2 = [sb("MV2A", [128, 2], F32), sb("MV2B", [128, 2], F32)]
        LNV = [sb("LNVA", [128, 1], F32), sb("LNVB", [128, 1], F32)]
        RST = [sb("RSTA", [128, 1], F32), sb("RSTB", [128, 1], F32)]
        AM1 = sb("AM1", [128, NSLAB * TTW * NST], BF16)
        AM2 = sb("AM2", [128, NSLAB * HC], BF16)
        AMV = sb("AMV", [128, 1], F32)
        SCAL = sb("SCAL", [128, 8], F32)
        CCA = sb("CCA", [1, 8], F32)
        CCB = sb("CCB", [1, 8], F32)
        SQD = sb("SQD", [128, 6], F32)
        DAB = sb("DAB", [128, 6], F32)


        PA = [es.enter_context(nc.psum_tensor(f"PA{i}", [128, 1024], F32))
              for i in range(4)]
        PS = [PA[i // 2][:, (i % 2) * 512:(i % 2) * 512 + 512] for i in range(8)]

        esems = {e: es.enter_context(nc.semaphore(f"sem_{e}"))
                 for e in ("pe", "act", "dve", "gp", "sp")}
        chain_sem = {e: es.enter_context(nc.semaphore(f"chain_{e}"))
                     for e in ("dve", "act")}
        chain_cnt = {"dve": 0, "act": 0}

        def ch(engname, eng, inst):
            chain_cnt[engname] += 1
            inst.then_inc(chain_sem[engname], 1)
            eng.wait_ge(chain_sem[engname], chain_cnt[engname])
            return inst
        dsems = {"gp": es.enter_context(nc.semaphore("dma_gp")),
                 "sp": es.enter_context(nc.semaphore("dma_sp")),
                 "act": es.enter_context(nc.semaphore("dma_act"))}

        pr = Prog()
        R = {}

        def rs(name):
            if name not in R:
                R[name] = Res(name)
            return R[name]

        rPS = [rs(f"PS{i}") for i in range(8)]

        def fsetup_gp(g):
            g.iota(SCR1[0:128, 0:128][:, :], [[1, 128]], channel_multiplier=0,
                   allow_small_or_imprecise_dtypes=True)
            g.iota(SCR2[0:128, 0:128][:, :], [[0, 128]], channel_multiplier=1,
                   allow_small_or_imprecise_dtypes=True)
            g.iota(XA[0:65, 0:128], [[0, 128]], base=-1, channel_multiplier=1,
                   allow_small_or_imprecise_dtypes=True)
            g.iota(XA[0:65, 128:256], [[0, 128]], base=63, channel_multiplier=1,
                   allow_small_or_imprecise_dtypes=True)
            g.memset(ONEB[:, :], 1.0)
            g.memset(E4[:, :], 0.0)
            g.memset(R1S[:, :], 0.0)
            g.memset(GS2[:, :], 0.0)
            for i in range(4):
                g.memset(E4[0:1, i * 4 + i: i * 4 + i + 1], 1.0)
                g.memset(R1S[0:1, i * 4 + i: i * 4 + i + 1], 1.0)
            g.memset(GS2[0:1, 0:2], 1.0)
            g.memset(GS2[0:1, 6:8], 1.0)
            g.memset(GS2[64:65, 0:2], 1.0)
            g.memset(GS2[64:65, 6:8], 1.0)
            g.memset(ONEF[:, :], 1.0)
            g.memset(DAB[:, 3:4], 1.0)
            g.memset(DAB[:, 4:5], C1)
            g.memset(DAB[:, 5:6], C2)
            g.memset(EPS[:, :], 1e-5)
            g.memset(ZR8[:, :], 0.0)
            return g.memset(VW[:, 0:KC * VWID:VB], 1.0)

        pr.st("gp", fsetup_gp,
              writes=[rs("ONEB"), rs("ONEF"), rs("EPS"), rs("ZR8"), rs("VW"),
                      rs("SCR1"), rs("SCR2"), rs("XAt0"), rs("E4"), rs("R1S"),
                      rs("GS2"), rs("DAB")])

        def fsetup_ve(v):
            v.tensor_tensor(IDB[:, :], SCR1[0:128, 0:128][:, :],
                            SCR2[0:128, 0:128][:, :], ALU.is_equal)
            v.tensor_tensor(IDF[:, :], SCR1[0:128, 0:128][:, :],
                            SCR2[0:128, 0:128][:, :], ALU.is_equal)
            v.tensor_tensor(SHE[:, :], SCR1[0:128, 0:128][0:65, :],
                            XA[0:65, 0:128], ALU.is_equal)
            v.tensor_tensor(SHO[:, :], SCR1[0:128, 0:128][0:65, :],
                            XA[0:65, 128:256], ALU.is_equal)
            return v.memset(SHO[0:1, :], 0.0)

        pr.st("dve", fsetup_ve,
              reads=[rs("SCR1"), rs("SCR2"), rs("XAt0")],
              writes=[rs("IDB"), rs("IDF"), rs("SHE"), rs("SHO")])

        def fdab(v):
            ch("dve", v, v.tensor_scalar(DAB[:, 0:1], DAB[:, 3:4], C2, C2,
                                         ALU.add, ALU.subtract))
            ch("dve", v, v.tensor_scalar(DAB[:, 1:2], DAB[:, 0:1], -1.0, 1.0,
                                         ALU.mult, ALU.add))
            ch("dve", v, v.tensor_scalar(DAB[:, 2:3], DAB[:, 0:1], C2, None,
                                         ALU.mult))
            ch("dve", v, v.tensor_scalar(DAB[:, 3:4], DAB[:, 1:2], C1, None,
                                         ALU.mult))
            ch("dve", v, v.tensor_add(DAB[:, 2:3], DAB[:, 2:3], DAB[:, 3:4]))
            return v.tensor_scalar_mul(DAB[:, 2:3], DAB[:, 2:3], -1.0)
        pr.st("dve", fdab, writes=[rs("DAB")])

        def rint_split(dst, src, src_res, scal, scal_res, n, wr):
            def fa(sc):
                ch("act", sc, sc.activation(SCR1[:, 0:n], src, AF.Copy,
                                            scale=scal, bias=C1))
                return sc.activation(SCR2[:, 0:n], src, AF.Copy,
                                     scale=scal, bias=C2)
            pr.st("act", fa, reads=[src_res, scal_res, rs("DAB")],
                  writes=[rs("SCR1"), rs("SCR2")])

            def fv(v):
                v.tensor_scalar(SCR1[:, 0:n], SCR1[:, 0:n], DAB[:, 1:2],
                                DAB[:, 2:3], ALU.mult, ALU.add)
                return v.scalar_tensor_tensor(dst, SCR2[:, 0:n], DAB[:, 0:1],
                                              SCR1[:, 0:n], ALU.mult, ALU.add)
            pr.st("dve", fv, reads=[rs("SCR1"), rs("SCR2"), rs("DAB")],
                  writes=[rs("SCR1")] + wr)

        def rint_ops(v, dst, src, scal, n):
            v.tensor_scalar(SCR1[:, 0:n], src, scal, C1, ALU.mult, ALU.add)
            v.tensor_scalar(SCR2[:, 0:n], src, scal, C2, ALU.mult, ALU.add)
            v.tensor_scalar(SCR1[:, 0:n], SCR1[:, 0:n], DAB[:, 1:2],
                            DAB[:, 2:3], ALU.mult, ALU.add)
            return v.scalar_tensor_tensor(dst, SCR2[:, 0:n], DAB[:, 0:1],
                                          SCR1[:, 0:n], ALU.mult, ALU.add)

        for fc in range(FC):
            pr.st("gp", lambda g, fc=fc: g.dma_start(
                out=AW[:, off_wqk + fc * 2 * HDH: off_wqk + (fc + 1) * 2 * HDH],
                in_=bass.AP(wqkv_t, fc * 128 * 3 * HDH, [[3 * HDH, 128], [1, 2 * HDH]])),
                writes=[rs("AW")], dma=True)
            pr.st("gp", lambda g, fc=fc: g.dma_start(
                out=AW[:, off_wv + fc * HDH: off_wv + (fc + 1) * HDH],
                in_=bass.AP(wqkv_t, fc * 128 * 3 * HDH + 2 * HDH,
                            [[3 * HDH, 128], [1, HDH]])),
                writes=[rs("AW")], dma=True)
        pr.st("gp", lambda g: g.dma_start(
            out=AW[:, off_wg: off_wg + FC],
            in_=bass.AP(wg_t, 0, [[1, 128], [128, FC]]),
            allow_slow_non_contiguous=True),
            writes=[rs("AW")], dma=True)
        for c in range(QKC):
            pr.st("gp", lambda g, c=c: g.dma_start(
                out=AW[:, off_wo + c * D: off_wo + (c + 1) * D],
                in_=bass.AP(wo_t, c * 128 * D, [[D, 128], [1, D]])),
                writes=[rs("AWO")], dma=True)
        pr.st("gp", lambda g: g.dma_start(out=BOR[:, :], in_=bo_t[:, :]),
              writes=[rs("BOR")], dma=True)
        pr.st("gp", lambda g: g.dma_start(out=BGT[:, :], in_=bg_t[:, :]),
              writes=[rs("BGT")], dma=True)
        pr.st("sp", lambda s: s.dma_start(
            out=BIN[:, :], in_=bass.AP(bi_t, 0, [[1, 128], [128, HC]]),
            allow_slow_non_contiguous=True),
            writes=[rs("BIN")], dma=True)
        pr.st("sp", lambda s: s.dma_start(out=XA[0:1, 0:D], in_=bo2_t[:, :]),
              writes=[rs("XAt0")], dma=True)
        pr.st("dve", lambda v: v.tensor_scalar_mul(NBG[:, :], BGT[:, :], -1.0),
              reads=[rs("BGT")], writes=[rs("NBG")])
        for oc in range(DOC):
            n = min(512, D - oc * 512)
            pr.st("pe", lambda t, oc=oc, n=n: t.matmul(
                PS[0][0:128, 0:n], ONEF[:, :], XA[0:1, oc * 512: oc * 512 + n],
                start=True, stop=True),
                reads=[rs("ONEF"), rs("XAt0")], writes=[rPS[0]])
            pr.st("dve", lambda v, oc=oc, n=n: v.tensor_copy(
                BB2[:, oc * 512: oc * 512 + n], PS[0][0:128, 0:n]),
                reads=[rPS[0]], writes=[rs("BB2")])

        def fs4(t):
            last = None
            for i in range(4):
                last = t.matmul(PS[2][0:4, i * 65:(i + 1) * 65],
                                R1S[0:1, i * 4:(i + 1) * 4], ONEB[0:1, 0:65],
                                start=True, stop=True)
            return last
        pr.st("pe", fs4, reads=[rs("R1S"), rs("ONEB")], writes=[rPS[2]])
        pr.st("dve", lambda v: v.tensor_copy(S4[:, :], PS[2][0:4, 0:4 * 65]),
              reads=[rPS[2]], writes=[rs("S4")])

        def ln_stats(src_res, src_ap_fn, par=0):
            mv6, mv2, lnv, rst = MV6[par], MV2[par], LNV[par], RST[par]

            def f(v, mv6=mv6, mv2=mv2):
                for i in range(NST):
                    n = min(512, D - i * 512)
                    ch("dve", v, v.bn_stats(mv6[:, 6 * i: 6 * i + 6],
                                            src_ap_fn(i, n)))
                return v.bn_aggr(mv2[:, :], mv6[:, 0:6 * NST])
            pr.st("dve", f, reads=[src_res],
                  writes=[rs(f"MV6{par}"), rs(f"MV2{par}")])

            def f2(sc, mv2=mv2, lnv=lnv, rst=rst):
                ch("act", sc, sc.activation(lnv[:, :], mv2[:, 1:2], AF.Ln,
                                            bias=EPS[:, :]))
                return sc.activation(rst[:, :], lnv[:, :], AF.Exp, scale=-0.5)
            pr.st("act", f2, reads=[rs(f"MV2{par}"), rs("EPS")],
                  writes=[rs(f"LNV{par}"), rs(f"RST{par}")])

        QW2 = AW[:, 0:QKC * W]
        KW2 = AW[:, QKC * W: QKC * W + QKC * 2 * W]
        VW2 = AW[:, QKC * 3 * W: QKC * 3 * W + KC * VWID]
        for b in range(B):
            for s in range(NW):
                FAb = FA if s % 2 == 0 else CB
                rFA = rs("FA") if s % 2 == 0 else rs("CB")
                pr.st("sp", lambda e, b=b, s=s: e.dma_start(
                    out=XA[:, :].rearrange("p (t d) -> p t d", t=TTW),
                    in_=bass.AP(x_t, (b * S + s * W) * D,
                                [[D, 128], [128 * D, TTW], [1, D]])),
                    writes=[rs(f"XAt{tt}") for tt in range(TTW)], dma=True)
                for tt in range(TTW):
                    par = tt % 2
                    x1b = X1[par]
                    rx1 = rs(f"X1{par}")
                    ln_stats(rs(f"XAt{tt}"),
                             lambda i, n, tt=tt: XA[:, tt * D + i * 512: tt * D + i * 512 + n],
                             par)
                    pr.st("dve", lambda v, tt=tt, par=par, x1b=x1b: v.tensor_scalar(
                        x1b[:, :], XA[:, tt * D: (tt + 1) * D],
                        MV2[par][:, 0:1], RST[par][:, :], ALU.subtract, ALU.mult),
                        reads=[rs(f"XAt{tt}"), rs(f"MV2{par}"), rs(f"RST{par}")],
                        writes=[rx1])
                    for rnd in range(2):
                        tbank = ((7, 4) if tt % 2 == 0 else (5, 6))[rnd]

                        def ftr(t, tt=tt, rnd=rnd, tbank=tbank, x1b=x1b):
                            last = None
                            for j in range(4):
                                fc = rnd * 4 + j
                                last = t.transpose(
                                    PS[tbank][:, j * 128:(j + 1) * 128],
                                    x1b[:, fc * 128:(fc + 1) * 128],
                                    IDF[:, :])
                            return last
                        pr.st("pe", ftr, reads=[rx1, rs("IDF")],
                              writes=[rPS[tbank]])
                        pr.st("dve", lambda v, tt=tt, rnd=rnd, tbank=tbank, FAb=FAb:
                              v.tensor_copy(
                                  FAb[:, 0:FC * W].rearrange(
                                      "p (c w) -> p c w", c=FC)[:, rnd * 4:rnd * 4 + 4,
                                                                tt * 128:(tt + 1) * 128],
                                  PS[tbank][:, 0:512].rearrange(
                                      "p (c k) -> p c k", c=4)),
                              reads=[rPS[tbank]], writes=[rFA])
                if debug_spill == "spill" and s == 0:
                    pr.st("sp", lambda e, b=b: e.dma_start(
                        out=bass.AP(xa_dbg, b * 128 * TTW * D, [[TTW * D, 128], [1, TTW * D]]),
                        in_=XA[:, :]), reads=[rs("XA")], dma=True)
                    pr.st("sp", lambda e, b=b: e.dma_start(
                        out=bass.AP(ta_dbg, b * 128 * TTW * D, [[TTW * D, 128], [1, TTW * D]]),
                        in_=TA[:, :]), reads=[rs("TA")], dma=True)
                    pr.st("sp", lambda e, b=b: e.dma_start(
                        out=bass.AP(fa_dbg, b * 128 * FC * W, [[FC * W, 128], [1, FC * W]]),
                        in_=FA[:, :]), reads=[rs("FA")], dma=True)
                for oc in range(2 * QKC):
                    bank = oc % 2
                    def fqk(t, oc=oc, bank=bank, FAb=FAb):
                        last = None
                        for fc in range(FC):
                            last = t.matmul(
                                PS[bank][0:128, 0:W],
                                AW[:, off_wqk + fc * 2 * HDH + oc * 128:
                                   off_wqk + fc * 2 * HDH + (oc + 1) * 128],
                                FAb[:, fc * W:(fc + 1) * W],
                                start=(fc == 0), stop=(fc == FC - 1))
                        return last
                    pr.st("pe", fqk, reads=[rs("AW"), rFA], writes=[rPS[bank]])
                    if oc < QKC:
                        pr.st("act", lambda sc, oc=oc, bank=bank: sc.copy(
                            QW[:, oc * W:(oc + 1) * W], PS[bank][0:128, 0:W]),
                            reads=[rPS[bank]], writes=[rs("QW")])
                    else:
                        c = oc - QKC
                        pr.st("act", lambda sc, c=c, bank=bank: sc.copy(
                            KW[:, c * W:(c + 1) * W], PS[bank][0:128, 0:W]),
                            reads=[rPS[bank]], writes=[rs("KW")])
                        pr.st("dve", lambda v, c=c, s=s, b=b: v.tensor_copy(
                            KSTR[:, b * QKC * SK + c * SK + s * (W // ST):
                                 b * QKC * SK + c * SK + (s + 1) * (W // ST)],
                            KW[:, c * W: (c + 1) * W: ST]),
                            reads=[rs("KW")], writes=[rs("KSTR")])
                pr.st("gp", lambda e, b=b, s=s: e.dma_start(
                    out=bass.AP(q_scr, ((b * NW + s) * QKC) * 128 * W,
                                [[W, 128], [128 * W, QKC], [1, W]]),
                    in_=QW[:, :].rearrange("p (c w) -> p c w", c=QKC)),
                    reads=[rs("QW")], writes=[rs(f"q_scr_{b}_{s}")], dma=True)
                for c in range(QKC):
                    pr.st("gp", lambda e, b=b, s=s, c=c: e.dma_start(
                        out=bass.AP(k_scr, (b * QKC + c) * 128 * S + s * W,
                                    [[S, 128], [1, W]]),
                        in_=KW[:, c * W:(c + 1) * W]),
                        reads=[rs("KW")], writes=[rs(f"k_scr_{b}_{s}")], dma=True)
                for tt in range(TTW):
                    for oc in range(VO):
                        n = min(512, HDH - oc * 512)
                        bank = 2 + (tt * VO + oc) % 2
                        def fv(t, tt=tt, oc=oc, n=n, bank=bank, FAb=FAb):
                            last = None
                            for fc in range(FC):
                                last = t.matmul(
                                    PS[bank][0:128, 0:n],
                                    FAb[:, fc * W + tt * 128: fc * W + (tt + 1) * 128],
                                    AW[:, off_wv + fc * HDH + oc * 512:
                                       off_wv + fc * HDH + oc * 512 + n],
                                    start=(fc == 0), stop=(fc == FC - 1))
                            return last
                        pr.st("pe", fv, reads=[rFA, rs("AW")], writes=[rPS[bank]])
                        nh = n // DH
                        pr.st("dve", lambda v, tt=tt, oc=oc, n=n, nh=nh, bank=bank:
                              v.tensor_copy(
                                  VW[:, tt * VWID + oc * 8 * VB:
                                     tt * VWID + oc * 8 * VB + nh * VB].rearrange(
                                         "p (h c) -> p h c", c=VB)[:, :, 1:VB],
                                  PS[bank][0:128, 0:n].rearrange(
                                      "p (h c) -> p h c", c=DH)),
                              reads=[rPS[bank]], writes=[rs("VW")])
                def fg(t, FAb=FAb):
                    last = None
                    for fc in range(FC):
                        last = t.matmul(PS[6][0:1, 0:W],
                                        AW[:, off_wg + fc: off_wg + fc + 1],
                                        FAb[:, fc * W:(fc + 1) * W],
                                        start=(fc == 0), stop=(fc == FC - 1))
                    return last
                pr.st("pe", fg, reads=[rs("AW"), rFA], writes=[rPS[6]])
                pr.st("act", lambda sc: sc.activation(
                    SCR1[0:1, 0:W], PS[6][0:1, 0:W], AF.Exp, scale=-1.0,
                    bias=NBG[:, :]),
                    reads=[rPS[6], rs("NBG")], writes=[rs("SCR1")])

                def fsig(v, b=b, s=s):
                    ch("dve", v, v.tensor_scalar_add(SCR2[0:1, 0:W],
                                                     SCR1[0:1, 0:W], 1.0))
                    ch("dve", v, v.reciprocal(SCR1[0:1, 0:W], SCR2[0:1, 0:W]))
                    return v.tensor_copy(G1[64 * b:64 * b + 1, s * W:(s + 1) * W],
                                         SCR1[0:1, 0:W])
                pr.st("dve", fsig, reads=[rs("SCR1")],
                      writes=[rs("SCR1"), rs("SCR2"), rs("G1")])
                pr.st("gp", lambda e, b=b, s=s: e.dma_start(
                    out=bass.AP(v_scr, (b * S + s * W) * VWID,
                                [[VWID, 128], [128 * VWID, TTW], [1, VWID]]),
                    in_=VW[:, 0:TTW * VWID].rearrange("p (t c) -> p t c", t=TTW)),
                    reads=[rs("VW")], writes=[rs(f"v_scr_{b}_{s}")], dma=True)

            pr.st("sp", lambda e, b=b: e.dma_start(
                out=VSTR[:, b * VWID:(b + 1) * VWID],
                in_=bass.AP(v_scr, b * S * VWID, [[VWID * ST, SK], [1, VWID]])),
                reads=[rs(f"v_scr_{b}_{s}") for s in range(NW)],
                writes=[rs("VSTR")], dma=True)

        def a2_buffers(gw):
            p = gw % 2
            return ((QW, rs("QW")) if p == 0 else (QW2, rs("QW2")),
                    (KW, rs("KW")) if p == 0 else (KW2, rs("KW2")),
                    (VW, rs("VW")) if p == 0 else (VW2, rs("VW2")))

        def a2_kcs(w):
            kbase = w * W - W // 2
            kcs = [kc for kc in range(KC)
                   if kbase + kc * 128 >= 0 and kbase + (kc + 1) * 128 <= S]
            return kbase, kcs

        def emit_a2_qkv_loads(gw):
            b, w = divmod(gw, NW)
            (QWb, rQW), (KWb, rKW), (VWb, rVW) = a2_buffers(gw)
            kbase, kcs = a2_kcs(w)
            j0, j1 = kcs[0], kcs[-1]
            qw_w = ([rs("AW")] if gw == 1 else [])
            pr.st("sp", lambda e, b=b, w=w, QWb=QWb: e.dma_start(
                out=QWb[:, :].rearrange("p (c w) -> p c w", c=QKC),
                in_=bass.AP(q_scr, (b * NW + w) * QKC * 128 * W,
                            [[W, 128], [128 * W, QKC], [1, W]])),
                reads=[rs(f"q_scr_{b}_{w}")], writes=[rQW] + qw_w, dma=True)
            for c in range(QKC):
                lo = kbase + j0 * 128
                nk = (j1 - j0 + 1) * 128
                kres = [rs(f"k_scr_{b}_{sx}")
                        for sx in range(max(0, w - 1), min(NW, w + 2))]
                pr.st("sp", lambda e, b=b, c=c, lo=lo, nk=nk, j0=j0, KWb=KWb:
                      e.dma_start(
                          out=KWb[:, c * 2 * W + j0 * 128: c * 2 * W + j0 * 128 + nk],
                          in_=bass.AP(k_scr, (b * QKC + c) * 128 * S + lo,
                                      [[S, 128], [1, nk]])),
                      reads=kres, writes=[rKW] + qw_w, dma=True)
            lo = kbase + j0 * 128
            nj = j1 - j0 + 1
            vres = [rs(f"v_scr_{b}_{sx}")
                    for sx in range(max(0, w - 1), min(NW, w + 2))]
            pr.st("sp", lambda e, b=b, lo=lo, nj=nj, j0=j0, VWb=VWb: e.dma_start(
                out=VWb[:, j0 * VWID:(j0 + nj) * VWID].rearrange(
                    "p (t c) -> p t c", t=nj),
                in_=bass.AP(v_scr, (b * S + lo) * VWID,
                            [[VWID, 128], [128 * VWID, nj], [1, VWID]])),
                reads=vres, writes=[rVW] + qw_w, dma=True)

        emit_a2_qkv_loads(0)
        for gw in range(B * NW):
            b, w = divmod(gw, NW)
            (QWb, rQW), (KWb, rKW), (VWb, rVW) = a2_buffers(gw)
            kbase, kcs = a2_kcs(w)
            j0, j1 = kcs[0], kcs[-1]
            if True:
                for tt in range(TTW):
                    pr.st("sp", lambda e, b=b, w=w, tt=tt: e.dma_start(
                        out=XA[:, tt * D:(tt + 1) * D],
                        in_=bass.AP(x_t, (b * S + w * W + tt * 128) * D,
                                    [[D, 128], [1, D]])),
                        writes=[rs(f"XAt{tt}")], dma=True)
                if gw + 1 < B * NW:
                    emit_a2_qkv_loads(gw + 1)
                def fgst(v, b=b, w=w):
                    ch("dve", v, v.tensor_copy(
                        GRW[:, :], G1[64 * b:64 * b + 1, w * W:(w + 1) * W]))
                    return v.tensor_scalar(
                        GMR[:, :], GRW[:, :], -1.0, 1.0, ALU.mult, ALU.add)
                pr.st("dve", fgst, reads=[rs("G1")],
                      writes=[rs("GRW"), rs("GMR")])

                def fg4(t):
                    t.matmul(PS[1][0:4, 0:W], GS2[0:1, 0:4], GRW[:, :],
                             start=True, stop=False,
                             skip_group_check=True)
                    return t.matmul(PS[1][0:4, 0:W], GS2[0:1, 4:8], GMR[:, :],
                                    start=False, stop=True, skip_group_check=True)
                pr.st("pe", fg4, reads=[rs("GS2"), rs("GRW"), rs("GMR")],
                      writes=[rPS[1]])
                pr.st("dve", lambda v: v.tensor_copy(SCR1[0:4, 0:W],
                                                     PS[1][0:4, 0:W]),
                      reads=[rPS[1]], writes=[rs("SCR1")])

                def emit_front(pair, kcs=tuple(kcs), w=w):
                    pp = pair % 2
                    qq = (pair + 1) % 2
                    blist = (0, 1, 2 + 2 * qq, 3 + 2 * qq)
                    for sub in range(2):
                        h = pair * 2 + sub
                        hb = 64 * (h % 2)
                        hc2 = h // 2
                        avbank = 2 + sub + 2 * pp
                        ETb = ET if sub == 0 else TA
                        rET = rs("ET") if sub == 0 else rs("TA")
                        for i, kc in enumerate(kcs):
                            bank = blist[(sub * len(kcs) + i) % 4]
                            pr.st("pe", lambda t, hb=hb, hc2=hc2, kc=kc,
                                  bank=bank, KWb=KWb, QWb=QWb:
                                  t.matmul(PS[bank][0:128, 0:W],
                                           KWb[hb:hb + 64,
                                               hc2 * 2 * W + kc * 128: hc2 * 2 * W + (kc + 1) * 128],
                                           QWb[hb:hb + 64, hc2 * W:(hc2 + 1) * W],
                                           start=True, stop=True),
                                  reads=[rKW, rQW], writes=[rPS[bank]])
                            if i % 2 == 1:
                                b0 = blist[(sub * len(kcs) + i - 1) % 4]
                                kc0 = kcs[i - 1]
                                pr.st("act", lambda sc, kc0=kc0, b0=b0, ETb=ETb:
                                      sc.activation(
                                          ETb[:, kc0 * W:(kc0 + 2) * W],
                                          PA[b0 // 2][0:128, 0:2 * W],
                                          AF.Exp, scale=SCALE),
                                      reads=[rPS[b0], rPS[b0 + 1]], writes=[rET])

                        def fav(t, h=h, kcs=kcs, avbank=avbank, ETb=ETb, VWb=VWb):
                            last = None
                            for i, kc in enumerate(kcs):
                                last = t.matmul(
                                    PS[avbank][0:65, 0:W],
                                    VWb[:, kc * VWID + h * VB: kc * VWID + (h + 1) * VB],
                                    ETb[:, kc * W:(kc + 1) * W],
                                    start=(i == 0), stop=(i == len(kcs) - 1))
                            return last
                        pr.st("pe", fav, reads=[rVW, rET],
                              writes=[rPS[avbank]])
                        pr.st("act", lambda sc, avbank=avbank,
                              LU=LU2[sub][pp]: sc.copy(
                                  LU[:, :], PS[avbank][0:65, 0:W]),
                              reads=[rPS[avbank]],
                              writes=[rs(f"LU{sub}{pp}")])

                def emit_back(pair, w=w):
                    pp = pair % 2
                    gb = 6 + pp
                    for sub in range(2):
                        h = pair * 2 + sub
                        hb = 64 * (h % 2)
                        hc2 = h // 2
                        sbank = sub
                        pr.st("pe", lambda t, hb=hb, hc2=hc2, sbank=sbank,
                            b=b, QWb=QWb: t.matmul(
                            PS[sbank][0:SK, 0:W],
                            KSTR[hb:hb + 64, b * QKC * SK + hc2 * SK:
                                 b * QKC * SK + (hc2 + 1) * SK],
                            QWb[hb:hb + 64, hc2 * W:(hc2 + 1) * W],
                            start=True, stop=True),
                            reads=[rs("KSTR"), rQW], writes=[rPS[sbank]])
                    pr.st("act", lambda sc, pp=pp: sc.activation(
                        FA[0:SK, 2 * pp * W:2 * pp * W + 2 * W],
                        PA[0][0:SK, 0:2 * W], AF.Exp, scale=SCALE),
                        reads=[rPS[0], rPS[1]],
                        writes=[rs(f"EG0{pp}"), rs(f"EG1{pp}")])
                    for sub in range(2):
                        h = pair * 2 + sub
                        ego = (2 * pp + sub) * W
                        pr.st("pe", lambda t, h=h, gb=gb, ego=ego, b=b: t.matmul(
                            PS[gb][0:65, 0:W],
                            VSTR[0:SK, b * VWID + h * VB: b * VWID + (h + 1) * VB],
                            FA[0:SK, ego:ego + W], start=True, stop=True),
                            reads=[rs("VSTR"), rs(f"EG{sub}{pp}")], writes=[rPS[gb]])
                        pr.st("act", lambda sc, gb=gb, GU=GU2[sub][pp]: sc.copy(
                            GU[:, :], PS[gb][0:65, 0:W]),
                            reads=[rPS[gb]], writes=[rs(f"GU{sub}{pp}")])

                    def fgat(t, gb=gb, pp=pp):
                        last = None
                        for i, srcb in enumerate((LU2[0][pp], LU2[1][pp],
                                                  GU2[0][pp], GU2[1][pp])):
                            last = t.matmul(PS[gb][0:4, 0:W],
                                            E4[:, i * 4:(i + 1) * 4], srcb[:, :],
                                            start=(i == 0), stop=(i == 3))
                        return last
                    pr.st("pe", fgat,
                          reads=[rs("E4"), rs(f"LU0{pp}"), rs(f"LU1{pp}"),
                                 rs(f"GU0{pp}"), rs(f"GU1{pp}")],
                          writes=[rPS[gb]])

                    def frcp(v, gb=gb):
                        ch("dve", v, v.reciprocal(SCR2[0:4, 0:W],
                                                  PS[gb][0:4, 0:W]))
                        return v.tensor_mul(AB4[:, :], SCR2[0:4, 0:W],
                                            SCR1[0:4, 0:W])
                    pr.st("dve", frcp, reads=[rPS[gb], rs("SCR1")],
                          writes=[rs("SCR2"), rs("AB4")])

                    for sub in range(2):
                        LU = LU2[sub][pp]
                        GU = GU2[sub][pp]
                        rLU = rs(f"LU{sub}{pp}")
                        rGU = rs(f"GU{sub}{pp}")
                        tl = TLE if sub == 0 else TLO
                        tlr = rs("TLE") if sub == 0 else rs("TLO")
                        tt = TT1 if sub == 0 else TT2
                        rtt = rs("TT1") if sub == 0 else rs("TT2")
                        pr.st("pe", lambda t, sub=sub, gb=gb: t.matmul(
                            PS[gb][0:65, 0:W], S4[:, sub * 65:(sub + 1) * 65],
                            AB4[:, :], start=True, stop=True),
                            reads=[rs("S4"), rs("AB4")], writes=[rPS[gb]])
                        pr.st("dve", lambda v, tt=tt, LU=LU, gb=gb: v.tensor_mul(
                            tt[:, :], LU[:, :], PS[gb][0:65, 0:W]),
                            reads=[rLU, rPS[gb]], writes=[rtt])
                        pr.st("pe", lambda t, sub=sub, gb=gb: t.matmul(
                            PS[gb][0:65, 0:W],
                            S4[:, (2 + sub) * 65:(3 + sub) * 65],
                            AB4[:, :], start=True, stop=True),
                            reads=[rs("S4"), rs("AB4")], writes=[rPS[gb]])

                        def fcmb(v, tl=tl, tt=tt, GU=GU, gb=gb):
                            v.tensor_mul(tl[:, :], GU[:, :], PS[gb][0:65, 0:W])
                            return v.tensor_add(tl[:, :], tt[:, :], tl[:, :])
                        pr.st("dve", fcmb,
                              reads=[rGU, rPS[gb], rtt],
                              writes=[tlr])

                    def fasm(t, gb=gb):
                        t.matmul(PS[gb][0:128, 0:W], SHE[:, :], TLE[:, :],
                                 start=True, stop=False, skip_group_check=True)
                        return t.matmul(PS[gb][0:128, 0:W], SHO[:, :], TLO[:, :],
                                        start=False, stop=True, skip_group_check=True)
                    pr.st("pe", fasm,
                          reads=[rs("SHE"), rs("SHO"), rs("TLE"), rs("TLO")],
                          writes=[rPS[gb]])
                    pr.st("dve", lambda v, pair=pair, gb=gb: v.tensor_copy(
                        CB[:, pair * W:(pair + 1) * W], PS[gb][0:128, 0:W]),
                        reads=[rPS[gb]], writes=[rs("CB")])

                for pair in range(NP):
                    emit_front(pair)
                    if pair >= 1:
                        emit_back(pair - 1)
                emit_back(NP - 1)

                for tt in range(TTW):
                    x1b = X1[tt % 2]
                    rx1 = rs(f"X1{tt % 2}")
                    for oc in range(DOC):
                        n = min(512, D - oc * 512)
                        bank = oc % 2

                        def fop(t, tt=tt, oc=oc, n=n, bank=bank):
                            t.matmul(PS[bank][0:128, 0:n], ONEB[0:1, 0:128],
                                     BOR[0:1, oc * 512: oc * 512 + n],
                                     start=True, stop=False, skip_group_check=True)
                            last = None
                            for c in range(QKC):
                                last = t.matmul(
                                    PS[bank][0:128, 0:n],
                                    CB[:, c * W + tt * 128: c * W + (tt + 1) * 128],
                                    AW[:, off_wo + c * D + oc * 512:
                                       off_wo + c * D + oc * 512 + n],
                                    start=False, stop=(c == QKC - 1),
                                    skip_group_check=True)
                            return last
                        pr.st("pe", fop,
                              reads=[rs("ONEB"), rs("BOR"), rs("CB"), rs("AWO")],
                              writes=[rPS[bank]])
                        pr.st("dve", lambda v, x1b=x1b, tt=tt, oc=oc, n=n, bank=bank:
                              v.tensor_add(x1b[:, oc * 512: oc * 512 + n],
                                           PS[bank][0:128, 0:n],
                                           XA[:, tt * D + oc * 512: tt * D + oc * 512 + n]),
                              reads=[rPS[bank], rs(f"XAt{tt}")], writes=[rx1])
                    pr.st("gp", lambda e, b=b, w=w, tt=tt, x1b=x1b: e.dma_start(
                        out=bass.AP(x1_scr, (b * S + w * W + tt * 128) * D,
                                    [[D, 128], [1, D]]),
                        in_=x1b[:, :]),
                        reads=[rx1], writes=[rs(f"x1_scr_{b * NW + w}")], dma=True)
                    par = tt % 2
                    ln_stats(rx1, lambda i, n, x1b=x1b: x1b[:, i * 512: i * 512 + n],
                             par)
                    slot0 = ((b * NW + w) * TTW + tt) * NST
                    for hlf in range(NST):
                        scr = SCR1 if hlf == 0 else SCR2
                        rscr = rs("SCR1") if hlf == 0 else rs("SCR2")
                        pr.st("dve", lambda v, x1b=x1b, par=par, hlf=hlf, scr=scr:
                              v.tensor_scalar(
                                  scr[:, 0:512], x1b[:, hlf * 512:(hlf + 1) * 512],
                                  MV2[par][:, 0:1], RST[par][:, :],
                                  ALU.subtract, ALU.mult),
                              reads=[rx1, rs(f"MV2{par}"), rs(f"RST{par}")],
                              writes=[rscr])
                        pr.st("dve", lambda v, hlf=hlf, slot0=slot0, scr=scr:
                              v.tensor_reduce(
                                  AM1[:, slot0 + hlf: slot0 + hlf + 1],
                                  scr[:, 0:512],
                                  AX.X, ALU.max, apply_absolute_value=True),
                              reads=[rscr], writes=[rs("AM1")])

                    for rnd in range(2):
                        tbank = ((4, 5) if tt % 2 == 0 else (6, 7))[rnd]
                        scr = SCR1 if rnd == 0 else SCR2
                        rscr = rs("SCR1") if rnd == 0 else rs("SCR2")

                        def ftr2(t, rnd=rnd, tbank=tbank, scr=scr):
                            last = None
                            for j in range(4):
                                last = t.transpose(
                                    PS[tbank][:, j * 128:(j + 1) * 128],
                                    scr[:, j * 128:(j + 1) * 128],
                                    IDF[:, :])
                            return last
                        pr.st("pe", ftr2, reads=[rscr, rs("IDF")],
                              writes=[rPS[tbank]])
                        pr.st("dve", lambda v, tt=tt, rnd=rnd, tbank=tbank:
                              v.tensor_copy(
                                  FA[:, 0:FC * W].rearrange(
                                      "p (c w) -> p c w", c=FC)[:, rnd * 4:rnd * 4 + 4,
                                                                tt * 128:(tt + 1) * 128],
                                  PS[tbank][:, 0:512].rearrange(
                                      "p (c k) -> p c k", c=4)),
                              reads=[rPS[tbank]], writes=[rs("FA")])
                pr.st("gp", lambda e, b=b, w=w: e.dma_start(
                    out=bass.AP(xn2_scr, (b * NW + w) * FC * 128 * W,
                                [[W, 128], [128 * W, FC], [1, W]]),
                    in_=FA[:, 0:FC * W].rearrange("p (c w) -> p c w", c=FC)),
                    reads=[rs("FA"), rs("EG00"), rs("EG10"), rs("EG01"),
                           rs("EG11")],
                    writes=[rs(f"xn2_scr_{b * NW + w}")], dma=True)

        pr.st("dve", lambda v: v.tensor_reduce(AMV[:, :], AM1[:, :], AX.X, ALU.max),
              reads=[rs("AM1")], writes=[rs("AMV")])
        pr.st("pe", lambda t: t.transpose(PS[0][0:1, 0:128], AMV[:, :], IDF[:, :]),
              reads=[rs("AMV"), rs("IDF")], writes=[rPS[0]])
        pr.st("dve", lambda v: v.tensor_reduce(
            CCA[0:1, 0:1], PS[0][0:1, 0:128], AX.X, ALU.max),
            reads=[rPS[0]], writes=[rs("CCA")])
        pr.st("dve", lambda v: v.tensor_scalar(
            CCA[0:1, 0:8], ZR8[:, :], CCA[0:1, 0:1], None, ALU.add),
            reads=[rs("ZR8"), rs("CCA")], writes=[rs("CCA")])
        pr.st("gp", lambda g: g.dma_start(out=cc1i[:, :], in_=CCA[:, :]),
              reads=[rs("CCA")], writes=[rs("cc1i")], dma=True)
        pr.st("gp", lambda g: g.collective_compute(
            "AllReduce", ALU.max, replica_groups=[list(range(NC))],
            ins=[cc1i[:, :]], outs=[cc1o[:, :]]),
            reads=[rs("cc1i")], writes=[rs("cc1o")])
        pr.st("gp", lambda g: g.dma_start(out=CCB[:, :], in_=cc1o[:, :]),
              reads=[rs("cc1o")], writes=[rs("CCB")], dma=True)
        pr.st("pe", lambda t: t.matmul(PS[0][0:128, 0:1], ONEF[:, :],
                                       CCB[0:1, 0:1], start=True, stop=True),
              reads=[rs("ONEF"), rs("CCB")], writes=[rPS[0]])

        def fscal1(v):
            ch("dve", v, v.tensor_copy(SCAL[:, 6:7], PS[0][0:128, 0:1]))
            ch("dve", v, v.tensor_single_scalar(SCAL[:, 0:1], SCAL[:, 6:7],
                                                1e-8, ALU.max))
            ch("dve", v, v.reciprocal(SCAL[:, 7:8], SCAL[:, 0:1]))
            ch("dve", v, v.tensor_scalar_mul(SCAL[:, 1:2], SCAL[:, 7:8], 32767.0))
            return v.tensor_scalar_mul(SCAL[:, 2:3], SCAL[:, 0:1], 1.0 / 32767.0)
        pr.st("dve", fscal1, reads=[rPS[0]], writes=[rs("SCAL")])

        CBK = DFF // 128
        for cb in range(CBK):
            p = cb % 2
            xat = rs(f"XAt{p}")
            xoff = p * D
            x1b = X1[p]
            rx1q = rs(f"X1Q{p}")
            bk0, bk1 = (4, 5) if p == 0 else (6, 7)
            pr.st("sp", lambda e, cb=cb, xoff=xoff: e.dma_start(
                out=XA[:, xoff:xoff + FC * 128].rearrange("p (c k) -> p c k", c=FC),
                in_=bass.AP(wi_t, cb * 128, [[DFF, 128], [128 * DFF, FC], [1, 128]])),
                writes=[xat], dma=True)

            def ftw(t, xoff=xoff, bk0=bk0, bk1=bk1):
                last = None
                for fc in range(FC):
                    bk = bk0 if fc < 4 else bk1
                    last = t.transpose(
                        PS[bk][:, (fc % 4) * 128:(fc % 4) * 128 + 128],
                        XA[:, xoff + fc * 128: xoff + (fc + 1) * 128], IDF[:, :])
                return last
            pr.st("pe", ftw, reads=[xat, rs("IDF")], writes=[rPS[bk0], rPS[bk1]])

            def fev(v, x1b=x1b, bk0=bk0, bk1=bk1):
                v.tensor_copy(x1b[:, 0:512], PS[bk0][:, 0:512])
                return v.tensor_copy(x1b[:, 512:1024], PS[bk1][:, 0:512])
            pr.st("dve", fev, reads=[rPS[bk0], rPS[bk1]], writes=[rx1q])

            def fqw(v, x1b=x1b):
                ch("dve", v, v.tensor_reduce(SQD[:, 0:1], x1b[:, 0:FC * 128],
                                             AX.X, ALU.max,
                                             apply_absolute_value=True))
                ch("dve", v, v.tensor_single_scalar(SQD[:, 1:2], SQD[:, 0:1],
                                                    1e-8, ALU.max))
                ch("dve", v, v.reciprocal(SQD[:, 2:3], SQD[:, 1:2]))
                ch("dve", v, v.tensor_scalar_mul(SQD[:, 3:4], SQD[:, 2:3], 127.0))
                ch("dve", v, v.tensor_scalar_mul(SQD[:, 4:5], SQD[:, 1:2],
                                                 1.0 / 127.0))
                last = None
                for i in range((FC * 128 + 511) // 512):
                    n = min(512, FC * 128 - i * 512)
                    sl = slice(i * 512, i * 512 + n)
                    rint_ops(v, SCR2[:, 0:n], x1b[:, sl], SQD[:, 3:4], n)
                    last = v.tensor_scalar(x1b[:, sl], SCR2[:, 0:n],
                                           SQD[:, 4:5], None, ALU.mult)
                return last
            pr.st("dve", fqw, reads=[rx1q, rs("DAB")],
                  writes=[rs("SQD"), rs("SCR1"), rs("SCR2"), rx1q])

            def ftb(t, x1b=x1b, bk0=bk0, bk1=bk1):
                last = None
                for fc in range(FC):
                    bk = bk0 if fc < 4 else bk1
                    last = t.transpose(
                        PS[bk][:, (fc % 4) * 128:(fc % 4) * 128 + 128],
                        x1b[:, fc * 128:(fc + 1) * 128], IDF[:, :])
                return last
            pr.st("pe", ftb, reads=[rx1q, rs("IDF")], writes=[rPS[bk0], rPS[bk1]])

            def fev2(v, cb=cb, bk0=bk0, bk1=bk1):
                last = None
                for fc in range(FC):
                    bk = bk0 if fc < 4 else bk1
                    last = v.tensor_copy(
                        AW[:, fc * DFF + cb * 128: fc * DFF + (cb + 1) * 128],
                        PS[bk][:, (fc % 4) * 128:(fc % 4) * 128 + 128])
                return last
            pr.st("dve", fev2, reads=[rPS[bk0], rPS[bk1]],
                  writes=[rs(f"AWF{cb}")] +
                  ([rs("AW"), rs("AWO"), rs("QW2"), rs("KW2"), rs("VW2")]
                   if cb == 0 else []))

        if debug_spill:
            pr.st("sp", lambda e: e.dma_start(
                out=bass.AP(aw_dbg, 0, [[FC * DFF, 128], [1, FC * DFF]]),
                in_=AW[:, 0:FC * DFF]), reads=[rs("AW")], dma=True)
            pr.st("sp", lambda e: e.dma_start(
                out=bass.AP(bin_dbg, 0, [[HC, 128], [1, HC]]),
                in_=BIN[:, :]), reads=[rs("BIN")], dma=True)
        for sl in range(NSLAB):
            pr.st("sp", lambda e, sl=sl: e.dma_start(
                out=FA[:, 0:FC * W].rearrange("p (c w) -> p c w", c=FC),
                in_=bass.AP(xn2_scr, sl * FC * 128 * W,
                            [[W, 128], [128 * W, FC], [1, W]])),
                reads=[rs(f"xn2_scr_{sl}")], writes=[rs("FA")], dma=True)

            def fq1(v):
                last = None
                for fc in range(FC):
                    last = rint_ops(v, TA[:, fc * W:(fc + 1) * W],
                                    FA[:, fc * W:(fc + 1) * W],
                                    SCAL[:, 1:2], W)
                return last
            pr.st("dve", fq1, reads=[rs("FA"), rs("SCAL"), rs("DAB")],
                  writes=[rs("SCR1"), rs("SCR2"), rs("TA")])
            for hc in range(HC):
                bank = hc % 4

                def ff1(t, hc=hc, bank=bank):
                    last = None
                    for fc in range(FC):
                        last = t.matmul(
                            PS[bank][0:128, 0:W],
                            AW[:, fc * DFF + hc * 128: fc * DFF + (hc + 1) * 128],
                            TA[:, fc * W:(fc + 1) * W],
                            start=(fc == 0), stop=(fc == FC - 1))
                    return last
                pr.st("pe", ff1, reads=[rs(f"AWF{hc}"), rs("TA")],
                      writes=[rPS[bank]])
                slot = hc % FC
                pr.st("act", lambda sc, hc=hc, bank=bank, slot=slot: sc.activation(
                    ET[:, slot * W:(slot + 1) * W], PS[bank][0:128, 0:W],
                    AF.Gelu, bias=BIN[:, hc:hc + 1], scale=SCAL[:, 2:3]),
                    reads=[rPS[bank], rs("BIN"), rs("SCAL")], writes=[rs("ET")])
                pr.st("dve", lambda v, sl=sl, hc=hc, slot=slot: v.tensor_reduce(
                    AM2[:, sl * HC + hc: sl * HC + hc + 1],
                    ET[:, slot * W:(slot + 1) * W],
                    AX.X, ALU.max, apply_absolute_value=True),
                    reads=[rs("ET")], writes=[rs("AM2")])
                if slot == FC - 1:
                    g0 = hc - FC + 1
                    pr.st("gp", lambda e, sl=sl, g0=g0: e.dma_start(
                        out=bass.AP(hid_scr, (sl * HC + g0) * 128 * W,
                                    [[W, 128], [128 * W, FC], [1, W]]),
                        in_=ET[:, 0:FC * W].rearrange("p (c w) -> p c w", c=FC)),
                        reads=[rs("ET")], writes=[rs(f"hid_scr_{sl}")], dma=True)

        pr.st("dve", lambda v: v.tensor_reduce(AMV[:, :], AM2[:, :], AX.X, ALU.max),
              reads=[rs("AM2")], writes=[rs("AMV")])
        pr.st("pe", lambda t: t.transpose(PS[0][0:1, 0:128], AMV[:, :], IDF[:, :]),
              reads=[rs("AMV"), rs("IDF")], writes=[rPS[0]])
        pr.st("dve", lambda v: v.tensor_reduce(
            CCA[0:1, 0:1], PS[0][0:1, 0:128], AX.X, ALU.max),
            reads=[rPS[0]], writes=[rs("CCA")])
        pr.st("dve", lambda v: v.tensor_scalar(
            CCA[0:1, 0:8], ZR8[:, :], CCA[0:1, 0:1], None, ALU.add),
            reads=[rs("ZR8"), rs("CCA")], writes=[rs("CCA")])
        pr.st("gp", lambda g: g.dma_start(out=cc2i[:, :], in_=CCA[:, :]),
              reads=[rs("CCA")], writes=[rs("cc2i")], dma=True)
        pr.st("gp", lambda g: g.collective_compute(
            "AllReduce", ALU.max, replica_groups=[list(range(NC))],
            ins=[cc2i[:, :]], outs=[cc2o[:, :]]),
            reads=[rs("cc2i")], writes=[rs("cc2o")])
        pr.st("gp", lambda g: g.dma_start(out=CCB[:, :], in_=cc2o[:, :]),
              reads=[rs("cc2o")], writes=[rs("CCB")], dma=True)
        pr.st("pe", lambda t: t.matmul(PS[0][0:128, 0:1], ONEF[:, :],
                                       CCB[0:1, 0:1], start=True, stop=True),
              reads=[rs("ONEF"), rs("CCB")], writes=[rPS[0]])

        def fscal2(v):
            ch("dve", v, v.tensor_copy(SCAL[:, 6:7], PS[0][0:128, 0:1]))
            ch("dve", v, v.tensor_single_scalar(SCAL[:, 3:4], SCAL[:, 6:7],
                                                1e-8, ALU.max))
            ch("dve", v, v.reciprocal(SCAL[:, 7:8], SCAL[:, 3:4]))
            ch("dve", v, v.tensor_scalar_mul(SCAL[:, 4:5], SCAL[:, 7:8], 32767.0))
            return v.tensor_scalar_mul(SCAL[:, 5:6], SCAL[:, 3:4], 1.0 / 32767.0)
        pr.st("dve", fscal2, reads=[rPS[0]], writes=[rs("SCAL")])

        for hc in range(HC):
            p = hc % 2
            xat = rs(f"XAt{p}")
            xoff = p * D
            pr.st("act", lambda e, hc=hc, xoff=xoff: e.dma_start(
                out=XA[:, xoff:xoff + D],
                in_=bass.AP(wo2_t, hc * 128 * D, [[D, 128], [1, D]])),
                writes=[xat], dma=True)

            def fqo(v, hc=hc, xoff=xoff):
                ch("dve", v, v.tensor_reduce(SQD[:, 0:1], XA[:, xoff:xoff + D],
                                             AX.X, ALU.max,
                                             apply_absolute_value=True))
                ch("dve", v, v.tensor_single_scalar(SQD[:, 1:2], SQD[:, 0:1],
                                                    1e-8, ALU.max))
                ch("dve", v, v.reciprocal(SQD[:, 2:3], SQD[:, 1:2]))
                ch("dve", v, v.tensor_scalar_mul(SQD[:, 3:4], SQD[:, 2:3], 127.0))
                ch("dve", v, v.tensor_scalar_mul(SQD[:, 4:5], SQD[:, 1:2],
                                                 1.0 / 127.0))
                last = None
                for i in range(DOC):
                    n = min(512, D - i * 512)
                    sl = slice(xoff + i * 512, xoff + i * 512 + n)
                    rint_ops(v, SCR2[:, 0:n], XA[:, sl], SQD[:, 3:4], n)
                    last = v.tensor_scalar(AW[:, hc * D + i * 512: hc * D + i * 512 + n],
                                           SCR2[:, 0:n], SQD[:, 4:5], None,
                                           ALU.mult)
                return last
            pr.st("dve", fqo, reads=[xat, rs("DAB")],
                  writes=[rs(f"AWF2_{hc}"), rs("SQD"), rs("SCR1"), rs("SCR2")] +
                  ([rs(f"AWF{c}") for c in range(CBK)] if hc == 0 else []))

        NPS = TTW * DOC
        assert NPS <= 8
        out_dmas = []
        for sl in range(NSLAB):
            b, w = sl // NW, sl % NW
            pr.st("sp", lambda e, b=b, w=w: e.dma_start(
                out=XA[:, :].rearrange("p (t d) -> p t d", t=TTW),
                in_=bass.AP(x1_scr, (b * S + w * W) * D,
                            [[D, 128], [128 * D, TTW], [1, D]])),
                reads=[rs(f"x1_scr_{sl}")],
                writes=[rs(f"XAt{tt}") for tt in range(TTW)], dma=True)
            for g8 in range((HC + FC - 1) // FC):
                buf, rbuf = (ET, rs("ET")) if g8 % 2 == 0 else (CB, rs("CB"))
                ng = min(FC, HC - g8 * FC)
                pr.st("sp", lambda e, sl=sl, g8=g8, ng=ng, buf=buf: e.dma_start(
                    out=buf[:, 0:ng * W].rearrange("p (c w) -> p c w", c=ng),
                    in_=bass.AP(hid_scr, (sl * HC + g8 * FC) * 128 * W,
                                [[W, 128], [128 * W, ng], [1, W]])),
                    reads=[rs(f"hid_scr_{sl}")], writes=[rbuf], dma=True)
                for j in range(ng):
                    hc = g8 * FC + j
                    r2b = QW[:, (hc % 2) * W:(hc % 2 + 1) * W]
                    rr2 = rs(f"R2{hc % 2}")

                    rint_split(r2b, buf[:, j * W:(j + 1) * W], rbuf,
                               SCAL[:, 4:5], rs("SCAL"), W, [rr2])

                    def ff2(t, hc=hc, r2b=r2b):
                        last = None
                        for tt in range(TTW):
                            for oc in range(DOC):
                                n = min(512, D - oc * 512)
                                last = t.matmul(
                                    PS[tt * DOC + oc][0:128, 0:n],
                                    r2b[0:128, tt * 128:(tt + 1) * 128],
                                    AW[:, hc * D + oc * 512: hc * D + oc * 512 + n],
                                    start=(hc == 0), stop=(hc == HC - 1))
                        return last
                    pr.st("pe", ff2, reads=[rr2, rs(f"AWF2_{hc}")],
                          writes=[rPS[i] for i in range(NPS)])
            for tt in range(TTW):
                for oc in range(DOC):
                    n = min(512, D - oc * 512)
                    ob = X1[(tt * DOC + oc) % 2]
                    rob = rs(f"X1{(tt * DOC + oc) % 2}")

                    def fep(v, tt=tt, oc=oc, n=n, ob=ob):
                        v.scalar_tensor_tensor(
                            SCR1[:, 0:n], PS[tt * DOC + oc][0:128, 0:n],
                            SCAL[:, 5:6], BB2[:, oc * 512: oc * 512 + n],
                            ALU.mult, ALU.add)
                        return v.tensor_add(
                            ob[0:128, 0:n], SCR1[:, 0:n],
                            XA[:, tt * D + oc * 512: tt * D + oc * 512 + n])
                    pr.st("dve", fep,
                          reads=[rPS[tt * DOC + oc], rs("SCAL"), rs("BB2"),
                                 rs(f"XAt{tt}")],
                          writes=[rs("SCR1"), rob])
                    stp = pr.st("gp", lambda e, b=b, w=w, tt=tt, oc=oc, n=n, ob=ob:
                                e.dma_start(
                                    out=bass.AP(out_t,
                                                (b * S + w * W + tt * 128) * D + oc * 512,
                                                [[D, 128], [1, n]]),
                                    in_=ob[0:128, 0:n]),
                                reads=[rob], dma=True)
                    out_dmas.append(stp)
        if debug_spill:
            pr.st("sp", lambda e: e.dma_start(
                out=bass.AP(sc_dbg, 0, [[8, 128], [1, 8]]), in_=SCAL[:, :]),
                reads=[rs("SCAL")], dma=True)
        pr.st("gp", lambda g: g.nop(), deps=out_dmas)

        block = es.enter_context(nc.Block())
        pr.finalize(block, esems, dsems)
    return nc


def kernel(L0_attn_x, L0_attn_W_qkv, L0_attn_W_gate, L0_attn_b_gate,
           L0_attn_W_out, L0_attn_b_out, L0_ffn_W_in, L0_ffn_b_in,
           L0_ffn_W_out, L0_ffn_b_out):
    P = REAL
    NC = P["NC"]
    x = np.ascontiguousarray(np.asarray(L0_attn_x, dtype=np.float32))
    Btot = x.shape[0]
    bper = Btot // NC
    common = {
        "w_qkv": np.ascontiguousarray(L0_attn_W_qkv, dtype=np.float32),
        "w_gate": np.ascontiguousarray(L0_attn_W_gate, dtype=np.float32),
        "b_gate": np.ascontiguousarray(L0_attn_b_gate, dtype=np.float32).reshape(1, 1),
        "w_out": np.ascontiguousarray(L0_attn_W_out, dtype=np.float32),
        "b_out": np.ascontiguousarray(L0_attn_b_out, dtype=np.float32).reshape(1, -1),
        "w_in": np.ascontiguousarray(L0_ffn_W_in, dtype=np.float32),
        "b_in": np.ascontiguousarray(L0_ffn_b_in, dtype=np.float32).reshape(1, -1),
        "w_out2": np.ascontiguousarray(L0_ffn_W_out, dtype=np.float32),
        "b_out2": np.ascontiguousarray(L0_ffn_b_out, dtype=np.float32).reshape(1, -1),
    }
    in_maps = []
    for c in range(NC):
        m = dict(common)
        m["x"] = np.ascontiguousarray(x[c * bper:(c + 1) * bper])
        in_maps.append(m)
    nc = build_program(P)
    res = run_bass_kernel_spmd(nc, in_maps, core_ids=list(range(NC)))
    out = np.concatenate([np.asarray(r["out"]) for r in res.results], axis=0)
    return out.astype(np.float32)


# version 29
# speedup vs baseline: 1.0539x; 1.0539x over previous
import numpy as np
from contextlib import ExitStack

import concourse.bass as bass
import concourse.mybir as mybir
from concourse.bass_utils import run_bass_kernel_spmd

F32 = mybir.dt.float32
BF16 = mybir.dt.bfloat16
AF = mybir.ActivationFunctionType
ALU = mybir.AluOpType
AX = mybir.AxisListType

REAL = dict(B=2, S=4096, D=1024, H=16, DH=64, W=512, ST=32, DFF=4096, NC=8)


class Res:
    __slots__ = ("writer", "readers", "name")

    def __init__(self, name=""):
        self.writer = None
        self.readers = []
        self.name = name


class Step:
    __slots__ = ("eng", "fn", "deps", "dma", "val")

    def __init__(self, eng, fn, deps, dma, val):
        self.eng, self.fn, self.deps, self.dma, self.val = eng, fn, deps, dma, val


class Prog:
    def __init__(self):
        self.steps = []
        self.dma_count = {"gp": 0, "sp": 0, "act": 0}
        self.eng_count = {e: 0 for e in ("pe", "act", "dve", "gp", "sp")}

    def st(self, eng, fn, reads=(), writes=(), deps=(), dma=False):
        d = set()
        for r in reads:
            if r.writer is not None:
                d.add(r.writer)
        for w in writes:
            if w.writer is not None:
                d.add(w.writer)
            d.update(w.readers)
        d.update(x for x in deps if x is not None)
        if dma:
            self.dma_count[eng] += 1
            val = 16 * self.dma_count[eng]
        else:
            self.eng_count[eng] += 1
            val = self.eng_count[eng]
        stp = Step(eng, fn, d, dma, val)
        for r in reads:
            r.readers.append(stp)
        for w in writes:
            w.writer = stp
            w.readers = []
        self.steps.append(stp)
        return stp

    def finalize(self, block, esems, dsems):
        order = {}
        for i, s in enumerate(self.steps):
            order[id(s)] = i

        def emit(name, eng):
            water = {}
            for s in self.steps:
                if s.eng != name:
                    continue
                waits = {}
                for dp in sorted(s.deps, key=lambda x: order[id(x)]):
                    if dp.eng == name and not dp.dma:
                        if s.val - dp.val > 4:
                            continue
                    sem = dsems[dp.eng] if dp.dma else esems[dp.eng]
                    key = id(sem)
                    if waits.get(key, (None, -1))[1] < dp.val:
                        waits[key] = (sem, dp.val)
                for sem, val in waits.values():
                    if water.get(id(sem), -1) >= val:
                        continue
                    eng.wait_ge(sem, val)
                    water[id(sem)] = val
                last = s.fn(eng)
                assert last is not None, f"step fn returned None on {name}"
                last.then_inc(dsems[name] if s.dma else esems[name],
                              16 if s.dma else 1)

        @block.tensor
        def _(t):
            emit("pe", t)

        @block.scalar
        def _(sc):
            emit("act", sc)

        @block.vector
        def _(v):
            emit("dve", v)

        @block.gpsimd
        def _(g):
            emit("gp", g)

        @block.sync
        def _(s):
            emit("sp", s)


def build_program(P, detect_races=False, debug_spill=False):
    B, S, D, H, DH, W, ST, DFF, NC = (P["B"], P["S"], P["D"], P["H"], P["DH"],
                                      P["W"], P["ST"], P["DFF"], P["NC"])
    FC = D // 128
    HDH = H * DH
    QKC = HDH // 128
    NW = S // W
    TTW = W // 128
    KC = 2 * W // 128
    SK = S // ST
    HC = DFF // 128
    NP = H // 2
    VB = 65
    VWID = H * VB
    VO = (HDH + 511) // 512
    DOC = (D + 511) // 512
    NST = (D + 511) // 512
    SCALE = 1.0 / float(np.sqrt(DH))
    C1 = 12582912.0
    C2 = 6755399441055744.0
    NSLAB = B * NW
    assert SK <= 128 and D % 128 == 0 and HDH % 128 == 0 and DFF % 128 == 0

    nc = bass.Bass("TRN2", target_bir_lowering=False, debug=False,
                   detect_race_conditions=detect_races)

    x_t = nc.dram_tensor("x", [B, S, D], F32, kind="ExternalInput")
    wqkv_t = nc.dram_tensor("w_qkv", [D, 3 * HDH], F32, kind="ExternalInput")
    wg_t = nc.dram_tensor("w_gate", [D, 1], F32, kind="ExternalInput")
    bg_t = nc.dram_tensor("b_gate", [1, 1], F32, kind="ExternalInput")
    wo_t = nc.dram_tensor("w_out", [HDH, D], F32, kind="ExternalInput")
    bo_t = nc.dram_tensor("b_out", [1, D], F32, kind="ExternalInput")
    wi_t = nc.dram_tensor("w_in", [D, DFF], F32, kind="ExternalInput")
    bi_t = nc.dram_tensor("b_in", [1, DFF], F32, kind="ExternalInput")
    wo2_t = nc.dram_tensor("w_out2", [DFF, D], F32, kind="ExternalInput")
    bo2_t = nc.dram_tensor("b_out2", [1, D], F32, kind="ExternalInput")
    out_t = nc.dram_tensor("out", [B, S, D], F32, kind="ExternalOutput")

    kscr = {"kind": "ExternalOutput"} if debug_spill else {}
    if debug_spill:
        xa_dbg = nc.dram_tensor("xa_dbg", [B, 128, (W // 128) * D], F32,
                                kind="ExternalOutput")
        ta_dbg = nc.dram_tensor("ta_dbg", [B, 128, (W // 128) * D], BF16,
                                kind="ExternalOutput")
        fa_dbg = nc.dram_tensor("fa_dbg", [B, 128, (D // 128) * W], BF16,
                                kind="ExternalOutput")
        aw_dbg = nc.dram_tensor("aw_dbg", [128, (D // 128) * DFF], BF16,
                                kind="ExternalOutput")
        bin_dbg = nc.dram_tensor("bin_dbg", [128, DFF // 128], F32,
                                 kind="ExternalOutput")
        sc_dbg = nc.dram_tensor("sc_dbg", [128, 8], F32, kind="ExternalOutput")
        wt_dbg = nc.dram_tensor("wt_dbg", [128, (D // 128) * 128], F32,
                                kind="ExternalOutput")
        wq_dbg = nc.dram_tensor("wq_dbg", [128, (D // 128) * 128], F32,
                                kind="ExternalOutput")
        sqd_dbg = nc.dram_tensor("sqd_dbg", [128, 6], F32, kind="ExternalOutput")
    q_scr = nc.dram_tensor("q_scr", [B, NW, QKC, 128, W], BF16, **kscr)
    k_scr = nc.dram_tensor("k_scr", [B, QKC, 128, S], BF16, **kscr)
    v_scr = nc.dram_tensor("v_scr", [B, S, VWID], BF16, **kscr)
    x1_scr = nc.dram_tensor("x1_scr", [B, S, D], F32, **kscr)
    xn2_scr = nc.dram_tensor("xn2_scr", [NSLAB, FC, 128, W], BF16, **kscr)
    hid_scr = nc.dram_tensor("hid_scr", [NSLAB, HC, 128, W], BF16, **kscr)
    cb_dbg = (nc.dram_tensor("cb_dbg", [B * NW, 128, (HDH // 128) * W], BF16)
              if debug_spill else None)
    cc1i = nc.dram_tensor("cc1i", [1, 8], F32)
    cc1o = nc.dram_tensor("cc1o", [1, 8], F32)
    cc2i = nc.dram_tensor("cc2i", [1, 8], F32)
    cc2o = nc.dram_tensor("cc2o", [1, 8], F32)

    off_wqk = 0
    off_wv = FC * 2 * HDH
    off_wg = off_wv + FC * HDH
    off_wo = off_wg + FC
    AW_COLS = max(off_wo + QKC * D, FC * DFF, HC * D)

    es = ExitStack()
    with es:
        sb = lambda n, s, d: es.enter_context(nc.sbuf_tensor(n, s, d))
        AW = sb("AW", [128, AW_COLS], BF16)
        XA = sb("XA", [128, TTW * D], F32)
        TA = sb("TA", [128, max(TTW * D, KC * W)], BF16)
        FA = sb("FA", [128, max(FC * W, 4 * W)], BF16)
        QW = sb("QW", [128, QKC * W], BF16)
        KW = sb("KW", [128, QKC * 2 * W], BF16)
        VW = sb("VW", [128, KC * VWID], BF16)
        X1 = [sb("X1A", [128, D], F32), sb("X1B", [128, D], F32)]
        ET = sb("ET", [128, KC * W], BF16)
        CB = sb("CB", [128, QKC * W], BF16)
        LU2 = [[sb("LUE0", [65, W], BF16), sb("LUE1", [65, W], BF16)],
               [sb("LUO0", [65, W], BF16), sb("LUO1", [65, W], BF16)]]
        GU2 = [[sb("GUE0", [65, W], BF16), sb("GUE1", [65, W], BF16)],
               [sb("GUO0", [65, W], BF16), sb("GUO1", [65, W], BF16)]]
        TLE = sb("TLE", [65, W], BF16)
        TLO = sb("TLO", [65, W], BF16)
        TT1 = sb("TT1", [65, W], BF16)
        TT2 = sb("TT2", [65, W], BF16)
        SCR1 = sb("SCR1", [128, 512], F32)
        SCR2 = sb("SCR2", [128, 512], F32)
        G1 = sb("G1", [65, S], BF16)
        GR1 = None

        AB4 = sb("AB4", [4, W], BF16)
        GMR = sb("GMR", [1, W], BF16)
        GRW = sb("GRW", [1, W], BF16)
        E4 = sb("E4", [65, 16], BF16)
        S4 = sb("S4", [4, 4 * 65], BF16)
        R1S = sb("R1S", [1, 16], BF16)
        GS2 = sb("GS2", [65, 8], BF16)
        KSTR = sb("KSTR", [128, B * QKC * SK], BF16)
        VSTR = sb("VSTR", [SK, B * VWID], BF16)
        BIN = sb("BIN", [128, HC], F32)
        BB2 = sb("BB2", [128, D], F32)
        BOR = sb("BOR", [1, D], BF16)
        IDB = sb("IDB", [128, 128], BF16)
        IDF = sb("IDF", [128, 128], F32)
        SHE = sb("SHE", [65, 128], BF16)
        SHO = sb("SHO", [65, 128], BF16)
        ONEB = sb("ONEB", [1, 128], BF16)
        ONEF = sb("ONEF", [1, 128], F32)
        EPS = sb("EPS", [128, 1], F32)
        ZR8 = sb("ZR8", [1, 8], F32)
        NBG = sb("NBG", [1, 1], F32)
        BGT = sb("BGT", [1, 1], F32)
        MV6 = [sb("MV6A", [128, 6 * NST], F32), sb("MV6B", [128, 6 * NST], F32)]
        MV2 = [sb("MV2A", [128, 2], F32), sb("MV2B", [128, 2], F32)]
        LNV = [sb("LNVA", [128, 1], F32), sb("LNVB", [128, 1], F32)]
        RST = [sb("RSTA", [128, 1], F32), sb("RSTB", [128, 1], F32)]
        AM1 = sb("AM1", [128, NSLAB * TTW * NST], BF16)
        AM2 = sb("AM2", [128, NSLAB * HC], BF16)
        AMV = sb("AMV", [128, 1], F32)
        SCAL = sb("SCAL", [128, 8], F32)
        CCA = sb("CCA", [1, 8], F32)
        CCB = sb("CCB", [1, 8], F32)
        SQD = sb("SQD", [128, 6], F32)
        DAB = sb("DAB", [128, 4], F32)


        PA = [es.enter_context(nc.psum_tensor(f"PA{i}", [128, 1024], F32))
              for i in range(4)]
        PS = [PA[i // 2][:, (i % 2) * 512:(i % 2) * 512 + 512] for i in range(8)]

        esems = {e: es.enter_context(nc.semaphore(f"sem_{e}"))
                 for e in ("pe", "act", "dve", "gp", "sp")}
        chain_sem = {e: es.enter_context(nc.semaphore(f"chain_{e}"))
                     for e in ("dve", "act")}
        chain_cnt = {"dve": 0, "act": 0}

        def ch(engname, eng, inst):
            chain_cnt[engname] += 1
            inst.then_inc(chain_sem[engname], 1)
            eng.wait_ge(chain_sem[engname], chain_cnt[engname])
            return inst
        dsems = {"gp": es.enter_context(nc.semaphore("dma_gp")),
                 "sp": es.enter_context(nc.semaphore("dma_sp")),
                 "act": es.enter_context(nc.semaphore("dma_act"))}

        pr = Prog()
        R = {}

        def rs(name):
            if name not in R:
                R[name] = Res(name)
            return R[name]

        rPS = [rs(f"PS{i}") for i in range(8)]

        def fsetup_gp(g):
            g.iota(SCR1[0:128, 0:128][:, :], [[1, 128]], channel_multiplier=0,
                   allow_small_or_imprecise_dtypes=True)
            g.iota(SCR2[0:128, 0:128][:, :], [[0, 128]], channel_multiplier=1,
                   allow_small_or_imprecise_dtypes=True)
            g.iota(XA[0:65, 0:128], [[0, 128]], base=-1, channel_multiplier=1,
                   allow_small_or_imprecise_dtypes=True)
            g.iota(XA[0:65, 128:256], [[0, 128]], base=63, channel_multiplier=1,
                   allow_small_or_imprecise_dtypes=True)
            g.memset(ONEB[:, :], 1.0)
            g.memset(E4[:, :], 0.0)
            g.memset(R1S[:, :], 0.0)
            g.memset(GS2[:, :], 0.0)
            for i in range(4):
                g.memset(E4[0:1, i * 4 + i: i * 4 + i + 1], 1.0)
                g.memset(R1S[0:1, i * 4 + i: i * 4 + i + 1], 1.0)
            g.memset(GS2[0:1, 0:2], 1.0)
            g.memset(GS2[0:1, 6:8], 1.0)
            g.memset(GS2[64:65, 0:2], 1.0)
            g.memset(GS2[64:65, 6:8], 1.0)
            g.memset(ONEF[:, :], 1.0)
            g.memset(DAB[:, 3:4], 1.0)
            g.memset(EPS[:, :], 1e-5)
            g.memset(ZR8[:, :], 0.0)
            return g.memset(VW[:, 0:KC * VWID:VB], 1.0)

        pr.st("gp", fsetup_gp,
              writes=[rs("ONEB"), rs("ONEF"), rs("EPS"), rs("ZR8"), rs("VW"),
                      rs("SCR1"), rs("SCR2"), rs("XAt0"), rs("E4"), rs("R1S"),
                      rs("GS2"), rs("DAB")])

        def fsetup_ve(v):
            v.tensor_tensor(IDB[:, :], SCR1[0:128, 0:128][:, :],
                            SCR2[0:128, 0:128][:, :], ALU.is_equal)
            v.tensor_tensor(IDF[:, :], SCR1[0:128, 0:128][:, :],
                            SCR2[0:128, 0:128][:, :], ALU.is_equal)
            v.tensor_tensor(SHE[:, :], SCR1[0:128, 0:128][0:65, :],
                            XA[0:65, 0:128], ALU.is_equal)
            v.tensor_tensor(SHO[:, :], SCR1[0:128, 0:128][0:65, :],
                            XA[0:65, 128:256], ALU.is_equal)
            return v.memset(SHO[0:1, :], 0.0)

        pr.st("dve", fsetup_ve,
              reads=[rs("SCR1"), rs("SCR2"), rs("XAt0")],
              writes=[rs("IDB"), rs("IDF"), rs("SHE"), rs("SHO")])

        def fdab(v):
            ch("dve", v, v.tensor_scalar(DAB[:, 0:1], DAB[:, 3:4], C2, C2,
                                         ALU.add, ALU.subtract))
            ch("dve", v, v.tensor_scalar(DAB[:, 1:2], DAB[:, 0:1], -1.0, 1.0,
                                         ALU.mult, ALU.add))
            ch("dve", v, v.tensor_scalar(DAB[:, 2:3], DAB[:, 0:1], C2, None,
                                         ALU.mult))
            ch("dve", v, v.tensor_scalar(DAB[:, 3:4], DAB[:, 1:2], C1, None,
                                         ALU.mult))
            ch("dve", v, v.tensor_add(DAB[:, 2:3], DAB[:, 2:3], DAB[:, 3:4]))
            return v.tensor_scalar_mul(DAB[:, 2:3], DAB[:, 2:3], -1.0)
        pr.st("dve", fdab, writes=[rs("DAB")])

        def rint_ops(v, dst, src, scal, n):
            v.tensor_scalar(SCR1[:, 0:n], src, scal, C1, ALU.mult, ALU.add)
            v.tensor_scalar(SCR2[:, 0:n], src, scal, C2, ALU.mult, ALU.add)
            v.tensor_scalar(SCR1[:, 0:n], SCR1[:, 0:n], DAB[:, 1:2],
                            DAB[:, 2:3], ALU.mult, ALU.add)
            return v.scalar_tensor_tensor(dst, SCR2[:, 0:n], DAB[:, 0:1],
                                          SCR1[:, 0:n], ALU.mult, ALU.add)

        for fc in range(FC):
            pr.st("gp", lambda g, fc=fc: g.dma_start(
                out=AW[:, off_wqk + fc * 2 * HDH: off_wqk + (fc + 1) * 2 * HDH],
                in_=bass.AP(wqkv_t, fc * 128 * 3 * HDH, [[3 * HDH, 128], [1, 2 * HDH]])),
                writes=[rs("AW")], dma=True)
            pr.st("gp", lambda g, fc=fc: g.dma_start(
                out=AW[:, off_wv + fc * HDH: off_wv + (fc + 1) * HDH],
                in_=bass.AP(wqkv_t, fc * 128 * 3 * HDH + 2 * HDH,
                            [[3 * HDH, 128], [1, HDH]])),
                writes=[rs("AW")], dma=True)
        pr.st("gp", lambda g: g.dma_start(
            out=AW[:, off_wg: off_wg + FC],
            in_=bass.AP(wg_t, 0, [[1, 128], [128, FC]]),
            allow_slow_non_contiguous=True),
            writes=[rs("AW")], dma=True)
        for c in range(QKC):
            pr.st("gp", lambda g, c=c: g.dma_start(
                out=AW[:, off_wo + c * D: off_wo + (c + 1) * D],
                in_=bass.AP(wo_t, c * 128 * D, [[D, 128], [1, D]])),
                writes=[rs("AW")], dma=True)
        pr.st("gp", lambda g: g.dma_start(out=BOR[:, :], in_=bo_t[:, :]),
              writes=[rs("BOR")], dma=True)
        pr.st("gp", lambda g: g.dma_start(out=BGT[:, :], in_=bg_t[:, :]),
              writes=[rs("BGT")], dma=True)
        pr.st("sp", lambda s: s.dma_start(
            out=BIN[:, :], in_=bass.AP(bi_t, 0, [[1, 128], [128, HC]]),
            allow_slow_non_contiguous=True),
            writes=[rs("BIN")], dma=True)
        pr.st("sp", lambda s: s.dma_start(out=XA[0:1, 0:D], in_=bo2_t[:, :]),
              writes=[rs("XAt0")], dma=True)
        pr.st("dve", lambda v: v.tensor_scalar_mul(NBG[:, :], BGT[:, :], -1.0),
              reads=[rs("BGT")], writes=[rs("NBG")])
        for oc in range(DOC):
            n = min(512, D - oc * 512)
            pr.st("pe", lambda t, oc=oc, n=n: t.matmul(
                PS[0][0:128, 0:n], ONEF[:, :], XA[0:1, oc * 512: oc * 512 + n],
                start=True, stop=True),
                reads=[rs("ONEF"), rs("XAt0")], writes=[rPS[0]])
            pr.st("dve", lambda v, oc=oc, n=n: v.tensor_copy(
                BB2[:, oc * 512: oc * 512 + n], PS[0][0:128, 0:n]),
                reads=[rPS[0]], writes=[rs("BB2")])

        def fs4(t):
            last = None
            for i in range(4):
                last = t.matmul(PS[2][0:4, i * 65:(i + 1) * 65],
                                R1S[0:1, i * 4:(i + 1) * 4], ONEB[0:1, 0:65],
                                start=True, stop=True)
            return last
        pr.st("pe", fs4, reads=[rs("R1S"), rs("ONEB")], writes=[rPS[2]])
        pr.st("dve", lambda v: v.tensor_copy(S4[:, :], PS[2][0:4, 0:4 * 65]),
              reads=[rPS[2]], writes=[rs("S4")])

        def ln_stats(src_res, src_ap_fn, par=0):
            mv6, mv2, lnv, rst = MV6[par], MV2[par], LNV[par], RST[par]

            def f(v, mv6=mv6, mv2=mv2):
                for i in range(NST):
                    n = min(512, D - i * 512)
                    ch("dve", v, v.bn_stats(mv6[:, 6 * i: 6 * i + 6],
                                            src_ap_fn(i, n)))
                return v.bn_aggr(mv2[:, :], mv6[:, 0:6 * NST])
            pr.st("dve", f, reads=[src_res],
                  writes=[rs(f"MV6{par}"), rs(f"MV2{par}")])

            def f2(sc, mv2=mv2, lnv=lnv, rst=rst):
                ch("act", sc, sc.activation(lnv[:, :], mv2[:, 1:2], AF.Ln,
                                            bias=EPS[:, :]))
                return sc.activation(rst[:, :], lnv[:, :], AF.Exp, scale=-0.5)
            pr.st("act", f2, reads=[rs(f"MV2{par}"), rs("EPS")],
                  writes=[rs(f"LNV{par}"), rs(f"RST{par}")])

        QW2 = AW[:, 0:QKC * W]
        KW2 = AW[:, QKC * W: QKC * W + QKC * 2 * W]
        VW2 = AW[:, QKC * 3 * W: QKC * 3 * W + KC * VWID]
        for b in range(B):
            for s in range(NW):
                FAb = FA if s % 2 == 0 else CB
                rFA = rs("FA") if s % 2 == 0 else rs("CB")
                pr.st("sp", lambda e, b=b, s=s: e.dma_start(
                    out=XA[:, :].rearrange("p (t d) -> p t d", t=TTW),
                    in_=bass.AP(x_t, (b * S + s * W) * D,
                                [[D, 128], [128 * D, TTW], [1, D]])),
                    writes=[rs(f"XAt{tt}") for tt in range(TTW)], dma=True)
                for tt in range(TTW):
                    par = tt % 2
                    x1b = X1[par]
                    rx1 = rs(f"X1{par}")
                    ln_stats(rs(f"XAt{tt}"),
                             lambda i, n, tt=tt: XA[:, tt * D + i * 512: tt * D + i * 512 + n],
                             par)
                    pr.st("dve", lambda v, tt=tt, par=par, x1b=x1b: v.tensor_scalar(
                        x1b[:, :], XA[:, tt * D: (tt + 1) * D],
                        MV2[par][:, 0:1], RST[par][:, :], ALU.subtract, ALU.mult),
                        reads=[rs(f"XAt{tt}"), rs(f"MV2{par}"), rs(f"RST{par}")],
                        writes=[rx1])
                    for rnd in range(2):
                        tbank = ((7, 4) if tt % 2 == 0 else (5, 6))[rnd]

                        def ftr(t, tt=tt, rnd=rnd, tbank=tbank, x1b=x1b):
                            last = None
                            for j in range(4):
                                fc = rnd * 4 + j
                                last = t.transpose(
                                    PS[tbank][:, j * 128:(j + 1) * 128],
                                    x1b[:, fc * 128:(fc + 1) * 128],
                                    IDF[:, :])
                            return last
                        pr.st("pe", ftr, reads=[rx1, rs("IDF")],
                              writes=[rPS[tbank]])
                        pr.st("dve", lambda v, tt=tt, rnd=rnd, tbank=tbank, FAb=FAb:
                              v.tensor_copy(
                                  FAb[:, 0:FC * W].rearrange(
                                      "p (c w) -> p c w", c=FC)[:, rnd * 4:rnd * 4 + 4,
                                                                tt * 128:(tt + 1) * 128],
                                  PS[tbank][:, 0:512].rearrange(
                                      "p (c k) -> p c k", c=4)),
                              reads=[rPS[tbank]], writes=[rFA])
                if debug_spill == "spill" and s == 0:
                    pr.st("sp", lambda e, b=b: e.dma_start(
                        out=bass.AP(xa_dbg, b * 128 * TTW * D, [[TTW * D, 128], [1, TTW * D]]),
                        in_=XA[:, :]), reads=[rs("XA")], dma=True)
                    pr.st("sp", lambda e, b=b: e.dma_start(
                        out=bass.AP(ta_dbg, b * 128 * TTW * D, [[TTW * D, 128], [1, TTW * D]]),
                        in_=TA[:, :]), reads=[rs("TA")], dma=True)
                    pr.st("sp", lambda e, b=b: e.dma_start(
                        out=bass.AP(fa_dbg, b * 128 * FC * W, [[FC * W, 128], [1, FC * W]]),
                        in_=FA[:, :]), reads=[rs("FA")], dma=True)
                for oc in range(2 * QKC):
                    bank = oc % 2
                    def fqk(t, oc=oc, bank=bank, FAb=FAb):
                        last = None
                        for fc in range(FC):
                            last = t.matmul(
                                PS[bank][0:128, 0:W],
                                AW[:, off_wqk + fc * 2 * HDH + oc * 128:
                                   off_wqk + fc * 2 * HDH + (oc + 1) * 128],
                                FAb[:, fc * W:(fc + 1) * W],
                                start=(fc == 0), stop=(fc == FC - 1))
                        return last
                    pr.st("pe", fqk, reads=[rs("AW"), rFA], writes=[rPS[bank]])
                    if oc < QKC:
                        pr.st("act", lambda sc, oc=oc, bank=bank: sc.copy(
                            QW[:, oc * W:(oc + 1) * W], PS[bank][0:128, 0:W]),
                            reads=[rPS[bank]], writes=[rs("QW")])
                    else:
                        c = oc - QKC
                        pr.st("act", lambda sc, c=c, bank=bank: sc.copy(
                            KW[:, c * W:(c + 1) * W], PS[bank][0:128, 0:W]),
                            reads=[rPS[bank]], writes=[rs("KW")])
                        pr.st("dve", lambda v, c=c, s=s, b=b: v.tensor_copy(
                            KSTR[:, b * QKC * SK + c * SK + s * (W // ST):
                                 b * QKC * SK + c * SK + (s + 1) * (W // ST)],
                            KW[:, c * W: (c + 1) * W: ST]),
                            reads=[rs("KW")], writes=[rs("KSTR")])
                pr.st("gp", lambda e, b=b, s=s: e.dma_start(
                    out=bass.AP(q_scr, ((b * NW + s) * QKC) * 128 * W,
                                [[W, 128], [128 * W, QKC], [1, W]]),
                    in_=QW[:, :].rearrange("p (c w) -> p c w", c=QKC)),
                    reads=[rs("QW")], writes=[rs(f"q_scr_{b}_{s}")], dma=True)
                for c in range(QKC):
                    pr.st("gp", lambda e, b=b, s=s, c=c: e.dma_start(
                        out=bass.AP(k_scr, (b * QKC + c) * 128 * S + s * W,
                                    [[S, 128], [1, W]]),
                        in_=KW[:, c * W:(c + 1) * W]),
                        reads=[rs("KW")], writes=[rs(f"k_scr_{b}_{s}")], dma=True)
                for tt in range(TTW):
                    for oc in range(VO):
                        n = min(512, HDH - oc * 512)
                        bank = 2 + (tt * VO + oc) % 2
                        def fv(t, tt=tt, oc=oc, n=n, bank=bank, FAb=FAb):
                            last = None
                            for fc in range(FC):
                                last = t.matmul(
                                    PS[bank][0:128, 0:n],
                                    FAb[:, fc * W + tt * 128: fc * W + (tt + 1) * 128],
                                    AW[:, off_wv + fc * HDH + oc * 512:
                                       off_wv + fc * HDH + oc * 512 + n],
                                    start=(fc == 0), stop=(fc == FC - 1))
                            return last
                        pr.st("pe", fv, reads=[rFA, rs("AW")], writes=[rPS[bank]])
                        nh = n // DH
                        pr.st("dve", lambda v, tt=tt, oc=oc, n=n, nh=nh, bank=bank:
                              v.tensor_copy(
                                  VW[:, tt * VWID + oc * 8 * VB:
                                     tt * VWID + oc * 8 * VB + nh * VB].rearrange(
                                         "p (h c) -> p h c", c=VB)[:, :, 1:VB],
                                  PS[bank][0:128, 0:n].rearrange(
                                      "p (h c) -> p h c", c=DH)),
                              reads=[rPS[bank]], writes=[rs("VW")])
                def fg(t, FAb=FAb):
                    last = None
                    for fc in range(FC):
                        last = t.matmul(PS[6][0:1, 0:W],
                                        AW[:, off_wg + fc: off_wg + fc + 1],
                                        FAb[:, fc * W:(fc + 1) * W],
                                        start=(fc == 0), stop=(fc == FC - 1))
                    return last
                pr.st("pe", fg, reads=[rs("AW"), rFA], writes=[rPS[6]])
                pr.st("act", lambda sc: sc.activation(
                    SCR1[0:1, 0:W], PS[6][0:1, 0:W], AF.Exp, scale=-1.0,
                    bias=NBG[:, :]),
                    reads=[rPS[6], rs("NBG")], writes=[rs("SCR1")])

                def fsig(v, b=b, s=s):
                    ch("dve", v, v.tensor_scalar_add(SCR2[0:1, 0:W],
                                                     SCR1[0:1, 0:W], 1.0))
                    ch("dve", v, v.reciprocal(SCR1[0:1, 0:W], SCR2[0:1, 0:W]))
                    return v.tensor_copy(G1[64 * b:64 * b + 1, s * W:(s + 1) * W],
                                         SCR1[0:1, 0:W])
                pr.st("dve", fsig, reads=[rs("SCR1")],
                      writes=[rs("SCR1"), rs("SCR2"), rs("G1")])
                pr.st("gp", lambda e, b=b, s=s: e.dma_start(
                    out=bass.AP(v_scr, (b * S + s * W) * VWID,
                                [[VWID, 128], [128 * VWID, TTW], [1, VWID]]),
                    in_=VW[:, 0:TTW * VWID].rearrange("p (t c) -> p t c", t=TTW)),
                    reads=[rs("VW")], writes=[rs(f"v_scr_{b}_{s}")], dma=True)

            pr.st("sp", lambda e, b=b: e.dma_start(
                out=VSTR[:, b * VWID:(b + 1) * VWID],
                in_=bass.AP(v_scr, b * S * VWID, [[VWID * ST, SK], [1, VWID]])),
                reads=[rs(f"v_scr_{b}_{s}") for s in range(NW)],
                writes=[rs("VSTR")], dma=True)

        def a2_buffers(gw):
            p = gw % 2
            return ((QW, rs("QW")) if p == 0 else (QW2, rs("QW2")),
                    (KW, rs("KW")) if p == 0 else (KW2, rs("KW2")),
                    (VW, rs("VW")) if p == 0 else (VW2, rs("VW2")))

        def a2_kcs(w):
            kbase = w * W - W // 2
            kcs = [kc for kc in range(KC)
                   if kbase + kc * 128 >= 0 and kbase + (kc + 1) * 128 <= S]
            return kbase, kcs

        def emit_a2_qkv_loads(gw):
            b, w = divmod(gw, NW)
            (QWb, rQW), (KWb, rKW), (VWb, rVW) = a2_buffers(gw)
            kbase, kcs = a2_kcs(w)
            j0, j1 = kcs[0], kcs[-1]
            qw_w = ([rs("AW")] if gw == 1 else [])
            pr.st("sp", lambda e, b=b, w=w, QWb=QWb: e.dma_start(
                out=QWb[:, :].rearrange("p (c w) -> p c w", c=QKC),
                in_=bass.AP(q_scr, (b * NW + w) * QKC * 128 * W,
                            [[W, 128], [128 * W, QKC], [1, W]])),
                reads=[rs(f"q_scr_{b}_{w}")], writes=[rQW] + qw_w, dma=True)
            for c in range(QKC):
                lo = kbase + j0 * 128
                nk = (j1 - j0 + 1) * 128
                kres = [rs(f"k_scr_{b}_{sx}")
                        for sx in range(max(0, w - 1), min(NW, w + 2))]
                pr.st("sp", lambda e, b=b, c=c, lo=lo, nk=nk, j0=j0, KWb=KWb:
                      e.dma_start(
                          out=KWb[:, c * 2 * W + j0 * 128: c * 2 * W + j0 * 128 + nk],
                          in_=bass.AP(k_scr, (b * QKC + c) * 128 * S + lo,
                                      [[S, 128], [1, nk]])),
                      reads=kres, writes=[rKW] + qw_w, dma=True)
            lo = kbase + j0 * 128
            nj = j1 - j0 + 1
            vres = [rs(f"v_scr_{b}_{sx}")
                    for sx in range(max(0, w - 1), min(NW, w + 2))]
            pr.st("sp", lambda e, b=b, lo=lo, nj=nj, j0=j0, VWb=VWb: e.dma_start(
                out=VWb[:, j0 * VWID:(j0 + nj) * VWID].rearrange(
                    "p (t c) -> p t c", t=nj),
                in_=bass.AP(v_scr, (b * S + lo) * VWID,
                            [[VWID, 128], [128 * VWID, nj], [1, VWID]])),
                reads=vres, writes=[rVW] + qw_w, dma=True)

        emit_a2_qkv_loads(0)
        for gw in range(B * NW):
            b, w = divmod(gw, NW)
            (QWb, rQW), (KWb, rKW), (VWb, rVW) = a2_buffers(gw)
            kbase, kcs = a2_kcs(w)
            j0, j1 = kcs[0], kcs[-1]
            if True:
                for tt in range(TTW):
                    pr.st("sp", lambda e, b=b, w=w, tt=tt: e.dma_start(
                        out=XA[:, tt * D:(tt + 1) * D],
                        in_=bass.AP(x_t, (b * S + w * W + tt * 128) * D,
                                    [[D, 128], [1, D]])),
                        writes=[rs(f"XAt{tt}")], dma=True)
                if gw + 1 < B * NW:
                    emit_a2_qkv_loads(gw + 1)
                def fgst(v, b=b, w=w):
                    ch("dve", v, v.tensor_copy(
                        GRW[:, :], G1[64 * b:64 * b + 1, w * W:(w + 1) * W]))
                    return v.tensor_scalar(
                        GMR[:, :], GRW[:, :], -1.0, 1.0, ALU.mult, ALU.add)
                pr.st("dve", fgst, reads=[rs("G1")],
                      writes=[rs("GRW"), rs("GMR")])

                def fg4(t):
                    t.matmul(PS[1][0:4, 0:W], GS2[0:1, 0:4], GRW[:, :],
                             start=True, stop=False,
                             skip_group_check=True)
                    return t.matmul(PS[1][0:4, 0:W], GS2[0:1, 4:8], GMR[:, :],
                                    start=False, stop=True, skip_group_check=True)
                pr.st("pe", fg4, reads=[rs("GS2"), rs("GRW"), rs("GMR")],
                      writes=[rPS[1]])
                pr.st("dve", lambda v: v.tensor_copy(SCR1[0:4, 0:W],
                                                     PS[1][0:4, 0:W]),
                      reads=[rPS[1]], writes=[rs("SCR1")])

                def emit_front(pair, kcs=tuple(kcs), w=w):
                    pp = pair % 2
                    qq = (pair + 1) % 2
                    blist = (0, 1, 2 + 2 * qq, 3 + 2 * qq)
                    for sub in range(2):
                        h = pair * 2 + sub
                        hb = 64 * (h % 2)
                        hc2 = h // 2
                        avbank = 2 + sub + 2 * pp
                        ETb = ET if sub == 0 else TA
                        rET = rs("ET") if sub == 0 else rs("TA")
                        for i, kc in enumerate(kcs):
                            bank = blist[(sub * len(kcs) + i) % 4]
                            pr.st("pe", lambda t, hb=hb, hc2=hc2, kc=kc,
                                  bank=bank, KWb=KWb, QWb=QWb:
                                  t.matmul(PS[bank][0:128, 0:W],
                                           KWb[hb:hb + 64,
                                               hc2 * 2 * W + kc * 128: hc2 * 2 * W + (kc + 1) * 128],
                                           QWb[hb:hb + 64, hc2 * W:(hc2 + 1) * W],
                                           start=True, stop=True),
                                  reads=[rKW, rQW], writes=[rPS[bank]])
                            if i % 2 == 1:
                                b0 = blist[(sub * len(kcs) + i - 1) % 4]
                                kc0 = kcs[i - 1]
                                pr.st("act", lambda sc, kc0=kc0, b0=b0, ETb=ETb:
                                      sc.activation(
                                          ETb[:, kc0 * W:(kc0 + 2) * W],
                                          PA[b0 // 2][0:128, 0:2 * W],
                                          AF.Exp, scale=SCALE),
                                      reads=[rPS[b0], rPS[b0 + 1]], writes=[rET])

                        def fav(t, h=h, kcs=kcs, avbank=avbank, ETb=ETb, VWb=VWb):
                            last = None
                            for i, kc in enumerate(kcs):
                                last = t.matmul(
                                    PS[avbank][0:65, 0:W],
                                    VWb[:, kc * VWID + h * VB: kc * VWID + (h + 1) * VB],
                                    ETb[:, kc * W:(kc + 1) * W],
                                    start=(i == 0), stop=(i == len(kcs) - 1))
                            return last
                        pr.st("pe", fav, reads=[rVW, rET],
                              writes=[rPS[avbank]])
                        pr.st("act", lambda sc, avbank=avbank,
                              LU=LU2[sub][pp]: sc.copy(
                                  LU[:, :], PS[avbank][0:65, 0:W]),
                              reads=[rPS[avbank]],
                              writes=[rs(f"LU{sub}{pp}")])

                def emit_back(pair, w=w):
                    pp = pair % 2
                    gb = 6 + pp
                    for sub in range(2):
                        h = pair * 2 + sub
                        hb = 64 * (h % 2)
                        hc2 = h // 2
                        sbank = sub
                        pr.st("pe", lambda t, hb=hb, hc2=hc2, sbank=sbank,
                            b=b, QWb=QWb: t.matmul(
                            PS[sbank][0:SK, 0:W],
                            KSTR[hb:hb + 64, b * QKC * SK + hc2 * SK:
                                 b * QKC * SK + (hc2 + 1) * SK],
                            QWb[hb:hb + 64, hc2 * W:(hc2 + 1) * W],
                            start=True, stop=True),
                            reads=[rs("KSTR"), rQW], writes=[rPS[sbank]])
                    pr.st("act", lambda sc, pp=pp: sc.activation(
                        FA[0:SK, 2 * pp * W:2 * pp * W + 2 * W],
                        PA[0][0:SK, 0:2 * W], AF.Exp, scale=SCALE),
                        reads=[rPS[0], rPS[1]],
                        writes=[rs(f"EG0{pp}"), rs(f"EG1{pp}")])
                    for sub in range(2):
                        h = pair * 2 + sub
                        ego = (2 * pp + sub) * W
                        pr.st("pe", lambda t, h=h, gb=gb, ego=ego, b=b: t.matmul(
                            PS[gb][0:65, 0:W],
                            VSTR[0:SK, b * VWID + h * VB: b * VWID + (h + 1) * VB],
                            FA[0:SK, ego:ego + W], start=True, stop=True),
                            reads=[rs("VSTR"), rs(f"EG{sub}{pp}")], writes=[rPS[gb]])
                        pr.st("act", lambda sc, gb=gb, GU=GU2[sub][pp]: sc.copy(
                            GU[:, :], PS[gb][0:65, 0:W]),
                            reads=[rPS[gb]], writes=[rs(f"GU{sub}{pp}")])

                    def fgat(t, gb=gb, pp=pp):
                        last = None
                        for i, srcb in enumerate((LU2[0][pp], LU2[1][pp],
                                                  GU2[0][pp], GU2[1][pp])):
                            last = t.matmul(PS[gb][0:4, 0:W],
                                            E4[:, i * 4:(i + 1) * 4], srcb[:, :],
                                            start=(i == 0), stop=(i == 3))
                        return last
                    pr.st("pe", fgat,
                          reads=[rs("E4"), rs(f"LU0{pp}"), rs(f"LU1{pp}"),
                                 rs(f"GU0{pp}"), rs(f"GU1{pp}")],
                          writes=[rPS[gb]])

                    def frcp(v, gb=gb):
                        ch("dve", v, v.reciprocal(SCR2[0:4, 0:W],
                                                  PS[gb][0:4, 0:W]))
                        return v.tensor_mul(AB4[:, :], SCR2[0:4, 0:W],
                                            SCR1[0:4, 0:W])
                    pr.st("dve", frcp, reads=[rPS[gb], rs("SCR1")],
                          writes=[rs("SCR2"), rs("AB4")])

                    for sub in range(2):
                        LU = LU2[sub][pp]
                        GU = GU2[sub][pp]
                        rLU = rs(f"LU{sub}{pp}")
                        rGU = rs(f"GU{sub}{pp}")
                        tl = TLE if sub == 0 else TLO
                        tlr = rs("TLE") if sub == 0 else rs("TLO")
                        tt = TT1 if sub == 0 else TT2
                        rtt = rs("TT1") if sub == 0 else rs("TT2")
                        pr.st("pe", lambda t, sub=sub, gb=gb: t.matmul(
                            PS[gb][0:65, 0:W], S4[:, sub * 65:(sub + 1) * 65],
                            AB4[:, :], start=True, stop=True),
                            reads=[rs("S4"), rs("AB4")], writes=[rPS[gb]])
                        pr.st("dve", lambda v, tt=tt, LU=LU, gb=gb: v.tensor_mul(
                            tt[:, :], LU[:, :], PS[gb][0:65, 0:W]),
                            reads=[rLU, rPS[gb]], writes=[rtt])
                        pr.st("pe", lambda t, sub=sub, gb=gb: t.matmul(
                            PS[gb][0:65, 0:W],
                            S4[:, (2 + sub) * 65:(3 + sub) * 65],
                            AB4[:, :], start=True, stop=True),
                            reads=[rs("S4"), rs("AB4")], writes=[rPS[gb]])

                        def fcmb(v, tl=tl, tt=tt, GU=GU, gb=gb):
                            v.tensor_mul(tl[:, :], GU[:, :], PS[gb][0:65, 0:W])
                            return v.tensor_add(tl[:, :], tt[:, :], tl[:, :])
                        pr.st("dve", fcmb,
                              reads=[rGU, rPS[gb], rtt],
                              writes=[tlr])

                    def fasm(t, gb=gb):
                        t.matmul(PS[gb][0:128, 0:W], SHE[:, :], TLE[:, :],
                                 start=True, stop=False, skip_group_check=True)
                        return t.matmul(PS[gb][0:128, 0:W], SHO[:, :], TLO[:, :],
                                        start=False, stop=True, skip_group_check=True)
                    pr.st("pe", fasm,
                          reads=[rs("SHE"), rs("SHO"), rs("TLE"), rs("TLO")],
                          writes=[rPS[gb]])
                    pr.st("dve", lambda v, pair=pair, gb=gb: v.tensor_copy(
                        CB[:, pair * W:(pair + 1) * W], PS[gb][0:128, 0:W]),
                        reads=[rPS[gb]], writes=[rs("CB")])

                for pair in range(NP):
                    emit_front(pair)
                    if pair >= 1:
                        emit_back(pair - 1)
                emit_back(NP - 1)

                for tt in range(TTW):
                    x1b = X1[tt % 2]
                    rx1 = rs(f"X1{tt % 2}")
                    for oc in range(DOC):
                        n = min(512, D - oc * 512)
                        bank = oc % 2

                        def fop(t, tt=tt, oc=oc, n=n, bank=bank):
                            t.matmul(PS[bank][0:128, 0:n], ONEB[0:1, 0:128],
                                     BOR[0:1, oc * 512: oc * 512 + n],
                                     start=True, stop=False, skip_group_check=True)
                            last = None
                            for c in range(QKC):
                                last = t.matmul(
                                    PS[bank][0:128, 0:n],
                                    CB[:, c * W + tt * 128: c * W + (tt + 1) * 128],
                                    AW[:, off_wo + c * D + oc * 512:
                                       off_wo + c * D + oc * 512 + n],
                                    start=False, stop=(c == QKC - 1),
                                    skip_group_check=True)
                            return last
                        pr.st("pe", fop,
                              reads=[rs("ONEB"), rs("BOR"), rs("CB"), rs("AW")],
                              writes=[rPS[bank]])
                        pr.st("dve", lambda v, x1b=x1b, tt=tt, oc=oc, n=n, bank=bank:
                              v.tensor_add(x1b[:, oc * 512: oc * 512 + n],
                                           PS[bank][0:128, 0:n],
                                           XA[:, tt * D + oc * 512: tt * D + oc * 512 + n]),
                              reads=[rPS[bank], rs(f"XAt{tt}")], writes=[rx1])
                    pr.st("gp", lambda e, b=b, w=w, tt=tt, x1b=x1b: e.dma_start(
                        out=bass.AP(x1_scr, (b * S + w * W + tt * 128) * D,
                                    [[D, 128], [1, D]]),
                        in_=x1b[:, :]),
                        reads=[rx1], writes=[rs(f"x1_scr_{b * NW + w}")], dma=True)
                    par = tt % 2
                    ln_stats(rx1, lambda i, n, x1b=x1b: x1b[:, i * 512: i * 512 + n],
                             par)
                    slot0 = ((b * NW + w) * TTW + tt) * NST
                    for hlf in range(NST):
                        scr = SCR1 if hlf == 0 else SCR2
                        rscr = rs("SCR1") if hlf == 0 else rs("SCR2")
                        pr.st("dve", lambda v, x1b=x1b, par=par, hlf=hlf, scr=scr:
                              v.tensor_scalar(
                                  scr[:, 0:512], x1b[:, hlf * 512:(hlf + 1) * 512],
                                  MV2[par][:, 0:1], RST[par][:, :],
                                  ALU.subtract, ALU.mult),
                              reads=[rx1, rs(f"MV2{par}"), rs(f"RST{par}")],
                              writes=[rscr])
                        pr.st("dve", lambda v, hlf=hlf, slot0=slot0, scr=scr:
                              v.tensor_reduce(
                                  AM1[:, slot0 + hlf: slot0 + hlf + 1],
                                  scr[:, 0:512],
                                  AX.X, ALU.max, apply_absolute_value=True),
                              reads=[rscr], writes=[rs("AM1")])

                    for rnd in range(2):
                        tbank = ((4, 5) if tt % 2 == 0 else (6, 7))[rnd]
                        scr = SCR1 if rnd == 0 else SCR2
                        rscr = rs("SCR1") if rnd == 0 else rs("SCR2")

                        def ftr2(t, rnd=rnd, tbank=tbank, scr=scr):
                            last = None
                            for j in range(4):
                                last = t.transpose(
                                    PS[tbank][:, j * 128:(j + 1) * 128],
                                    scr[:, j * 128:(j + 1) * 128],
                                    IDF[:, :])
                            return last
                        pr.st("pe", ftr2, reads=[rscr, rs("IDF")],
                              writes=[rPS[tbank]])
                        pr.st("dve", lambda v, tt=tt, rnd=rnd, tbank=tbank:
                              v.tensor_copy(
                                  FA[:, 0:FC * W].rearrange(
                                      "p (c w) -> p c w", c=FC)[:, rnd * 4:rnd * 4 + 4,
                                                                tt * 128:(tt + 1) * 128],
                                  PS[tbank][:, 0:512].rearrange(
                                      "p (c k) -> p c k", c=4)),
                              reads=[rPS[tbank]], writes=[rs("FA")])
                pr.st("gp", lambda e, b=b, w=w: e.dma_start(
                    out=bass.AP(xn2_scr, (b * NW + w) * FC * 128 * W,
                                [[W, 128], [128 * W, FC], [1, W]]),
                    in_=FA[:, 0:FC * W].rearrange("p (c w) -> p c w", c=FC)),
                    reads=[rs("FA"), rs("EG00"), rs("EG10"), rs("EG01"),
                           rs("EG11")],
                    writes=[rs(f"xn2_scr_{b * NW + w}")], dma=True)

        pr.st("dve", lambda v: v.tensor_reduce(AMV[:, :], AM1[:, :], AX.X, ALU.max),
              reads=[rs("AM1")], writes=[rs("AMV")])
        pr.st("pe", lambda t: t.transpose(PS[0][0:1, 0:128], AMV[:, :], IDF[:, :]),
              reads=[rs("AMV"), rs("IDF")], writes=[rPS[0]])
        pr.st("dve", lambda v: v.tensor_reduce(
            CCA[0:1, 0:1], PS[0][0:1, 0:128], AX.X, ALU.max),
            reads=[rPS[0]], writes=[rs("CCA")])
        pr.st("dve", lambda v: v.tensor_scalar(
            CCA[0:1, 0:8], ZR8[:, :], CCA[0:1, 0:1], None, ALU.add),
            reads=[rs("ZR8"), rs("CCA")], writes=[rs("CCA")])
        pr.st("gp", lambda g: g.dma_start(out=cc1i[:, :], in_=CCA[:, :]),
              reads=[rs("CCA")], writes=[rs("cc1i")], dma=True)
        pr.st("gp", lambda g: g.collective_compute(
            "AllReduce", ALU.max, replica_groups=[list(range(NC))],
            ins=[cc1i[:, :]], outs=[cc1o[:, :]]),
            reads=[rs("cc1i")], writes=[rs("cc1o")])
        pr.st("gp", lambda g: g.dma_start(out=CCB[:, :], in_=cc1o[:, :]),
              reads=[rs("cc1o")], writes=[rs("CCB")], dma=True)
        pr.st("pe", lambda t: t.matmul(PS[0][0:128, 0:1], ONEF[:, :],
                                       CCB[0:1, 0:1], start=True, stop=True),
              reads=[rs("ONEF"), rs("CCB")], writes=[rPS[0]])

        def fscal1(v):
            ch("dve", v, v.tensor_copy(SCAL[:, 6:7], PS[0][0:128, 0:1]))
            ch("dve", v, v.tensor_single_scalar(SCAL[:, 0:1], SCAL[:, 6:7],
                                                1e-8, ALU.max))
            ch("dve", v, v.reciprocal(SCAL[:, 7:8], SCAL[:, 0:1]))
            ch("dve", v, v.tensor_scalar_mul(SCAL[:, 1:2], SCAL[:, 7:8], 32767.0))
            return v.tensor_scalar_mul(SCAL[:, 2:3], SCAL[:, 0:1], 1.0 / 32767.0)
        pr.st("dve", fscal1, reads=[rPS[0]], writes=[rs("SCAL")])

        CBK = DFF // 128
        for cb in range(CBK):
            p = cb % 2
            xat = rs(f"XAt{p}")
            xoff = p * D
            x1b = X1[p]
            rx1q = rs(f"X1Q{p}")
            bk0, bk1 = (4, 5) if p == 0 else (6, 7)
            pr.st("sp", lambda e, cb=cb, xoff=xoff: e.dma_start(
                out=XA[:, xoff:xoff + FC * 128].rearrange("p (c k) -> p c k", c=FC),
                in_=bass.AP(wi_t, cb * 128, [[DFF, 128], [128 * DFF, FC], [1, 128]])),
                writes=[xat], dma=True)

            def ftw(t, xoff=xoff, bk0=bk0, bk1=bk1):
                last = None
                for fc in range(FC):
                    bk = bk0 if fc < 4 else bk1
                    last = t.transpose(
                        PS[bk][:, (fc % 4) * 128:(fc % 4) * 128 + 128],
                        XA[:, xoff + fc * 128: xoff + (fc + 1) * 128], IDF[:, :])
                return last
            pr.st("pe", ftw, reads=[xat, rs("IDF")], writes=[rPS[bk0], rPS[bk1]])

            def fev(v, x1b=x1b, bk0=bk0, bk1=bk1):
                v.tensor_copy(x1b[:, 0:512], PS[bk0][:, 0:512])
                return v.tensor_copy(x1b[:, 512:1024], PS[bk1][:, 0:512])
            pr.st("dve", fev, reads=[rPS[bk0], rPS[bk1]], writes=[rx1q])

            def fqw(v, x1b=x1b):
                ch("dve", v, v.tensor_reduce(SQD[:, 0:1], x1b[:, 0:FC * 128],
                                             AX.X, ALU.max,
                                             apply_absolute_value=True))
                ch("dve", v, v.tensor_single_scalar(SQD[:, 1:2], SQD[:, 0:1],
                                                    1e-8, ALU.max))
                ch("dve", v, v.reciprocal(SQD[:, 2:3], SQD[:, 1:2]))
                ch("dve", v, v.tensor_scalar_mul(SQD[:, 3:4], SQD[:, 2:3], 127.0))
                ch("dve", v, v.tensor_scalar_mul(SQD[:, 4:5], SQD[:, 1:2],
                                                 1.0 / 127.0))
                last = None
                for i in range((FC * 128 + 511) // 512):
                    n = min(512, FC * 128 - i * 512)
                    sl = slice(i * 512, i * 512 + n)
                    rint_ops(v, SCR2[:, 0:n], x1b[:, sl], SQD[:, 3:4], n)
                    last = v.tensor_scalar(x1b[:, sl], SCR2[:, 0:n],
                                           SQD[:, 4:5], None, ALU.mult)
                return last
            pr.st("dve", fqw, reads=[rx1q, rs("DAB")],
                  writes=[rs("SQD"), rs("SCR1"), rs("SCR2"), rx1q])

            def ftb(t, x1b=x1b, bk0=bk0, bk1=bk1):
                last = None
                for fc in range(FC):
                    bk = bk0 if fc < 4 else bk1
                    last = t.transpose(
                        PS[bk][:, (fc % 4) * 128:(fc % 4) * 128 + 128],
                        x1b[:, fc * 128:(fc + 1) * 128], IDF[:, :])
                return last
            pr.st("pe", ftb, reads=[rx1q, rs("IDF")], writes=[rPS[bk0], rPS[bk1]])

            def fev2(v, cb=cb, bk0=bk0, bk1=bk1):
                last = None
                for fc in range(FC):
                    bk = bk0 if fc < 4 else bk1
                    last = v.tensor_copy(
                        AW[:, fc * DFF + cb * 128: fc * DFF + (cb + 1) * 128],
                        PS[bk][:, (fc % 4) * 128:(fc % 4) * 128 + 128])
                return last
            pr.st("dve", fev2, reads=[rPS[bk0], rPS[bk1]],
                  writes=[rs(f"AWF{cb}")] +
                  ([rs("AW"), rs("QW2"), rs("KW2"), rs("VW2")]
                   if cb == 0 else []))

        if debug_spill:
            pr.st("sp", lambda e: e.dma_start(
                out=bass.AP(aw_dbg, 0, [[FC * DFF, 128], [1, FC * DFF]]),
                in_=AW[:, 0:FC * DFF]), reads=[rs("AW")], dma=True)
            pr.st("sp", lambda e: e.dma_start(
                out=bass.AP(bin_dbg, 0, [[HC, 128], [1, HC]]),
                in_=BIN[:, :]), reads=[rs("BIN")], dma=True)
        for sl in range(NSLAB):
            pr.st("sp", lambda e, sl=sl: e.dma_start(
                out=FA[:, 0:FC * W].rearrange("p (c w) -> p c w", c=FC),
                in_=bass.AP(xn2_scr, sl * FC * 128 * W,
                            [[W, 128], [128 * W, FC], [1, W]])),
                reads=[rs(f"xn2_scr_{sl}")], writes=[rs("FA")], dma=True)

            def fq1(v):
                last = None
                for fc in range(FC):
                    last = rint_ops(v, TA[:, fc * W:(fc + 1) * W],
                                    FA[:, fc * W:(fc + 1) * W],
                                    SCAL[:, 1:2], W)
                return last
            pr.st("dve", fq1, reads=[rs("FA"), rs("SCAL"), rs("DAB")],
                  writes=[rs("SCR1"), rs("SCR2"), rs("TA")])
            for hc in range(HC):
                bank = hc % 4

                def ff1(t, hc=hc, bank=bank):
                    last = None
                    for fc in range(FC):
                        last = t.matmul(
                            PS[bank][0:128, 0:W],
                            AW[:, fc * DFF + hc * 128: fc * DFF + (hc + 1) * 128],
                            TA[:, fc * W:(fc + 1) * W],
                            start=(fc == 0), stop=(fc == FC - 1))
                    return last
                pr.st("pe", ff1, reads=[rs(f"AWF{hc}"), rs("TA")],
                      writes=[rPS[bank]])
                slot = hc % FC
                pr.st("act", lambda sc, hc=hc, bank=bank, slot=slot: sc.activation(
                    ET[:, slot * W:(slot + 1) * W], PS[bank][0:128, 0:W],
                    AF.Gelu, bias=BIN[:, hc:hc + 1], scale=SCAL[:, 2:3]),
                    reads=[rPS[bank], rs("BIN"), rs("SCAL")], writes=[rs("ET")])
                pr.st("dve", lambda v, sl=sl, hc=hc, slot=slot: v.tensor_reduce(
                    AM2[:, sl * HC + hc: sl * HC + hc + 1],
                    ET[:, slot * W:(slot + 1) * W],
                    AX.X, ALU.max, apply_absolute_value=True),
                    reads=[rs("ET")], writes=[rs("AM2")])
                if slot == FC - 1:
                    g0 = hc - FC + 1
                    pr.st("gp", lambda e, sl=sl, g0=g0: e.dma_start(
                        out=bass.AP(hid_scr, (sl * HC + g0) * 128 * W,
                                    [[W, 128], [128 * W, FC], [1, W]]),
                        in_=ET[:, 0:FC * W].rearrange("p (c w) -> p c w", c=FC)),
                        reads=[rs("ET")], writes=[rs(f"hid_scr_{sl}")], dma=True)

        pr.st("dve", lambda v: v.tensor_reduce(AMV[:, :], AM2[:, :], AX.X, ALU.max),
              reads=[rs("AM2")], writes=[rs("AMV")])
        pr.st("pe", lambda t: t.transpose(PS[0][0:1, 0:128], AMV[:, :], IDF[:, :]),
              reads=[rs("AMV"), rs("IDF")], writes=[rPS[0]])
        pr.st("dve", lambda v: v.tensor_reduce(
            CCA[0:1, 0:1], PS[0][0:1, 0:128], AX.X, ALU.max),
            reads=[rPS[0]], writes=[rs("CCA")])
        pr.st("dve", lambda v: v.tensor_scalar(
            CCA[0:1, 0:8], ZR8[:, :], CCA[0:1, 0:1], None, ALU.add),
            reads=[rs("ZR8"), rs("CCA")], writes=[rs("CCA")])
        pr.st("gp", lambda g: g.dma_start(out=cc2i[:, :], in_=CCA[:, :]),
              reads=[rs("CCA")], writes=[rs("cc2i")], dma=True)
        pr.st("gp", lambda g: g.collective_compute(
            "AllReduce", ALU.max, replica_groups=[list(range(NC))],
            ins=[cc2i[:, :]], outs=[cc2o[:, :]]),
            reads=[rs("cc2i")], writes=[rs("cc2o")])
        pr.st("gp", lambda g: g.dma_start(out=CCB[:, :], in_=cc2o[:, :]),
              reads=[rs("cc2o")], writes=[rs("CCB")], dma=True)
        pr.st("pe", lambda t: t.matmul(PS[0][0:128, 0:1], ONEF[:, :],
                                       CCB[0:1, 0:1], start=True, stop=True),
              reads=[rs("ONEF"), rs("CCB")], writes=[rPS[0]])

        def fscal2(v):
            ch("dve", v, v.tensor_copy(SCAL[:, 6:7], PS[0][0:128, 0:1]))
            ch("dve", v, v.tensor_single_scalar(SCAL[:, 3:4], SCAL[:, 6:7],
                                                1e-8, ALU.max))
            ch("dve", v, v.reciprocal(SCAL[:, 7:8], SCAL[:, 3:4]))
            ch("dve", v, v.tensor_scalar_mul(SCAL[:, 4:5], SCAL[:, 7:8], 32767.0))
            return v.tensor_scalar_mul(SCAL[:, 5:6], SCAL[:, 3:4], 1.0 / 32767.0)
        pr.st("dve", fscal2, reads=[rPS[0]], writes=[rs("SCAL")])

        for hc in range(HC):
            p = hc % 2
            xat = rs(f"XAt{p}")
            xoff = p * D
            pr.st("act", lambda e, hc=hc, xoff=xoff: e.dma_start(
                out=XA[:, xoff:xoff + D],
                in_=bass.AP(wo2_t, hc * 128 * D, [[D, 128], [1, D]])),
                writes=[xat], dma=True)

            def fqo(v, hc=hc, xoff=xoff):
                ch("dve", v, v.tensor_reduce(SQD[:, 0:1], XA[:, xoff:xoff + D],
                                             AX.X, ALU.max,
                                             apply_absolute_value=True))
                ch("dve", v, v.tensor_single_scalar(SQD[:, 1:2], SQD[:, 0:1],
                                                    1e-8, ALU.max))
                ch("dve", v, v.reciprocal(SQD[:, 2:3], SQD[:, 1:2]))
                ch("dve", v, v.tensor_scalar_mul(SQD[:, 3:4], SQD[:, 2:3], 127.0))
                ch("dve", v, v.tensor_scalar_mul(SQD[:, 4:5], SQD[:, 1:2],
                                                 1.0 / 127.0))
                last = None
                for i in range(DOC):
                    n = min(512, D - i * 512)
                    sl = slice(xoff + i * 512, xoff + i * 512 + n)
                    rint_ops(v, SCR2[:, 0:n], XA[:, sl], SQD[:, 3:4], n)
                    last = v.tensor_scalar(AW[:, hc * D + i * 512: hc * D + i * 512 + n],
                                           SCR2[:, 0:n], SQD[:, 4:5], None,
                                           ALU.mult)
                return last
            pr.st("dve", fqo, reads=[xat, rs("DAB")],
                  writes=[rs(f"AWF2_{hc}"), rs("SQD"), rs("SCR1"), rs("SCR2")] +
                  ([rs(f"AWF{c}") for c in range(CBK)] if hc == 0 else []))

        NPS = TTW * DOC
        assert NPS <= 8
        out_dmas = []
        def b2_hid_load(sl, g8):
            buf, rbuf = (ET, rs("ET")) if g8 % 2 == 0 else (CB, rs("CB"))
            ng = min(FC, HC - g8 * FC)
            pr.st("sp", lambda e, sl=sl, g8=g8, ng=ng, buf=buf: e.dma_start(
                out=buf[:, 0:ng * W].rearrange("p (c w) -> p c w", c=ng),
                in_=bass.AP(hid_scr, (sl * HC + g8 * FC) * 128 * W,
                            [[W, 128], [128 * W, ng], [1, W]])),
                reads=[rs(f"hid_scr_{sl}")], writes=[rbuf], dma=True)

        for sl in range(NSLAB):
            b, w = sl // NW, sl % NW
            b2_hid_load(sl, 0)
            pr.st("sp", lambda e, b=b, w=w: e.dma_start(
                out=XA[:, :].rearrange("p (t d) -> p t d", t=TTW),
                in_=bass.AP(x1_scr, (b * S + w * W) * D,
                            [[D, 128], [128 * D, TTW], [1, D]])),
                reads=[rs(f"x1_scr_{sl}")],
                writes=[rs(f"XAt{tt}") for tt in range(TTW)], dma=True)
            for g8 in range((HC + FC - 1) // FC):
                buf, rbuf = (ET, rs("ET")) if g8 % 2 == 0 else (CB, rs("CB"))
                ng = min(FC, HC - g8 * FC)
                if g8 > 0:
                    b2_hid_load(sl, g8)
                for j in range(ng):
                    hc = g8 * FC + j
                    r2b = QW[:, (hc % 2) * W:(hc % 2 + 1) * W]
                    rr2 = rs(f"R2{hc % 2}")

                    def fq2(v, buf=buf, j=j, r2b=r2b):
                        return rint_ops(v, r2b, buf[:, j * W:(j + 1) * W],
                                        SCAL[:, 4:5], W)
                    pr.st("dve", fq2, reads=[rbuf, rs("SCAL"), rs("DAB")],
                          writes=[rs("SCR1"), rs("SCR2"), rr2])

                    def ff2(t, hc=hc, r2b=r2b):
                        last = None
                        for tt in range(TTW):
                            for oc in range(DOC):
                                n = min(512, D - oc * 512)
                                last = t.matmul(
                                    PS[tt * DOC + oc][0:128, 0:n],
                                    r2b[0:128, tt * 128:(tt + 1) * 128],
                                    AW[:, hc * D + oc * 512: hc * D + oc * 512 + n],
                                    start=(hc == 0), stop=(hc == HC - 1))
                        return last
                    pr.st("pe", ff2, reads=[rr2, rs(f"AWF2_{hc}")],
                          writes=[rPS[i] for i in range(NPS)])
            for tt in range(TTW):
                for oc in range(DOC):
                    n = min(512, D - oc * 512)
                    ob = X1[(tt * DOC + oc) % 2]
                    rob = rs(f"X1{(tt * DOC + oc) % 2}")

                    def fep(v, tt=tt, oc=oc, n=n, ob=ob):
                        v.scalar_tensor_tensor(
                            SCR1[:, 0:n], PS[tt * DOC + oc][0:128, 0:n],
                            SCAL[:, 5:6], BB2[:, oc * 512: oc * 512 + n],
                            ALU.mult, ALU.add)
                        return v.tensor_add(
                            ob[0:128, 0:n], SCR1[:, 0:n],
                            XA[:, tt * D + oc * 512: tt * D + oc * 512 + n])
                    pr.st("dve", fep,
                          reads=[rPS[tt * DOC + oc], rs("SCAL"), rs("BB2"),
                                 rs(f"XAt{tt}")],
                          writes=[rs("SCR1"), rob])
                    stp = pr.st("gp", lambda e, b=b, w=w, tt=tt, oc=oc, n=n, ob=ob:
                                e.dma_start(
                                    out=bass.AP(out_t,
                                                (b * S + w * W + tt * 128) * D + oc * 512,
                                                [[D, 128], [1, n]]),
                                    in_=ob[0:128, 0:n]),
                                reads=[rob], dma=True)
                    out_dmas.append(stp)
        if debug_spill:
            pr.st("sp", lambda e: e.dma_start(
                out=bass.AP(sc_dbg, 0, [[8, 128], [1, 8]]), in_=SCAL[:, :]),
                reads=[rs("SCAL")], dma=True)
        pr.st("gp", lambda g: g.nop(), deps=out_dmas)

        block = es.enter_context(nc.Block())
        pr.finalize(block, esems, dsems)
    return nc


def kernel(L0_attn_x, L0_attn_W_qkv, L0_attn_W_gate, L0_attn_b_gate,
           L0_attn_W_out, L0_attn_b_out, L0_ffn_W_in, L0_ffn_b_in,
           L0_ffn_W_out, L0_ffn_b_out):
    P = REAL
    NC = P["NC"]
    x = np.ascontiguousarray(np.asarray(L0_attn_x, dtype=np.float32))
    Btot = x.shape[0]
    bper = Btot // NC
    common = {
        "w_qkv": np.ascontiguousarray(L0_attn_W_qkv, dtype=np.float32),
        "w_gate": np.ascontiguousarray(L0_attn_W_gate, dtype=np.float32),
        "b_gate": np.ascontiguousarray(L0_attn_b_gate, dtype=np.float32).reshape(1, 1),
        "w_out": np.ascontiguousarray(L0_attn_W_out, dtype=np.float32),
        "b_out": np.ascontiguousarray(L0_attn_b_out, dtype=np.float32).reshape(1, -1),
        "w_in": np.ascontiguousarray(L0_ffn_W_in, dtype=np.float32),
        "b_in": np.ascontiguousarray(L0_ffn_b_in, dtype=np.float32).reshape(1, -1),
        "w_out2": np.ascontiguousarray(L0_ffn_W_out, dtype=np.float32),
        "b_out2": np.ascontiguousarray(L0_ffn_b_out, dtype=np.float32).reshape(1, -1),
    }
    in_maps = []
    for c in range(NC):
        m = dict(common)
        m["x"] = np.ascontiguousarray(x[c * bper:(c + 1) * bper])
        in_maps.append(m)
    nc = build_program(P)
    res = run_bass_kernel_spmd(nc, in_maps, core_ids=list(range(NC)))
    out = np.concatenate([np.asarray(r["out"]) for r in res.results], axis=0)
    return out.astype(np.float32)


# version 31 (final)
# speedup vs baseline: 1.0709x; 1.0161x over previous
import numpy as np
from contextlib import ExitStack

import concourse.bass as bass
import concourse.mybir as mybir
from concourse.bass_utils import run_bass_kernel_spmd

F32 = mybir.dt.float32
BF16 = mybir.dt.bfloat16
AF = mybir.ActivationFunctionType
ALU = mybir.AluOpType
AX = mybir.AxisListType

REAL = dict(B=2, S=4096, D=1024, H=16, DH=64, W=512, ST=32, DFF=4096, NC=8)


class Res:
    __slots__ = ("writer", "readers", "name")

    def __init__(self, name=""):
        self.writer = None
        self.readers = []
        self.name = name


class Step:
    __slots__ = ("eng", "fn", "deps", "dma", "val")

    def __init__(self, eng, fn, deps, dma, val):
        self.eng, self.fn, self.deps, self.dma, self.val = eng, fn, deps, dma, val


class Prog:
    def __init__(self):
        self.steps = []
        self.dma_count = {"gp": 0, "sp": 0, "act": 0}
        self.eng_count = {e: 0 for e in ("pe", "act", "dve", "gp", "sp")}

    def st(self, eng, fn, reads=(), writes=(), deps=(), dma=False):
        d = set()
        for r in reads:
            if r.writer is not None:
                d.add(r.writer)
        for w in writes:
            if w.writer is not None:
                d.add(w.writer)
            d.update(w.readers)
        d.update(x for x in deps if x is not None)
        if dma:
            self.dma_count[eng] += 1
            val = 16 * self.dma_count[eng]
        else:
            self.eng_count[eng] += 1
            val = self.eng_count[eng]
        stp = Step(eng, fn, d, dma, val)
        for r in reads:
            r.readers.append(stp)
        for w in writes:
            w.writer = stp
            w.readers = []
        self.steps.append(stp)
        return stp

    def finalize(self, block, esems, dsems):
        order = {}
        for i, s in enumerate(self.steps):
            order[id(s)] = i

        def emit(name, eng):
            water = {}
            for s in self.steps:
                if s.eng != name:
                    continue
                waits = {}
                for dp in sorted(s.deps, key=lambda x: order[id(x)]):
                    if dp.eng == name and not dp.dma:
                        if s.val - dp.val > 4:
                            continue
                    sem = dsems[dp.eng] if dp.dma else esems[dp.eng]
                    key = id(sem)
                    if waits.get(key, (None, -1))[1] < dp.val:
                        waits[key] = (sem, dp.val)
                for sem, val in waits.values():
                    if water.get(id(sem), -1) >= val:
                        continue
                    eng.wait_ge(sem, val)
                    water[id(sem)] = val
                last = s.fn(eng)
                assert last is not None, f"step fn returned None on {name}"
                last.then_inc(dsems[name] if s.dma else esems[name],
                              16 if s.dma else 1)

        @block.tensor
        def _(t):
            emit("pe", t)

        @block.scalar
        def _(sc):
            emit("act", sc)

        @block.vector
        def _(v):
            emit("dve", v)

        @block.gpsimd
        def _(g):
            emit("gp", g)

        @block.sync
        def _(s):
            emit("sp", s)


def build_program(P, detect_races=False, debug_spill=False):
    B, S, D, H, DH, W, ST, DFF, NC = (P["B"], P["S"], P["D"], P["H"], P["DH"],
                                      P["W"], P["ST"], P["DFF"], P["NC"])
    FC = D // 128
    HDH = H * DH
    QKC = HDH // 128
    NW = S // W
    TTW = W // 128
    KC = 2 * W // 128
    SK = S // ST
    HC = DFF // 128
    NP = H // 2
    VB = 65
    VWID = H * VB
    VO = (HDH + 511) // 512
    DOC = (D + 511) // 512
    NST = (D + 511) // 512
    SCALE = 1.0 / float(np.sqrt(DH))
    C1 = 12582912.0
    C2 = 6755399441055744.0
    NSLAB = B * NW
    assert SK <= 128 and D % 128 == 0 and HDH % 128 == 0 and DFF % 128 == 0

    nc = bass.Bass("TRN2", target_bir_lowering=False, debug=False,
                   detect_race_conditions=detect_races)

    x_t = nc.dram_tensor("x", [B, S, D], F32, kind="ExternalInput")
    wqkv_t = nc.dram_tensor("w_qkv", [D, 3 * HDH], F32, kind="ExternalInput")
    wg_t = nc.dram_tensor("w_gate", [D, 1], F32, kind="ExternalInput")
    bg_t = nc.dram_tensor("b_gate", [1, 1], F32, kind="ExternalInput")
    wo_t = nc.dram_tensor("w_out", [HDH, D], F32, kind="ExternalInput")
    bo_t = nc.dram_tensor("b_out", [1, D], F32, kind="ExternalInput")
    wi_t = nc.dram_tensor("w_in", [D, DFF], F32, kind="ExternalInput")
    bi_t = nc.dram_tensor("b_in", [1, DFF], F32, kind="ExternalInput")
    wo2_t = nc.dram_tensor("w_out2", [DFF, D], F32, kind="ExternalInput")
    bo2_t = nc.dram_tensor("b_out2", [1, D], F32, kind="ExternalInput")
    out_t = nc.dram_tensor("out", [B, S, D], F32, kind="ExternalOutput")

    kscr = {"kind": "ExternalOutput"} if debug_spill else {}
    if debug_spill:
        xa_dbg = nc.dram_tensor("xa_dbg", [B, 128, (W // 128) * D], F32,
                                kind="ExternalOutput")
        ta_dbg = nc.dram_tensor("ta_dbg", [B, 128, (W // 128) * D], BF16,
                                kind="ExternalOutput")
        fa_dbg = nc.dram_tensor("fa_dbg", [B, 128, (D // 128) * W], BF16,
                                kind="ExternalOutput")
        aw_dbg = nc.dram_tensor("aw_dbg", [128, (D // 128) * DFF], BF16,
                                kind="ExternalOutput")
        bin_dbg = nc.dram_tensor("bin_dbg", [128, DFF // 128], F32,
                                 kind="ExternalOutput")
        sc_dbg = nc.dram_tensor("sc_dbg", [128, 8], F32, kind="ExternalOutput")
        wt_dbg = nc.dram_tensor("wt_dbg", [128, (D // 128) * 128], F32,
                                kind="ExternalOutput")
        wq_dbg = nc.dram_tensor("wq_dbg", [128, (D // 128) * 128], F32,
                                kind="ExternalOutput")
        sqd_dbg = nc.dram_tensor("sqd_dbg", [128, 6], F32, kind="ExternalOutput")
    q_scr = nc.dram_tensor("q_scr", [B, NW, QKC, 128, W], BF16, **kscr)
    k_scr = nc.dram_tensor("k_scr", [B, QKC, 128, S], BF16, **kscr)
    v_scr = nc.dram_tensor("v_scr", [B, S, VWID], BF16, **kscr)
    x1_scr = nc.dram_tensor("x1_scr", [B, S, D], F32, **kscr)
    xn2_scr = nc.dram_tensor("xn2_scr", [NSLAB, FC, 128, W], BF16, **kscr)
    hid_scr = nc.dram_tensor("hid_scr", [NSLAB, HC, 128, W], BF16, **kscr)
    cb_dbg = (nc.dram_tensor("cb_dbg", [B * NW, 128, (HDH // 128) * W], BF16)
              if debug_spill else None)
    cc1i = nc.dram_tensor("cc1i", [1, 8], F32)
    cc1o = nc.dram_tensor("cc1o", [1, 8], F32)
    cc2i = nc.dram_tensor("cc2i", [1, 8], F32)
    cc2o = nc.dram_tensor("cc2o", [1, 8], F32)

    off_wqk = 0
    off_wv = FC * 2 * HDH
    off_wg = off_wv + FC * HDH
    off_wo = off_wg + FC
    AW_COLS = max(off_wo + QKC * D, FC * DFF, HC * D)

    es = ExitStack()
    with es:
        sb = lambda n, s, d: es.enter_context(nc.sbuf_tensor(n, s, d))
        AW = sb("AW", [128, AW_COLS], BF16)
        XA = sb("XA", [128, TTW * D], F32)
        TA = sb("TA", [128, max(TTW * D, KC * W)], BF16)
        FA = sb("FA", [128, max(FC * W, 4 * W)], BF16)
        QW = sb("QW", [128, QKC * W], BF16)
        KW = sb("KW", [128, QKC * 2 * W], BF16)
        VW = sb("VW", [128, KC * VWID], BF16)
        X1 = [sb("X1A", [128, D], F32), sb("X1B", [128, D], F32)]
        ET = sb("ET", [128, KC * W], BF16)
        CB = sb("CB", [128, QKC * W], BF16)
        LU2 = [[sb("LUE0", [65, W], BF16), sb("LUE1", [65, W], BF16)],
               [sb("LUO0", [65, W], BF16), sb("LUO1", [65, W], BF16)]]
        GU2 = [[sb("GUE0", [65, W], BF16), sb("GUE1", [65, W], BF16)],
               [sb("GUO0", [65, W], BF16), sb("GUO1", [65, W], BF16)]]
        TLE = sb("TLE", [65, W], BF16)
        TLO = sb("TLO", [65, W], BF16)
        TT1 = sb("TT1", [65, W], BF16)
        TT2 = sb("TT2", [65, W], BF16)
        SCR1 = sb("SCR1", [128, 512], F32)
        SCR2 = sb("SCR2", [128, 512], F32)
        G1 = sb("G1", [65, S], BF16)
        GR1 = None

        AB4 = sb("AB4", [4, W], BF16)
        GMR = sb("GMR", [1, W], BF16)
        GRW = sb("GRW", [1, W], BF16)
        E4 = sb("E4", [65, 16], BF16)
        S4 = sb("S4", [4, 4 * 65], BF16)
        R1S = sb("R1S", [1, 16], BF16)
        GS2 = sb("GS2", [65, 8], BF16)
        KSTR = sb("KSTR", [128, B * QKC * SK], BF16)
        VSTR = sb("VSTR", [SK, B * VWID], BF16)
        BIN = sb("BIN", [128, HC], F32)
        BB2 = sb("BB2", [128, D], F32)
        BOR = sb("BOR", [1, D], BF16)
        IDB = sb("IDB", [128, 128], BF16)
        IDF = sb("IDF", [128, 128], F32)
        SHE = sb("SHE", [65, 128], BF16)
        SHO = sb("SHO", [65, 128], BF16)
        ONEB = sb("ONEB", [1, 128], BF16)
        ONEF = sb("ONEF", [1, 128], F32)
        EPS = sb("EPS", [128, 1], F32)
        ZR8 = sb("ZR8", [1, 8], F32)
        NBG = sb("NBG", [1, 1], F32)
        BGT = sb("BGT", [1, 1], F32)
        MV6 = [sb("MV6A", [128, 6 * NST], F32), sb("MV6B", [128, 6 * NST], F32)]
        MV2 = [sb("MV2A", [128, 2], F32), sb("MV2B", [128, 2], F32)]
        LNV = [sb("LNVA", [128, 1], F32), sb("LNVB", [128, 1], F32)]
        RST = [sb("RSTA", [128, 1], F32), sb("RSTB", [128, 1], F32)]
        AM1 = sb("AM1", [128, NSLAB * TTW * NST], BF16)
        AM2 = sb("AM2", [128, NSLAB * HC], BF16)
        AMV = sb("AMV", [128, 1], F32)
        SCAL = sb("SCAL", [128, 8], F32)
        CCA = sb("CCA", [1, 8], F32)
        CCB = sb("CCB", [1, 8], F32)
        SQD = sb("SQD", [128, 6], F32)
        DAB = sb("DAB", [128, 4], F32)


        PA = [es.enter_context(nc.psum_tensor(f"PA{i}", [128, 1024], F32))
              for i in range(4)]
        PS = [PA[i // 2][:, (i % 2) * 512:(i % 2) * 512 + 512] for i in range(8)]

        esems = {e: es.enter_context(nc.semaphore(f"sem_{e}"))
                 for e in ("pe", "act", "dve", "gp", "sp")}
        chain_sem = {e: es.enter_context(nc.semaphore(f"chain_{e}"))
                     for e in ("dve", "act")}
        chain_cnt = {"dve": 0, "act": 0}

        def ch(engname, eng, inst):
            chain_cnt[engname] += 1
            inst.then_inc(chain_sem[engname], 1)
            eng.wait_ge(chain_sem[engname], chain_cnt[engname])
            return inst
        dsems = {"gp": es.enter_context(nc.semaphore("dma_gp")),
                 "sp": es.enter_context(nc.semaphore("dma_sp")),
                 "act": es.enter_context(nc.semaphore("dma_act"))}

        pr = Prog()
        R = {}

        def rs(name):
            if name not in R:
                R[name] = Res(name)
            return R[name]

        rPS = [rs(f"PS{i}") for i in range(8)]

        def fsetup_gp(g):
            g.iota(SCR1[0:128, 0:128][:, :], [[1, 128]], channel_multiplier=0,
                   allow_small_or_imprecise_dtypes=True)
            g.iota(SCR2[0:128, 0:128][:, :], [[0, 128]], channel_multiplier=1,
                   allow_small_or_imprecise_dtypes=True)
            g.iota(XA[0:65, 0:128], [[0, 128]], base=-1, channel_multiplier=1,
                   allow_small_or_imprecise_dtypes=True)
            g.iota(XA[0:65, 128:256], [[0, 128]], base=63, channel_multiplier=1,
                   allow_small_or_imprecise_dtypes=True)
            g.memset(ONEB[:, :], 1.0)
            g.memset(E4[:, :], 0.0)
            g.memset(R1S[:, :], 0.0)
            g.memset(GS2[:, :], 0.0)
            for i in range(4):
                g.memset(E4[0:1, i * 4 + i: i * 4 + i + 1], 1.0)
                g.memset(R1S[0:1, i * 4 + i: i * 4 + i + 1], 1.0)
            g.memset(GS2[0:1, 0:2], 1.0)
            g.memset(GS2[0:1, 6:8], 1.0)
            g.memset(GS2[64:65, 0:2], 1.0)
            g.memset(GS2[64:65, 6:8], 1.0)
            g.memset(ONEF[:, :], 1.0)
            g.memset(DAB[:, 3:4], 1.0)
            g.memset(EPS[:, :], 1e-5)
            g.memset(ZR8[:, :], 0.0)
            return g.memset(VW[:, 0:KC * VWID:VB], 1.0)

        pr.st("gp", fsetup_gp,
              writes=[rs("ONEB"), rs("ONEF"), rs("EPS"), rs("ZR8"), rs("VW"),
                      rs("SCR1"), rs("SCR2"), rs("XAt0"), rs("E4"), rs("R1S"),
                      rs("GS2"), rs("DAB")])

        def fsetup_ve(v):
            v.tensor_tensor(IDB[:, :], SCR1[0:128, 0:128][:, :],
                            SCR2[0:128, 0:128][:, :], ALU.is_equal)
            v.tensor_tensor(IDF[:, :], SCR1[0:128, 0:128][:, :],
                            SCR2[0:128, 0:128][:, :], ALU.is_equal)
            v.tensor_tensor(SHE[:, :], SCR1[0:128, 0:128][0:65, :],
                            XA[0:65, 0:128], ALU.is_equal)
            v.tensor_tensor(SHO[:, :], SCR1[0:128, 0:128][0:65, :],
                            XA[0:65, 128:256], ALU.is_equal)
            return v.memset(SHO[0:1, :], 0.0)

        pr.st("dve", fsetup_ve,
              reads=[rs("SCR1"), rs("SCR2"), rs("XAt0")],
              writes=[rs("IDB"), rs("IDF"), rs("SHE"), rs("SHO")])

        def fdab(v):
            ch("dve", v, v.tensor_scalar(DAB[:, 0:1], DAB[:, 3:4], C2, C2,
                                         ALU.add, ALU.subtract))
            ch("dve", v, v.tensor_scalar(DAB[:, 1:2], DAB[:, 0:1], -1.0, 1.0,
                                         ALU.mult, ALU.add))
            ch("dve", v, v.tensor_scalar(DAB[:, 2:3], DAB[:, 0:1], C2, None,
                                         ALU.mult))
            ch("dve", v, v.tensor_scalar(DAB[:, 3:4], DAB[:, 1:2], C1, None,
                                         ALU.mult))
            ch("dve", v, v.tensor_add(DAB[:, 2:3], DAB[:, 2:3], DAB[:, 3:4]))
            return v.tensor_scalar_mul(DAB[:, 2:3], DAB[:, 2:3], -1.0)
        pr.st("dve", fdab, writes=[rs("DAB")])

        def rint_ops(v, dst, src, scal, n):
            v.tensor_scalar(SCR1[:, 0:n], src, scal, C1, ALU.mult, ALU.add)
            v.tensor_scalar(SCR2[:, 0:n], src, scal, C2, ALU.mult, ALU.add)
            v.tensor_scalar(SCR1[:, 0:n], SCR1[:, 0:n], DAB[:, 1:2],
                            DAB[:, 2:3], ALU.mult, ALU.add)
            return v.scalar_tensor_tensor(dst, SCR2[:, 0:n], DAB[:, 0:1],
                                          SCR1[:, 0:n], ALU.mult, ALU.add)

        for fc in range(FC):
            pr.st("gp", lambda g, fc=fc: g.dma_start(
                out=AW[:, off_wqk + fc * 2 * HDH: off_wqk + (fc + 1) * 2 * HDH],
                in_=bass.AP(wqkv_t, fc * 128 * 3 * HDH, [[3 * HDH, 128], [1, 2 * HDH]])),
                writes=[rs("AW")], dma=True)
            pr.st("gp", lambda g, fc=fc: g.dma_start(
                out=AW[:, off_wv + fc * HDH: off_wv + (fc + 1) * HDH],
                in_=bass.AP(wqkv_t, fc * 128 * 3 * HDH + 2 * HDH,
                            [[3 * HDH, 128], [1, HDH]])),
                writes=[rs("AW")], dma=True)
        pr.st("gp", lambda g: g.dma_start(
            out=AW[:, off_wg: off_wg + FC],
            in_=bass.AP(wg_t, 0, [[1, 128], [128, FC]]),
            allow_slow_non_contiguous=True),
            writes=[rs("AW")], dma=True)
        for c in range(QKC):
            pr.st("gp", lambda g, c=c: g.dma_start(
                out=AW[:, off_wo + c * D: off_wo + (c + 1) * D],
                in_=bass.AP(wo_t, c * 128 * D, [[D, 128], [1, D]])),
                writes=[rs("AW")], dma=True)
        pr.st("gp", lambda g: g.dma_start(out=BOR[:, :], in_=bo_t[:, :]),
              writes=[rs("BOR")], dma=True)
        pr.st("gp", lambda g: g.dma_start(out=BGT[:, :], in_=bg_t[:, :]),
              writes=[rs("BGT")], dma=True)
        pr.st("sp", lambda s: s.dma_start(
            out=BIN[:, :], in_=bass.AP(bi_t, 0, [[1, 128], [128, HC]]),
            allow_slow_non_contiguous=True),
            writes=[rs("BIN")], dma=True)
        pr.st("sp", lambda s: s.dma_start(out=XA[0:1, 0:D], in_=bo2_t[:, :]),
              writes=[rs("XAt0")], dma=True)
        pr.st("dve", lambda v: v.tensor_scalar_mul(NBG[:, :], BGT[:, :], -1.0),
              reads=[rs("BGT")], writes=[rs("NBG")])
        for oc in range(DOC):
            n = min(512, D - oc * 512)
            pr.st("pe", lambda t, oc=oc, n=n: t.matmul(
                PS[0][0:128, 0:n], ONEF[:, :], XA[0:1, oc * 512: oc * 512 + n],
                start=True, stop=True),
                reads=[rs("ONEF"), rs("XAt0")], writes=[rPS[0]])
            pr.st("dve", lambda v, oc=oc, n=n: v.tensor_copy(
                BB2[:, oc * 512: oc * 512 + n], PS[0][0:128, 0:n]),
                reads=[rPS[0]], writes=[rs("BB2")])

        def fs4(t):
            last = None
            for i in range(4):
                last = t.matmul(PS[2][0:4, i * 65:(i + 1) * 65],
                                R1S[0:1, i * 4:(i + 1) * 4], ONEB[0:1, 0:65],
                                start=True, stop=True)
            return last
        pr.st("pe", fs4, reads=[rs("R1S"), rs("ONEB")], writes=[rPS[2]])
        pr.st("dve", lambda v: v.tensor_copy(S4[:, :], PS[2][0:4, 0:4 * 65]),
              reads=[rPS[2]], writes=[rs("S4")])

        def ln_stats(src_res, src_ap_fn, par=0):
            mv6, mv2, lnv, rst = MV6[par], MV2[par], LNV[par], RST[par]

            def f(v, mv6=mv6, mv2=mv2):
                for i in range(NST):
                    n = min(512, D - i * 512)
                    ch("dve", v, v.bn_stats(mv6[:, 6 * i: 6 * i + 6],
                                            src_ap_fn(i, n)))
                return v.bn_aggr(mv2[:, :], mv6[:, 0:6 * NST])
            pr.st("dve", f, reads=[src_res],
                  writes=[rs(f"MV6{par}"), rs(f"MV2{par}")])

            def f2(sc, mv2=mv2, lnv=lnv, rst=rst):
                ch("act", sc, sc.activation(lnv[:, :], mv2[:, 1:2], AF.Ln,
                                            bias=EPS[:, :]))
                return sc.activation(rst[:, :], lnv[:, :], AF.Exp, scale=-0.5)
            pr.st("act", f2, reads=[rs(f"MV2{par}"), rs("EPS")],
                  writes=[rs(f"LNV{par}"), rs(f"RST{par}")])

        QW2 = AW[:, 0:QKC * W]
        KW2 = AW[:, QKC * W: QKC * W + QKC * 2 * W]
        VW2 = AW[:, QKC * 3 * W: QKC * 3 * W + KC * VWID]
        def a1_bufs(s):
            return (FA, rs("FA")) if s % 2 == 0 else (CB, rs("CB"))

        def a1_phase1(b, s):
            FAb, rFA = a1_bufs(s)
            pr.st("sp", lambda e, b=b, s=s: e.dma_start(
                out=XA[:, :].rearrange("p (t d) -> p t d", t=TTW),
                in_=bass.AP(x_t, (b * S + s * W) * D,
                            [[D, 128], [128 * D, TTW], [1, D]])),
                writes=[rs(f"XAt{tt}") for tt in range(TTW)], dma=True)
            for tt in range(TTW):
                par = tt % 2
                x1b = X1[par]
                rx1 = rs(f"X1{par}")
                ln_stats(rs(f"XAt{tt}"),
                         lambda i, n, tt=tt: XA[:, tt * D + i * 512: tt * D + i * 512 + n],
                         par)
                pr.st("dve", lambda v, tt=tt, par=par, x1b=x1b: v.tensor_scalar(
                    x1b[:, :], XA[:, tt * D: (tt + 1) * D],
                    MV2[par][:, 0:1], RST[par][:, :], ALU.subtract, ALU.mult),
                    reads=[rs(f"XAt{tt}"), rs(f"MV2{par}"), rs(f"RST{par}")],
                    writes=[rx1])
                for rnd in range(2):
                    tbank = ((7, 4) if tt % 2 == 0 else (5, 6))[rnd]

                    def ftr(t, tt=tt, rnd=rnd, tbank=tbank, x1b=x1b):
                        last = None
                        for j in range(4):
                            fc = rnd * 4 + j
                            last = t.transpose(
                                PS[tbank][:, j * 128:(j + 1) * 128],
                                x1b[:, fc * 128:(fc + 1) * 128],
                                IDF[:, :])
                        return last
                    pr.st("pe", ftr, reads=[rx1, rs("IDF")],
                          writes=[rPS[tbank]])
                    pr.st("dve", lambda v, tt=tt, rnd=rnd, tbank=tbank, FAb=FAb:
                          v.tensor_copy(
                              FAb[:, 0:FC * W].rearrange(
                                  "p (c w) -> p c w", c=FC)[:, rnd * 4:rnd * 4 + 4,
                                                            tt * 128:(tt + 1) * 128],
                              PS[tbank][:, 0:512].rearrange(
                                  "p (c k) -> p c k", c=4)),
                          reads=[rPS[tbank]], writes=[rFA])

        def a1_phase2a(b, s):
            FAb, rFA = a1_bufs(s)
            for oc in range(2 * QKC):
                bank = oc % 2

                def fqk(t, oc=oc, bank=bank, FAb=FAb):
                    last = None
                    for fc in range(FC):
                        last = t.matmul(
                            PS[bank][0:128, 0:W],
                            AW[:, off_wqk + fc * 2 * HDH + oc * 128:
                               off_wqk + fc * 2 * HDH + (oc + 1) * 128],
                            FAb[:, fc * W:(fc + 1) * W],
                            start=(fc == 0), stop=(fc == FC - 1))
                    return last
                pr.st("pe", fqk, reads=[rs("AW"), rFA], writes=[rPS[bank]])
                if oc < QKC:
                    pr.st("act", lambda sc, oc=oc, bank=bank: sc.copy(
                        QW[:, oc * W:(oc + 1) * W], PS[bank][0:128, 0:W]),
                        reads=[rPS[bank]], writes=[rs("QW")])
                else:
                    c = oc - QKC
                    pr.st("act", lambda sc, c=c, bank=bank: sc.copy(
                        KW[:, c * W:(c + 1) * W], PS[bank][0:128, 0:W]),
                        reads=[rPS[bank]], writes=[rs("KW")])
                    pr.st("dve", lambda v, c=c, s=s, b=b: v.tensor_copy(
                        KSTR[:, b * QKC * SK + c * SK + s * (W // ST):
                             b * QKC * SK + c * SK + (s + 1) * (W // ST)],
                        KW[:, c * W: (c + 1) * W: ST]),
                        reads=[rs("KW")], writes=[rs("KSTR")])
            pr.st("gp", lambda e, b=b, s=s: e.dma_start(
                out=bass.AP(q_scr, ((b * NW + s) * QKC) * 128 * W,
                            [[W, 128], [128 * W, QKC], [1, W]]),
                in_=QW[:, :].rearrange("p (c w) -> p c w", c=QKC)),
                reads=[rs("QW")], writes=[rs(f"q_scr_{b}_{s}")], dma=True)
            for c in range(QKC):
                pr.st("gp", lambda e, b=b, s=s, c=c: e.dma_start(
                    out=bass.AP(k_scr, (b * QKC + c) * 128 * S + s * W,
                                [[S, 128], [1, W]]),
                    in_=KW[:, c * W:(c + 1) * W]),
                    reads=[rs("KW")], writes=[rs(f"k_scr_{b}_{s}")], dma=True)

        def a1_phase2b(b, s):
            FAb, rFA = a1_bufs(s)
            for tt in range(TTW):
                for oc in range(VO):
                    n = min(512, HDH - oc * 512)
                    bank = 2 + (tt * VO + oc) % 2

                    def fv(t, tt=tt, oc=oc, n=n, bank=bank, FAb=FAb):
                        last = None
                        for fc in range(FC):
                            last = t.matmul(
                                PS[bank][0:128, 0:n],
                                FAb[:, fc * W + tt * 128: fc * W + (tt + 1) * 128],
                                AW[:, off_wv + fc * HDH + oc * 512:
                                   off_wv + fc * HDH + oc * 512 + n],
                                start=(fc == 0), stop=(fc == FC - 1))
                        return last
                    pr.st("pe", fv, reads=[rFA, rs("AW")], writes=[rPS[bank]])
                    nh = n // DH
                    pr.st("dve", lambda v, tt=tt, oc=oc, n=n, nh=nh, bank=bank:
                          v.tensor_copy(
                              VW[:, tt * VWID + oc * 8 * VB:
                                 tt * VWID + oc * 8 * VB + nh * VB].rearrange(
                                     "p (h c) -> p h c", c=VB)[:, :, 1:VB],
                              PS[bank][0:128, 0:n].rearrange(
                                  "p (h c) -> p h c", c=DH)),
                          reads=[rPS[bank]], writes=[rs("VW")])

            def fg(t, FAb=FAb):
                last = None
                for fc in range(FC):
                    last = t.matmul(PS[6][0:1, 0:W],
                                    AW[:, off_wg + fc: off_wg + fc + 1],
                                    FAb[:, fc * W:(fc + 1) * W],
                                    start=(fc == 0), stop=(fc == FC - 1))
                return last
            pr.st("pe", fg, reads=[rs("AW"), rFA], writes=[rPS[6]])
            pr.st("act", lambda sc: sc.activation(
                SCR1[0:1, 0:W], PS[6][0:1, 0:W], AF.Exp, scale=-1.0,
                bias=NBG[:, :]),
                reads=[rPS[6], rs("NBG")], writes=[rs("SCR1")])

            def fsig(v, b=b, s=s):
                ch("dve", v, v.tensor_scalar_add(SCR2[0:1, 0:W],
                                                 SCR1[0:1, 0:W], 1.0))
                ch("dve", v, v.reciprocal(SCR1[0:1, 0:W], SCR2[0:1, 0:W]))
                return v.tensor_copy(G1[64 * b:64 * b + 1, s * W:(s + 1) * W],
                                     SCR1[0:1, 0:W])
            pr.st("dve", fsig, reads=[rs("SCR1")],
                  writes=[rs("SCR1"), rs("SCR2"), rs("G1")])
            pr.st("gp", lambda e, b=b, s=s: e.dma_start(
                out=bass.AP(v_scr, (b * S + s * W) * VWID,
                            [[VWID, 128], [128 * VWID, TTW], [1, VWID]]),
                in_=VW[:, 0:TTW * VWID].rearrange("p (t c) -> p t c", t=TTW)),
                reads=[rs("VW")], writes=[rs(f"v_scr_{b}_{s}")], dma=True)

        a1_phase1(0, 0)
        for b in range(B):
            for s in range(NW):
                a1_phase2a(b, s)
                nb, ns = (b, s + 1) if s + 1 < NW else (b + 1, 0)
                if nb < B:
                    a1_phase1(nb, ns)
                a1_phase2b(b, s)
            pr.st("sp", lambda e, b=b: e.dma_start(
                out=VSTR[:, b * VWID:(b + 1) * VWID],
                in_=bass.AP(v_scr, b * S * VWID, [[VWID * ST, SK], [1, VWID]])),
                reads=[rs(f"v_scr_{b}_{s}") for s in range(NW)],
                writes=[rs("VSTR")], dma=True)

        def a2_buffers(gw):
            p = gw % 2
            return ((QW, rs("QW")) if p == 0 else (QW2, rs("QW2")),
                    (KW, rs("KW")) if p == 0 else (KW2, rs("KW2")),
                    (VW, rs("VW")) if p == 0 else (VW2, rs("VW2")))

        def a2_kcs(w):
            kbase = w * W - W // 2
            kcs = [kc for kc in range(KC)
                   if kbase + kc * 128 >= 0 and kbase + (kc + 1) * 128 <= S]
            return kbase, kcs

        def emit_a2_qkv_loads(gw):
            b, w = divmod(gw, NW)
            (QWb, rQW), (KWb, rKW), (VWb, rVW) = a2_buffers(gw)
            kbase, kcs = a2_kcs(w)
            j0, j1 = kcs[0], kcs[-1]
            qw_w = ([rs("AW")] if gw == 1 else [])
            pr.st("sp", lambda e, b=b, w=w, QWb=QWb: e.dma_start(
                out=QWb[:, :].rearrange("p (c w) -> p c w", c=QKC),
                in_=bass.AP(q_scr, (b * NW + w) * QKC * 128 * W,
                            [[W, 128], [128 * W, QKC], [1, W]])),
                reads=[rs(f"q_scr_{b}_{w}")], writes=[rQW] + qw_w, dma=True)
            for c in range(QKC):
                lo = kbase + j0 * 128
                nk = (j1 - j0 + 1) * 128
                kres = [rs(f"k_scr_{b}_{sx}")
                        for sx in range(max(0, w - 1), min(NW, w + 2))]
                pr.st("sp", lambda e, b=b, c=c, lo=lo, nk=nk, j0=j0, KWb=KWb:
                      e.dma_start(
                          out=KWb[:, c * 2 * W + j0 * 128: c * 2 * W + j0 * 128 + nk],
                          in_=bass.AP(k_scr, (b * QKC + c) * 128 * S + lo,
                                      [[S, 128], [1, nk]])),
                      reads=kres, writes=[rKW] + qw_w, dma=True)
            lo = kbase + j0 * 128
            nj = j1 - j0 + 1
            vres = [rs(f"v_scr_{b}_{sx}")
                    for sx in range(max(0, w - 1), min(NW, w + 2))]
            pr.st("sp", lambda e, b=b, lo=lo, nj=nj, j0=j0, VWb=VWb: e.dma_start(
                out=VWb[:, j0 * VWID:(j0 + nj) * VWID].rearrange(
                    "p (t c) -> p t c", t=nj),
                in_=bass.AP(v_scr, (b * S + lo) * VWID,
                            [[VWID, 128], [128 * VWID, nj], [1, VWID]])),
                reads=vres, writes=[rVW] + qw_w, dma=True)

        emit_a2_qkv_loads(0)
        for gw in range(B * NW):
            b, w = divmod(gw, NW)
            (QWb, rQW), (KWb, rKW), (VWb, rVW) = a2_buffers(gw)
            kbase, kcs = a2_kcs(w)
            j0, j1 = kcs[0], kcs[-1]
            if True:
                for tt in range(TTW):
                    pr.st("sp", lambda e, b=b, w=w, tt=tt: e.dma_start(
                        out=XA[:, tt * D:(tt + 1) * D],
                        in_=bass.AP(x_t, (b * S + w * W + tt * 128) * D,
                                    [[D, 128], [1, D]])),
                        writes=[rs(f"XAt{tt}")], dma=True)
                if gw + 1 < B * NW:
                    emit_a2_qkv_loads(gw + 1)
                def fgst(v, b=b, w=w):
                    ch("dve", v, v.tensor_copy(
                        GRW[:, :], G1[64 * b:64 * b + 1, w * W:(w + 1) * W]))
                    return v.tensor_scalar(
                        GMR[:, :], GRW[:, :], -1.0, 1.0, ALU.mult, ALU.add)
                pr.st("dve", fgst, reads=[rs("G1")],
                      writes=[rs("GRW"), rs("GMR")])

                def fg4(t):
                    t.matmul(PS[1][0:4, 0:W], GS2[0:1, 0:4], GRW[:, :],
                             start=True, stop=False,
                             skip_group_check=True)
                    return t.matmul(PS[1][0:4, 0:W], GS2[0:1, 4:8], GMR[:, :],
                                    start=False, stop=True, skip_group_check=True)
                pr.st("pe", fg4, reads=[rs("GS2"), rs("GRW"), rs("GMR")],
                      writes=[rPS[1]])
                pr.st("dve", lambda v: v.tensor_copy(SCR1[0:4, 0:W],
                                                     PS[1][0:4, 0:W]),
                      reads=[rPS[1]], writes=[rs("SCR1")])

                def emit_front(pair, kcs=tuple(kcs), w=w):
                    pp = pair % 2
                    qq = (pair + 1) % 2
                    blist = (0, 1, 2 + 2 * qq, 3 + 2 * qq)
                    for sub in range(2):
                        h = pair * 2 + sub
                        hb = 64 * (h % 2)
                        hc2 = h // 2
                        avbank = 2 + sub + 2 * pp
                        ETb = ET if sub == 0 else TA
                        rET = rs("ET") if sub == 0 else rs("TA")
                        for i, kc in enumerate(kcs):
                            bank = blist[(sub * len(kcs) + i) % 4]
                            pr.st("pe", lambda t, hb=hb, hc2=hc2, kc=kc,
                                  bank=bank, KWb=KWb, QWb=QWb:
                                  t.matmul(PS[bank][0:128, 0:W],
                                           KWb[hb:hb + 64,
                                               hc2 * 2 * W + kc * 128: hc2 * 2 * W + (kc + 1) * 128],
                                           QWb[hb:hb + 64, hc2 * W:(hc2 + 1) * W],
                                           start=True, stop=True),
                                  reads=[rKW, rQW], writes=[rPS[bank]])
                            if i % 2 == 1:
                                b0 = blist[(sub * len(kcs) + i - 1) % 4]
                                kc0 = kcs[i - 1]
                                pr.st("act", lambda sc, kc0=kc0, b0=b0, ETb=ETb:
                                      sc.activation(
                                          ETb[:, kc0 * W:(kc0 + 2) * W],
                                          PA[b0 // 2][0:128, 0:2 * W],
                                          AF.Exp, scale=SCALE),
                                      reads=[rPS[b0], rPS[b0 + 1]], writes=[rET])

                        def fav(t, h=h, kcs=kcs, avbank=avbank, ETb=ETb, VWb=VWb):
                            last = None
                            for i, kc in enumerate(kcs):
                                last = t.matmul(
                                    PS[avbank][0:65, 0:W],
                                    VWb[:, kc * VWID + h * VB: kc * VWID + (h + 1) * VB],
                                    ETb[:, kc * W:(kc + 1) * W],
                                    start=(i == 0), stop=(i == len(kcs) - 1))
                            return last
                        pr.st("pe", fav, reads=[rVW, rET],
                              writes=[rPS[avbank]])
                        pr.st("act", lambda sc, avbank=avbank,
                              LU=LU2[sub][pp]: sc.copy(
                                  LU[:, :], PS[avbank][0:65, 0:W]),
                              reads=[rPS[avbank]],
                              writes=[rs(f"LU{sub}{pp}")])

                def emit_back(pair, w=w):
                    pp = pair % 2
                    gb = 6 + pp
                    for sub in range(2):
                        h = pair * 2 + sub
                        hb = 64 * (h % 2)
                        hc2 = h // 2
                        sbank = sub
                        pr.st("pe", lambda t, hb=hb, hc2=hc2, sbank=sbank,
                            b=b, QWb=QWb: t.matmul(
                            PS[sbank][0:SK, 0:W],
                            KSTR[hb:hb + 64, b * QKC * SK + hc2 * SK:
                                 b * QKC * SK + (hc2 + 1) * SK],
                            QWb[hb:hb + 64, hc2 * W:(hc2 + 1) * W],
                            start=True, stop=True),
                            reads=[rs("KSTR"), rQW], writes=[rPS[sbank]])
                    pr.st("act", lambda sc, pp=pp: sc.activation(
                        FA[0:SK, 2 * pp * W:2 * pp * W + 2 * W],
                        PA[0][0:SK, 0:2 * W], AF.Exp, scale=SCALE),
                        reads=[rPS[0], rPS[1]],
                        writes=[rs(f"EG0{pp}"), rs(f"EG1{pp}")])
                    for sub in range(2):
                        h = pair * 2 + sub
                        ego = (2 * pp + sub) * W
                        pr.st("pe", lambda t, h=h, gb=gb, ego=ego, b=b: t.matmul(
                            PS[gb][0:65, 0:W],
                            VSTR[0:SK, b * VWID + h * VB: b * VWID + (h + 1) * VB],
                            FA[0:SK, ego:ego + W], start=True, stop=True),
                            reads=[rs("VSTR"), rs(f"EG{sub}{pp}")], writes=[rPS[gb]])
                        pr.st("act", lambda sc, gb=gb, GU=GU2[sub][pp]: sc.copy(
                            GU[:, :], PS[gb][0:65, 0:W]),
                            reads=[rPS[gb]], writes=[rs(f"GU{sub}{pp}")])

                    def fgat(t, gb=gb, pp=pp):
                        last = None
                        for i, srcb in enumerate((LU2[0][pp], LU2[1][pp],
                                                  GU2[0][pp], GU2[1][pp])):
                            last = t.matmul(PS[gb][0:4, 0:W],
                                            E4[:, i * 4:(i + 1) * 4], srcb[:, :],
                                            start=(i == 0), stop=(i == 3))
                        return last
                    pr.st("pe", fgat,
                          reads=[rs("E4"), rs(f"LU0{pp}"), rs(f"LU1{pp}"),
                                 rs(f"GU0{pp}"), rs(f"GU1{pp}")],
                          writes=[rPS[gb]])

                    def frcp(v, gb=gb):
                        ch("dve", v, v.reciprocal(SCR2[0:4, 0:W],
                                                  PS[gb][0:4, 0:W]))
                        return v.tensor_mul(AB4[:, :], SCR2[0:4, 0:W],
                                            SCR1[0:4, 0:W])
                    pr.st("dve", frcp, reads=[rPS[gb], rs("SCR1")],
                          writes=[rs("SCR2"), rs("AB4")])

                    for sub in range(2):
                        LU = LU2[sub][pp]
                        GU = GU2[sub][pp]
                        rLU = rs(f"LU{sub}{pp}")
                        rGU = rs(f"GU{sub}{pp}")
                        tl = TLE if sub == 0 else TLO
                        tlr = rs("TLE") if sub == 0 else rs("TLO")
                        tt = TT1 if sub == 0 else TT2
                        rtt = rs("TT1") if sub == 0 else rs("TT2")
                        pr.st("pe", lambda t, sub=sub, gb=gb: t.matmul(
                            PS[gb][0:65, 0:W], S4[:, sub * 65:(sub + 1) * 65],
                            AB4[:, :], start=True, stop=True),
                            reads=[rs("S4"), rs("AB4")], writes=[rPS[gb]])
                        pr.st("dve", lambda v, tt=tt, LU=LU, gb=gb: v.tensor_mul(
                            tt[:, :], LU[:, :], PS[gb][0:65, 0:W]),
                            reads=[rLU, rPS[gb]], writes=[rtt])
                        pr.st("pe", lambda t, sub=sub, gb=gb: t.matmul(
                            PS[gb][0:65, 0:W],
                            S4[:, (2 + sub) * 65:(3 + sub) * 65],
                            AB4[:, :], start=True, stop=True),
                            reads=[rs("S4"), rs("AB4")], writes=[rPS[gb]])

                        def fcmb(v, tl=tl, tt=tt, GU=GU, gb=gb):
                            v.tensor_mul(tl[:, :], GU[:, :], PS[gb][0:65, 0:W])
                            return v.tensor_add(tl[:, :], tt[:, :], tl[:, :])
                        pr.st("dve", fcmb,
                              reads=[rGU, rPS[gb], rtt],
                              writes=[tlr])

                    def fasm(t, gb=gb):
                        t.matmul(PS[gb][0:128, 0:W], SHE[:, :], TLE[:, :],
                                 start=True, stop=False, skip_group_check=True)
                        return t.matmul(PS[gb][0:128, 0:W], SHO[:, :], TLO[:, :],
                                        start=False, stop=True, skip_group_check=True)
                    pr.st("pe", fasm,
                          reads=[rs("SHE"), rs("SHO"), rs("TLE"), rs("TLO")],
                          writes=[rPS[gb]])
                    pr.st("dve", lambda v, pair=pair, gb=gb: v.tensor_copy(
                        CB[:, pair * W:(pair + 1) * W], PS[gb][0:128, 0:W]),
                        reads=[rPS[gb]], writes=[rs("CB")])

                for pair in range(NP):
                    emit_front(pair)
                    if pair >= 1:
                        emit_back(pair - 1)
                emit_back(NP - 1)

                for tt in range(TTW):
                    x1b = X1[tt % 2]
                    rx1 = rs(f"X1{tt % 2}")
                    for oc in range(DOC):
                        n = min(512, D - oc * 512)
                        bank = oc % 2

                        def fop(t, tt=tt, oc=oc, n=n, bank=bank):
                            t.matmul(PS[bank][0:128, 0:n], ONEB[0:1, 0:128],
                                     BOR[0:1, oc * 512: oc * 512 + n],
                                     start=True, stop=False, skip_group_check=True)
                            last = None
                            for c in range(QKC):
                                last = t.matmul(
                                    PS[bank][0:128, 0:n],
                                    CB[:, c * W + tt * 128: c * W + (tt + 1) * 128],
                                    AW[:, off_wo + c * D + oc * 512:
                                       off_wo + c * D + oc * 512 + n],
                                    start=False, stop=(c == QKC - 1),
                                    skip_group_check=True)
                            return last
                        pr.st("pe", fop,
                              reads=[rs("ONEB"), rs("BOR"), rs("CB"), rs("AW")],
                              writes=[rPS[bank]])
                        pr.st("dve", lambda v, x1b=x1b, tt=tt, oc=oc, n=n, bank=bank:
                              v.tensor_add(x1b[:, oc * 512: oc * 512 + n],
                                           PS[bank][0:128, 0:n],
                                           XA[:, tt * D + oc * 512: tt * D + oc * 512 + n]),
                              reads=[rPS[bank], rs(f"XAt{tt}")], writes=[rx1])
                    pr.st("gp", lambda e, b=b, w=w, tt=tt, x1b=x1b: e.dma_start(
                        out=bass.AP(x1_scr, (b * S + w * W + tt * 128) * D,
                                    [[D, 128], [1, D]]),
                        in_=x1b[:, :]),
                        reads=[rx1], writes=[rs(f"x1_scr_{b * NW + w}")], dma=True)
                    par = tt % 2
                    ln_stats(rx1, lambda i, n, x1b=x1b: x1b[:, i * 512: i * 512 + n],
                             par)
                    slot0 = ((b * NW + w) * TTW + tt) * NST
                    for hlf in range(NST):
                        scr = SCR1 if hlf == 0 else SCR2
                        rscr = rs("SCR1") if hlf == 0 else rs("SCR2")
                        pr.st("dve", lambda v, x1b=x1b, par=par, hlf=hlf, scr=scr:
                              v.tensor_scalar(
                                  scr[:, 0:512], x1b[:, hlf * 512:(hlf + 1) * 512],
                                  MV2[par][:, 0:1], RST[par][:, :],
                                  ALU.subtract, ALU.mult),
                              reads=[rx1, rs(f"MV2{par}"), rs(f"RST{par}")],
                              writes=[rscr])
                        pr.st("dve", lambda v, hlf=hlf, slot0=slot0, scr=scr:
                              v.tensor_reduce(
                                  AM1[:, slot0 + hlf: slot0 + hlf + 1],
                                  scr[:, 0:512],
                                  AX.X, ALU.max, apply_absolute_value=True),
                              reads=[rscr], writes=[rs("AM1")])

                    for rnd in range(2):
                        tbank = ((4, 5) if tt % 2 == 0 else (6, 7))[rnd]
                        scr = SCR1 if rnd == 0 else SCR2
                        rscr = rs("SCR1") if rnd == 0 else rs("SCR2")

                        def ftr2(t, rnd=rnd, tbank=tbank, scr=scr):
                            last = None
                            for j in range(4):
                                last = t.transpose(
                                    PS[tbank][:, j * 128:(j + 1) * 128],
                                    scr[:, j * 128:(j + 1) * 128],
                                    IDF[:, :])
                            return last
                        pr.st("pe", ftr2, reads=[rscr, rs("IDF")],
                              writes=[rPS[tbank]])
                        pr.st("dve", lambda v, tt=tt, rnd=rnd, tbank=tbank:
                              v.tensor_copy(
                                  FA[:, 0:FC * W].rearrange(
                                      "p (c w) -> p c w", c=FC)[:, rnd * 4:rnd * 4 + 4,
                                                                tt * 128:(tt + 1) * 128],
                                  PS[tbank][:, 0:512].rearrange(
                                      "p (c k) -> p c k", c=4)),
                              reads=[rPS[tbank]], writes=[rs("FA")])
                pr.st("gp", lambda e, b=b, w=w: e.dma_start(
                    out=bass.AP(xn2_scr, (b * NW + w) * FC * 128 * W,
                                [[W, 128], [128 * W, FC], [1, W]]),
                    in_=FA[:, 0:FC * W].rearrange("p (c w) -> p c w", c=FC)),
                    reads=[rs("FA"), rs("EG00"), rs("EG10"), rs("EG01"),
                           rs("EG11")],
                    writes=[rs(f"xn2_scr_{b * NW + w}")], dma=True)

        pr.st("dve", lambda v: v.tensor_reduce(AMV[:, :], AM1[:, :], AX.X, ALU.max),
              reads=[rs("AM1")], writes=[rs("AMV")])
        pr.st("pe", lambda t: t.transpose(PS[0][0:1, 0:128], AMV[:, :], IDF[:, :]),
              reads=[rs("AMV"), rs("IDF")], writes=[rPS[0]])
        pr.st("dve", lambda v: v.tensor_reduce(
            CCA[0:1, 0:1], PS[0][0:1, 0:128], AX.X, ALU.max),
            reads=[rPS[0]], writes=[rs("CCA")])
        pr.st("dve", lambda v: v.tensor_scalar(
            CCA[0:1, 0:8], ZR8[:, :], CCA[0:1, 0:1], None, ALU.add),
            reads=[rs("ZR8"), rs("CCA")], writes=[rs("CCA")])
        pr.st("gp", lambda g: g.dma_start(out=cc1i[:, :], in_=CCA[:, :]),
              reads=[rs("CCA")], writes=[rs("cc1i")], dma=True)
        pr.st("gp", lambda g: g.collective_compute(
            "AllReduce", ALU.max, replica_groups=[list(range(NC))],
            ins=[cc1i[:, :]], outs=[cc1o[:, :]]),
            reads=[rs("cc1i")], writes=[rs("cc1o")])
        pr.st("gp", lambda g: g.dma_start(out=CCB[:, :], in_=cc1o[:, :]),
              reads=[rs("cc1o")], writes=[rs("CCB")], dma=True)
        pr.st("pe", lambda t: t.matmul(PS[0][0:128, 0:1], ONEF[:, :],
                                       CCB[0:1, 0:1], start=True, stop=True),
              reads=[rs("ONEF"), rs("CCB")], writes=[rPS[0]])

        def fscal1(v):
            ch("dve", v, v.tensor_copy(SCAL[:, 6:7], PS[0][0:128, 0:1]))
            ch("dve", v, v.tensor_single_scalar(SCAL[:, 0:1], SCAL[:, 6:7],
                                                1e-8, ALU.max))
            ch("dve", v, v.reciprocal(SCAL[:, 7:8], SCAL[:, 0:1]))
            ch("dve", v, v.tensor_scalar_mul(SCAL[:, 1:2], SCAL[:, 7:8], 32767.0))
            return v.tensor_scalar_mul(SCAL[:, 2:3], SCAL[:, 0:1], 1.0 / 32767.0)
        pr.st("dve", fscal1, reads=[rPS[0]], writes=[rs("SCAL")])

        CBK = DFF // 128
        for cb in range(CBK):
            p = cb % 2
            xat = rs(f"XAt{p}")
            xoff = p * D
            x1b = X1[p]
            rx1q = rs(f"X1Q{p}")
            bk0, bk1 = (4, 5) if p == 0 else (6, 7)
            pr.st("sp", lambda e, cb=cb, xoff=xoff: e.dma_start(
                out=XA[:, xoff:xoff + FC * 128].rearrange("p (c k) -> p c k", c=FC),
                in_=bass.AP(wi_t, cb * 128, [[DFF, 128], [128 * DFF, FC], [1, 128]])),
                writes=[xat], dma=True)

            def ftw(t, xoff=xoff, bk0=bk0, bk1=bk1):
                last = None
                for fc in range(FC):
                    bk = bk0 if fc < 4 else bk1
                    last = t.transpose(
                        PS[bk][:, (fc % 4) * 128:(fc % 4) * 128 + 128],
                        XA[:, xoff + fc * 128: xoff + (fc + 1) * 128], IDF[:, :])
                return last
            pr.st("pe", ftw, reads=[xat, rs("IDF")], writes=[rPS[bk0], rPS[bk1]])

            def fev(v, x1b=x1b, bk0=bk0, bk1=bk1):
                v.tensor_copy(x1b[:, 0:512], PS[bk0][:, 0:512])
                return v.tensor_copy(x1b[:, 512:1024], PS[bk1][:, 0:512])
            pr.st("dve", fev, reads=[rPS[bk0], rPS[bk1]], writes=[rx1q])

            def fqw(v, x1b=x1b):
                ch("dve", v, v.tensor_reduce(SQD[:, 0:1], x1b[:, 0:FC * 128],
                                             AX.X, ALU.max,
                                             apply_absolute_value=True))
                ch("dve", v, v.tensor_single_scalar(SQD[:, 1:2], SQD[:, 0:1],
                                                    1e-8, ALU.max))
                ch("dve", v, v.reciprocal(SQD[:, 2:3], SQD[:, 1:2]))
                ch("dve", v, v.tensor_scalar_mul(SQD[:, 3:4], SQD[:, 2:3], 127.0))
                ch("dve", v, v.tensor_scalar_mul(SQD[:, 4:5], SQD[:, 1:2],
                                                 1.0 / 127.0))
                last = None
                for i in range((FC * 128 + 511) // 512):
                    n = min(512, FC * 128 - i * 512)
                    sl = slice(i * 512, i * 512 + n)
                    rint_ops(v, SCR2[:, 0:n], x1b[:, sl], SQD[:, 3:4], n)
                    last = v.tensor_scalar(x1b[:, sl], SCR2[:, 0:n],
                                           SQD[:, 4:5], None, ALU.mult)
                return last
            pr.st("dve", fqw, reads=[rx1q, rs("DAB")],
                  writes=[rs("SQD"), rs("SCR1"), rs("SCR2"), rx1q])

            def ftb(t, x1b=x1b, bk0=bk0, bk1=bk1):
                last = None
                for fc in range(FC):
                    bk = bk0 if fc < 4 else bk1
                    last = t.transpose(
                        PS[bk][:, (fc % 4) * 128:(fc % 4) * 128 + 128],
                        x1b[:, fc * 128:(fc + 1) * 128], IDF[:, :])
                return last
            pr.st("pe", ftb, reads=[rx1q, rs("IDF")], writes=[rPS[bk0], rPS[bk1]])

            def fev2(v, cb=cb, bk0=bk0, bk1=bk1):
                last = None
                for fc in range(FC):
                    bk = bk0 if fc < 4 else bk1
                    last = v.tensor_copy(
                        AW[:, fc * DFF + cb * 128: fc * DFF + (cb + 1) * 128],
                        PS[bk][:, (fc % 4) * 128:(fc % 4) * 128 + 128])
                return last
            pr.st("dve", fev2, reads=[rPS[bk0], rPS[bk1]],
                  writes=[rs(f"AWF{cb}")] +
                  ([rs("AW"), rs("QW2"), rs("KW2"), rs("VW2")]
                   if cb == 0 else []))

        if debug_spill:
            pr.st("sp", lambda e: e.dma_start(
                out=bass.AP(aw_dbg, 0, [[FC * DFF, 128], [1, FC * DFF]]),
                in_=AW[:, 0:FC * DFF]), reads=[rs("AW")], dma=True)
            pr.st("sp", lambda e: e.dma_start(
                out=bass.AP(bin_dbg, 0, [[HC, 128], [1, HC]]),
                in_=BIN[:, :]), reads=[rs("BIN")], dma=True)
        for sl in range(NSLAB):
            TAb = TA if sl % 2 == 0 else CB
            rTAb = rs("TA") if sl % 2 == 0 else rs("CB")
            pr.st("sp", lambda e, sl=sl: e.dma_start(
                out=FA[:, 0:FC * W].rearrange("p (c w) -> p c w", c=FC),
                in_=bass.AP(xn2_scr, sl * FC * 128 * W,
                            [[W, 128], [128 * W, FC], [1, W]])),
                reads=[rs(f"xn2_scr_{sl}")], writes=[rs("FA")], dma=True)

            def fq1(v, TAb=TAb):
                last = None
                for fc in range(FC):
                    last = rint_ops(v, TAb[:, fc * W:(fc + 1) * W],
                                    FA[:, fc * W:(fc + 1) * W],
                                    SCAL[:, 1:2], W)
                return last
            pr.st("dve", fq1, reads=[rs("FA"), rs("SCAL"), rs("DAB")],
                  writes=[rs("SCR1"), rs("SCR2"), rTAb])
            for hc in range(HC):
                bank = hc % 4

                def ff1(t, hc=hc, bank=bank, TAb=TAb):
                    last = None
                    for fc in range(FC):
                        last = t.matmul(
                            PS[bank][0:128, 0:W],
                            AW[:, fc * DFF + hc * 128: fc * DFF + (hc + 1) * 128],
                            TAb[:, fc * W:(fc + 1) * W],
                            start=(fc == 0), stop=(fc == FC - 1))
                    return last
                pr.st("pe", ff1, reads=[rs(f"AWF{hc}"), rTAb],
                      writes=[rPS[bank]])
                slot = hc % FC
                pr.st("act", lambda sc, hc=hc, bank=bank, slot=slot: sc.activation(
                    ET[:, slot * W:(slot + 1) * W], PS[bank][0:128, 0:W],
                    AF.Gelu, bias=BIN[:, hc:hc + 1], scale=SCAL[:, 2:3]),
                    reads=[rPS[bank], rs("BIN"), rs("SCAL")], writes=[rs("ET")])
                pr.st("dve", lambda v, sl=sl, hc=hc, slot=slot: v.tensor_reduce(
                    AM2[:, sl * HC + hc: sl * HC + hc + 1],
                    ET[:, slot * W:(slot + 1) * W],
                    AX.X, ALU.max, apply_absolute_value=True),
                    reads=[rs("ET")], writes=[rs("AM2")])
                if slot == FC - 1:
                    g0 = hc - FC + 1
                    pr.st("gp", lambda e, sl=sl, g0=g0: e.dma_start(
                        out=bass.AP(hid_scr, (sl * HC + g0) * 128 * W,
                                    [[W, 128], [128 * W, FC], [1, W]]),
                        in_=ET[:, 0:FC * W].rearrange("p (c w) -> p c w", c=FC)),
                        reads=[rs("ET")], writes=[rs(f"hid_scr_{sl}")], dma=True)

        pr.st("dve", lambda v: v.tensor_reduce(AMV[:, :], AM2[:, :], AX.X, ALU.max),
              reads=[rs("AM2")], writes=[rs("AMV")])
        pr.st("pe", lambda t: t.transpose(PS[0][0:1, 0:128], AMV[:, :], IDF[:, :]),
              reads=[rs("AMV"), rs("IDF")], writes=[rPS[0]])
        pr.st("dve", lambda v: v.tensor_reduce(
            CCA[0:1, 0:1], PS[0][0:1, 0:128], AX.X, ALU.max),
            reads=[rPS[0]], writes=[rs("CCA")])
        pr.st("dve", lambda v: v.tensor_scalar(
            CCA[0:1, 0:8], ZR8[:, :], CCA[0:1, 0:1], None, ALU.add),
            reads=[rs("ZR8"), rs("CCA")], writes=[rs("CCA")])
        pr.st("gp", lambda g: g.dma_start(out=cc2i[:, :], in_=CCA[:, :]),
              reads=[rs("CCA")], writes=[rs("cc2i")], dma=True)
        pr.st("gp", lambda g: g.collective_compute(
            "AllReduce", ALU.max, replica_groups=[list(range(NC))],
            ins=[cc2i[:, :]], outs=[cc2o[:, :]]),
            reads=[rs("cc2i")], writes=[rs("cc2o")])
        pr.st("gp", lambda g: g.dma_start(out=CCB[:, :], in_=cc2o[:, :]),
              reads=[rs("cc2o")], writes=[rs("CCB")], dma=True)
        pr.st("pe", lambda t: t.matmul(PS[0][0:128, 0:1], ONEF[:, :],
                                       CCB[0:1, 0:1], start=True, stop=True),
              reads=[rs("ONEF"), rs("CCB")], writes=[rPS[0]])

        def fscal2(v):
            ch("dve", v, v.tensor_copy(SCAL[:, 6:7], PS[0][0:128, 0:1]))
            ch("dve", v, v.tensor_single_scalar(SCAL[:, 3:4], SCAL[:, 6:7],
                                                1e-8, ALU.max))
            ch("dve", v, v.reciprocal(SCAL[:, 7:8], SCAL[:, 3:4]))
            ch("dve", v, v.tensor_scalar_mul(SCAL[:, 4:5], SCAL[:, 7:8], 32767.0))
            return v.tensor_scalar_mul(SCAL[:, 5:6], SCAL[:, 3:4], 1.0 / 32767.0)
        pr.st("dve", fscal2, reads=[rPS[0]], writes=[rs("SCAL")])

        for hc in range(HC):
            p = hc % 2
            xat = rs(f"XAt{p}")
            xoff = p * D
            pr.st("act", lambda e, hc=hc, xoff=xoff: e.dma_start(
                out=XA[:, xoff:xoff + D],
                in_=bass.AP(wo2_t, hc * 128 * D, [[D, 128], [1, D]])),
                writes=[xat], dma=True)

            def fqo(v, hc=hc, xoff=xoff):
                ch("dve", v, v.tensor_reduce(SQD[:, 0:1], XA[:, xoff:xoff + D],
                                             AX.X, ALU.max,
                                             apply_absolute_value=True))
                ch("dve", v, v.tensor_single_scalar(SQD[:, 1:2], SQD[:, 0:1],
                                                    1e-8, ALU.max))
                ch("dve", v, v.reciprocal(SQD[:, 2:3], SQD[:, 1:2]))
                ch("dve", v, v.tensor_scalar_mul(SQD[:, 3:4], SQD[:, 2:3], 127.0))
                ch("dve", v, v.tensor_scalar_mul(SQD[:, 4:5], SQD[:, 1:2],
                                                 1.0 / 127.0))
                last = None
                for i in range(DOC):
                    n = min(512, D - i * 512)
                    sl = slice(xoff + i * 512, xoff + i * 512 + n)
                    rint_ops(v, SCR2[:, 0:n], XA[:, sl], SQD[:, 3:4], n)
                    last = v.tensor_scalar(AW[:, hc * D + i * 512: hc * D + i * 512 + n],
                                           SCR2[:, 0:n], SQD[:, 4:5], None,
                                           ALU.mult)
                return last
            pr.st("dve", fqo, reads=[xat, rs("DAB")],
                  writes=[rs(f"AWF2_{hc}"), rs("SQD"), rs("SCR1"), rs("SCR2")] +
                  ([rs(f"AWF{c}") for c in range(CBK)] if hc == 0 else []))

        NPS = TTW * DOC
        assert NPS <= 8
        out_dmas = []
        def b2_hid_load(sl, g8):
            buf, rbuf = (ET, rs("ET")) if g8 % 2 == 0 else (CB, rs("CB"))
            ng = min(FC, HC - g8 * FC)
            pr.st("sp", lambda e, sl=sl, g8=g8, ng=ng, buf=buf: e.dma_start(
                out=buf[:, 0:ng * W].rearrange("p (c w) -> p c w", c=ng),
                in_=bass.AP(hid_scr, (sl * HC + g8 * FC) * 128 * W,
                            [[W, 128], [128 * W, ng], [1, W]])),
                reads=[rs(f"hid_scr_{sl}")], writes=[rbuf], dma=True)

        for sl in range(NSLAB):
            b, w = sl // NW, sl % NW
            b2_hid_load(sl, 0)
            pr.st("sp", lambda e, b=b, w=w: e.dma_start(
                out=XA[:, :].rearrange("p (t d) -> p t d", t=TTW),
                in_=bass.AP(x1_scr, (b * S + w * W) * D,
                            [[D, 128], [128 * D, TTW], [1, D]])),
                reads=[rs(f"x1_scr_{sl}")],
                writes=[rs(f"XAt{tt}") for tt in range(TTW)], dma=True)
            for g8 in range((HC + FC - 1) // FC):
                buf, rbuf = (ET, rs("ET")) if g8 % 2 == 0 else (CB, rs("CB"))
                ng = min(FC, HC - g8 * FC)
                if g8 > 0:
                    b2_hid_load(sl, g8)
                for j in range(ng):
                    hc = g8 * FC + j
                    r2b = QW[:, (hc % 2) * W:(hc % 2 + 1) * W]
                    rr2 = rs(f"R2{hc % 2}")

                    def fq2(v, buf=buf, j=j, r2b=r2b):
                        return rint_ops(v, r2b, buf[:, j * W:(j + 1) * W],
                                        SCAL[:, 4:5], W)
                    pr.st("dve", fq2, reads=[rbuf, rs("SCAL"), rs("DAB")],
                          writes=[rs("SCR1"), rs("SCR2"), rr2])

                    def ff2(t, hc=hc, r2b=r2b):
                        last = None
                        for tt in range(TTW):
                            for oc in range(DOC):
                                n = min(512, D - oc * 512)
                                last = t.matmul(
                                    PS[tt * DOC + oc][0:128, 0:n],
                                    r2b[0:128, tt * 128:(tt + 1) * 128],
                                    AW[:, hc * D + oc * 512: hc * D + oc * 512 + n],
                                    start=(hc == 0), stop=(hc == HC - 1))
                        return last
                    pr.st("pe", ff2, reads=[rr2, rs(f"AWF2_{hc}")],
                          writes=[rPS[i] for i in range(NPS)])
            for tt in range(TTW):
                for oc in range(DOC):
                    n = min(512, D - oc * 512)
                    ob = X1[(tt * DOC + oc) % 2]
                    rob = rs(f"X1{(tt * DOC + oc) % 2}")

                    def fep(v, tt=tt, oc=oc, n=n, ob=ob):
                        v.scalar_tensor_tensor(
                            SCR1[:, 0:n], PS[tt * DOC + oc][0:128, 0:n],
                            SCAL[:, 5:6], BB2[:, oc * 512: oc * 512 + n],
                            ALU.mult, ALU.add)
                        return v.tensor_add(
                            ob[0:128, 0:n], SCR1[:, 0:n],
                            XA[:, tt * D + oc * 512: tt * D + oc * 512 + n])
                    pr.st("dve", fep,
                          reads=[rPS[tt * DOC + oc], rs("SCAL"), rs("BB2"),
                                 rs(f"XAt{tt}")],
                          writes=[rs("SCR1"), rob])
                    stp = pr.st("gp", lambda e, b=b, w=w, tt=tt, oc=oc, n=n, ob=ob:
                                e.dma_start(
                                    out=bass.AP(out_t,
                                                (b * S + w * W + tt * 128) * D + oc * 512,
                                                [[D, 128], [1, n]]),
                                    in_=ob[0:128, 0:n]),
                                reads=[rob], dma=True)
                    out_dmas.append(stp)
        if debug_spill:
            pr.st("sp", lambda e: e.dma_start(
                out=bass.AP(sc_dbg, 0, [[8, 128], [1, 8]]), in_=SCAL[:, :]),
                reads=[rs("SCAL")], dma=True)
        pr.st("gp", lambda g: g.nop(), deps=out_dmas)

        block = es.enter_context(nc.Block())
        pr.finalize(block, esems, dsems)
    return nc


def kernel(L0_attn_x, L0_attn_W_qkv, L0_attn_W_gate, L0_attn_b_gate,
           L0_attn_W_out, L0_attn_b_out, L0_ffn_W_in, L0_ffn_b_in,
           L0_ffn_W_out, L0_ffn_b_out):
    P = REAL
    NC = P["NC"]
    x = np.ascontiguousarray(np.asarray(L0_attn_x, dtype=np.float32))
    Btot = x.shape[0]
    bper = Btot // NC
    common = {
        "w_qkv": np.ascontiguousarray(L0_attn_W_qkv, dtype=np.float32),
        "w_gate": np.ascontiguousarray(L0_attn_W_gate, dtype=np.float32),
        "b_gate": np.ascontiguousarray(L0_attn_b_gate, dtype=np.float32).reshape(1, 1),
        "w_out": np.ascontiguousarray(L0_attn_W_out, dtype=np.float32),
        "b_out": np.ascontiguousarray(L0_attn_b_out, dtype=np.float32).reshape(1, -1),
        "w_in": np.ascontiguousarray(L0_ffn_W_in, dtype=np.float32),
        "b_in": np.ascontiguousarray(L0_ffn_b_in, dtype=np.float32).reshape(1, -1),
        "w_out2": np.ascontiguousarray(L0_ffn_W_out, dtype=np.float32),
        "b_out2": np.ascontiguousarray(L0_ffn_b_out, dtype=np.float32).reshape(1, -1),
    }
    in_maps = []
    for c in range(NC):
        m = dict(common)
        m["x"] = np.ascontiguousarray(x[c * bper:(c + 1) * bper])
        in_maps.append(m)
    nc = build_program(P)
    res = run_bass_kernel_spmd(nc, in_maps, core_ids=list(range(NC)))
    out = np.concatenate([np.asarray(r["out"]) for r in res.results], axis=0)
    return out.astype(np.float32)
